# version 09
# baseline (speedup 1.0000x reference)
import numpy as np
import ml_dtypes
import concourse.bass as bass
import concourse.mybir as mybir
from concourse.bass_utils import run_bass_kernel_spmd

F32 = mybir.dt.float32
BF16 = mybir.dt.bfloat16
AF = mybir.ActivationFunctionType
ALU = mybir.AluOpType

DEBUG_DUMP = None
LN_EPS = 1e-5
RMS_EPS = float(np.finfo(np.float32).eps)


class Cfg:
    def __init__(self, D, TC, H, BS=256):
        self.D = D
        self.TC = TC
        self.H = H
        self.BS = BS
        self.DT = D // 128
        self.TT = TC // 128
        self.E = D // 4
        self.ET = self.E // 128
        self.NCC = TC // BS
        self.NCB = 4 * self.NCC
        self.TH = TC // 2
        self.HT = H // 128
        self.TPC = BS // 128
        assert D % 512 == 0 and TC % 512 == 0 and BS == 256
        assert self.TH % 512 == 0 and H % 128 == 0


class SemS:
    def __init__(self, handle):
        self.h = handle
        self.n = 0


class Stream:

    def __init__(self, mk, name, k=2):
        self.sems = [mk(f"{name}{i}") for i in range(k)]
        self.k = k
        self._rr = 0

    def issue(self, eng, out, in_):
        s = self.sems[self._rr % self.k]
        self._rr += 1
        if s.n > 0:
            eng.wait_ge(s.h, s.n)
        eng.dma_start(out=out, in_=in_).then_inc(s.h, 16)
        s.n += 16
        return (s.h, s.n)

    def totals(self):
        return [(s.h, s.n) for s in self.sems]


def _wait(eng, tok):
    eng.wait_ge(tok[0], tok[1])


def _wait_all(eng, toks):
    for t in toks:
        eng.wait_ge(t[0], t[1])


def build_program(c: Cfg, phases=99):
    global DEBUG_DUMP
    nc = bass.Bass("TRN2", target_bir_lowering=False, detect_race_conditions=False)
    D, TC, DT, TT, BS, ET, E = c.D, c.TC, c.DT, c.TT, c.BS, c.ET, c.E
    NCC, NCB, TH, HT, TPC = c.NCC, c.NCB, c.TH, c.HT, c.TPC

    EQ = D // 4
    DQ = D // 4
    x_fm = nc.dram_tensor("x_fm", [D, TC], F32, kind="ExternalInput").ap()
    x_tm = nc.dram_tensor("x_tm", [TC, D], F32, kind="ExternalInput").ap()
    wq_t = nc.dram_tensor("wq_t", [DT, 128, DT * 128], F32, kind="ExternalInput").ap()
    wk_t = nc.dram_tensor("wk_t", [DT, 128, DT * 128], F32, kind="ExternalInput").ap()
    wv_t = nc.dram_tensor("wv_t", [4, 128, DT * EQ], F32, kind="ExternalInput").ap()
    wg_t = nc.dram_tensor("wg_t", [DT, 128, DT * 128], F32, kind="ExternalInput").ap()
    wp_t = nc.dram_tensor("wp_t", [DT, 128, DT * 128], F32, kind="ExternalInput").ap()
    wgate_b = nc.dram_tensor("wgate_b", [HT, 128, DT * 128], F32, kind="ExternalInput").ap()
    wup_b = nc.dram_tensor("wup_b", [HT, 128, DT * 128], F32, kind="ExternalInput").ap()
    wdown_b = nc.dram_tensor("wdown_b", [4, 128, HT * DQ], F32, kind="ExternalInput").ap()
    out_fm = nc.dram_tensor("out_fm", [D, TC], F32, kind="ExternalOutput").ap()

    ident_f = nc.inline_tensor(np.eye(128, dtype=np.float32), "ident_f").ap()
    ident_b = nc.inline_tensor(np.eye(128, dtype=np.float32).astype(ml_dtypes.bfloat16), "ident_b").ap()
    ones_f = nc.inline_tensor(np.ones((1, 128), np.float32), "ones_f").ap()
    mtri = np.triu(np.ones((BS, BS), np.float32))
    mtri = np.ascontiguousarray(
        mtri.reshape(TPC, 128, BS).transpose(1, 0, 2).reshape(128, TPC * BS)
    ).astype(ml_dtypes.bfloat16)
    mask_c = nc.inline_tensor(mtri, "mask_c").ap()

    def dram(name, shape, dt, **kw):
        return nc.dram_tensor(name, shape, dt, **kw).ap()

    ln_scr = dram("ln_scr", [DT, 128, TC], BF16)
    g_scr = dram("g_scr", [DT, 128, TC], BF16)
    kfm_scr = dram("kfm_scr", [DT, 128, TC], BF16)
    oi_scr = dram("oi_scr", [NCC, DT, 128, BS], BF16)
    gated_scr = dram("gated_scr", [HT, 128, TC], BF16)

    wq_bf = dram("wq_bf", [DT, 128, DT * 128], BF16)
    wk_bf = dram("wk_bf", [DT, 128, DT * 128], BF16)
    wv_bf = dram("wv_bf", [4, 128, DT * EQ], BF16)
    wg_bf = dram("wg_bf", [DT, 128, DT * 128], BF16)
    wp_bf = dram("wp_bf", [DT, 128, DT * 128], BF16)
    wgate_bf = dram("wgate_bf", [HT, 128, DT * 128], BF16)
    wup_bf = dram("wup_bf", [HT, 128, DT * 128], BF16)
    wdown_bf = dram("wdown_bf", [4, 128, HT * DQ], BF16)
    xattn_scr = dram("xattn_scr", [DT, 128, TC], F32)
    vecs_scr = dram("vecs_scr", [4 * TT, 128], F32)

    QSZ = NCC * DT * 128 * BS
    KSZ = NCC * TPC * 128 * D
    MGE = QSZ + 2 * KSZ
    mg_pub = dram("mg_pub", [MGE], BF16)
    mg_ag = dram("mg_ag", [4 * MGE], BF16)
    mg_pub2d = mg_pub.rearrange("(x c) -> x c", c=2048)
    mg_ag2d = mg_ag.rearrange("(x c) -> x c", c=2048)
    oint_pub = dram("oint_pub", [NCB * E, BS], BF16)
    oint_ag = dram("oint_ag", [4 * NCB * E, BS], BF16)

    v_loc = dram("v_loc", [4 * NCC * TPC * 128, E], BF16)
    oint_loc = dram("oint_loc", [4 * NCC * E, BS], BF16)

    q_pub_b = mg_pub[0:QSZ].rearrange("(l a p f) -> l a p f", l=NCC, a=DT, p=128)
    ktm_pub_b = mg_pub[QSZ:QSZ + KSZ].rearrange("(l t p d) -> l t p d", l=NCC, t=TPC, p=128)
    v_pub_b = mg_pub[QSZ + KSZ:MGE].rearrange("(l t p d) -> l t p d", l=NCC, t=TPC, p=128)
    q_ag_v = [mg_ag[r * MGE:r * MGE + QSZ].rearrange("(l a p f) -> l a p f", l=NCC, a=DT, p=128)
              for r in range(4)]
    ktm_ag_v = [mg_ag[r * MGE + QSZ:r * MGE + QSZ + KSZ].rearrange("(l t p d) -> l t p d", l=NCC, t=TPC, p=128)
                for r in range(4)]
    v_ag_v = [mg_ag[r * MGE + QSZ + KSZ:(r + 1) * MGE].rearrange("(x d) -> x d", d=D)
              for r in range(4)]
    v_loc_b = v_loc.rearrange("(r l t p) e -> r l t p e", r=4, l=NCC, t=TPC, p=128)
    oint_pub_b = oint_pub.rearrange("(ch e p) f -> ch e p f", ch=NCB, e=ET, p=128)
    oint_loc_b = oint_loc.rearrange("(r l e p) f -> r l e p f", r=4, l=NCC, e=ET, p=128)

    def sb(name, shape, dtype, off):
        return nc.alloc_sbuf_tensor_at(name, shape, dtype, offset=off).ap()

    AR = 16512
    A0, B0, C0 = AR, AR + DT * TC * 2, AR + 2 * DT * TC * 2
    rn_sb = sb("rn_sb", [128, DT * TC], BF16, A0)
    lnh_sb = sb("lnh_sb", [128, DT * TC], BF16, A0)
    og_sb = sb("og_sb", [128, DT * TH], BF16, A0)
    gh_sb = sb("gh_sb", [128, DT * TH], BF16, A0 + DT * TH * 2)
    wdA_sb = sb("wdA_sb", [128, HT * DQ], BF16, A0)
    ktm_sb = sb("ktm_sb", [128, TT * D], BF16, B0)
    p0need = 4 * TC * 4 + 2 * TC * 2 + 2 * TC * 4
    p0b = B0 if p0need <= DT * TC * 2 else C0 + 77824
    assert p0b + p0need <= 229344
    bcast = sb("bcast", [128, 4 * TC], F32, p0b)
    lnt_buf = sb("lnt_buf", [128, 2 * TC], BF16, p0b + 4 * TC * 4)
    tmp_v = sb("tmp_v", [128, TC], F32, p0b + 4 * TC * 4 + 2 * TC * 2)
    tmp_g = sb("tmp_g", [128, TC], F32, p0b + 4 * TC * 4 + 2 * TC * 2 + TC * 4)
    kv_sb = sb("kv_sb", [128, DT * E], F32, B0)
    kv_bf = sb("kv_bf", [128, DT * E], BF16, B0 + DT * E * 4)
    gatB_sb = sb("gatB_sb", [128, 4 * TC], BF16, B0 + 6 * TC * 4)
    silB_sb = sb("silB_sb", [128, 2 * TC], BF16, C0 + 71680)
    wdB_sb = sb("wdB_sb", [128, HT * DQ], BF16, B0)
    assert HT * DQ * 2 == DT * TC * 2

    xtm_buf = sb("xtm_buf", [128, 2 * D], F32, C0)
    wbA1 = sb("wbA1", [128, 2 * DT * 128], BF16, C0)
    wbA2 = sb("wbA2", [128, 2 * DT * 128], BF16, C0 + 8192)
    qc_buf = sb("qc_buf", [128, 2 * DT * BS], BF16, C0)
    xsq_buf = sb("xsq_buf", [128, 2 * D], F32, C0 + 16384)
    ktc_buf = sb("ktc_buf", [128, 2 * TPC * D], BF16, C0 + 16384)
    oint_buf = sb("oint_buf", [128, DT * BS], BF16, C0 + 16384)
    wvq_buf = sb("wvq_buf", [128, 2 * DT * EQ], BF16, C0 + 16384)
    xf_buf = sb("xf_buf", [128, 2 * TC], F32, C0 + 32768)
    vv_buf = sb("vv_buf", [128, 2 * TPC * D], BF16, C0 + 32768)
    vte_buf = sb("vte_buf", [128, 2 * TPC * E], BF16, C0 + 32768)
    oi4_sb = sb("oi4_sb", [128, DT * BS], BF16, C0 + 40960)
    bsrc_sb = sb("bsrc_sb", [1, 4 * TC], F32, C0)
    stat = sb("stat", [128, 18 * TT], F32, C0 + 49152)
    packT_sb = sb("packT_sb", [4 * TT, 128], F32, C0 + 50304)
    onesb = sb("onesb", [1, 128], F32, C0 + 50816)
    identf_sb = sb("identf_sb", [128, 128], F32, C0 + 51328)
    identb_sb = sb("identb_sb", [128, 128], BF16, C0 + 51840)
    mask_sb = sb("mask_sb", [128, TPC * BS], BF16, C0 + 52096)
    w_buf = sb("w_buf", [128, 2 * DT * 128], BF16, C0 + 53248)
    oi_sb = sb("oi_sb", [128, DT * BS], BF16, C0 + 53248)
    w_buf2 = sb("w_buf2", [128, 2 * DT * 128], BF16, C0 + 61440)
    xa_buf = sb("xa_buf", [128, 2 * 1024], F32, C0 + 61440)
    sT_sb = sb("sT_sb", [128, TPC * BS], BF16, C0 + 61440)
    strip_buf = sb("strip_buf", [128, 2 * TC], BF16, C0 + 69632)
    gld_buf = sb("gld_buf", [128, 8 * 512], BF16, C0 + 69632)
    oi3_sb = sb("oi3_sb", [128, ET * BS], BF16, C0 + 69632)
    assert 18 * TT * 4 <= 1152 and 4 * TT <= 128
    assert 2 * TPC * D * 2 <= 16384 and DT * BS * 2 <= 8192
    assert C0 + 77824 <= 229344, "sbuf overflow"

    pb = [nc.alloc_psum_tensor(f"pb{i}", [128, 512], F32).ap() for i in range(8)]

    guards = []

    def mk(name):
        g = nc.semaphore(name)
        h = g.__enter__()
        guards.append(g)
        return SemS(h)

    tC, vC, sC, gC = mk("tC"), mk("vC"), mk("sC"), mk("gC")
    p5aT, p5aS, p5aV = mk("p5aT"), mk("p5aS"), mk("p5aV")
    ccm, cco = mk("ccm"), mk("cco")
    barS = mk("bar")
    sX = Stream(mk, "sX", 4)
    tT = Stream(mk, "tT", 4)
    gW = Stream(mk, "gW", 4)
    sS = Stream(mk, "sS", 4)
    sL = Stream(mk, "sL", 4)
    oT = Stream(mk, "oT", 4)

    RG8 = [list(range(8))]
    RG4 = [[0, 1, 2, 3], [4, 5, 6, 7]]
    toks = {}
    misc = {}

    bg = nc.Block()
    block = bg.__enter__()

    def barrier():
        barS.n += 5
        tgt = barS.n
        for sec in (block.tensor, block.vector, block.scalar, block.gpsimd, block.sync):
            def _f(eng, tgt=tgt):
                eng.sem_inc(barS.h, 1)
                eng.wait_ge(barS.h, tgt)
            sec(_f)

    def p0():
        xtm_t = {}
        xf_t = {}
        ln_t = {}
        b_sC, b_vC, b_gC = sC.n, vC.n, gC.n

        @block.sync
        def _(sy):
            for nm, t, src in (("ones", onesb, ones_f), ("idf", identf_sb, ident_f),
                               ("idb", identb_sb, ident_b), ("mask", mask_sb, mask_c)):
                toks[nm] = sX.issue(sy, t[:, :], src[:, :])
            for j in range(TT):
                if j >= 2:
                    sy.wait_ge(sC.h, b_sC + j - 1)
                    sy.wait_ge(vC.h, b_vC + 2 * (j - 1))
                xtm_t[j] = sX.issue(sy, xtm_buf[:, (j % 2) * D:(j % 2 + 1) * D],
                                    x_tm[j * 128:(j + 1) * 128, :])

        @block.scalar
        def _(s):
            for j in range(TT):
                _wait(s, xtm_t[j])
                if j >= 2:
                    s.wait_ge(vC.h, b_vC + 2 * (j - 1))
                s.activation(out=xsq_buf[:, (j % 2) * D:(j % 2 + 1) * D],
                             in_=xtm_buf[:, (j % 2) * D:(j % 2 + 1) * D],
                             func=AF.Square).then_inc(sC.h, 1)
            s.wait_ge(vC.h, b_vC + 2 * TT + 4)
            s.activation(out=stat[:, 5 * TT:6 * TT], in_=stat[:, 3 * TT:4 * TT],
                         func=AF.Ln, bias=stat[:, 17 * TT:17 * TT + 1]).then_inc(sC.h, 1)
            s.wait_ge(sC.h, b_sC + TT + 1)
            s.activation(out=stat[:, 6 * TT:7 * TT], in_=stat[:, 5 * TT:6 * TT],
                         func=AF.Exp, scale=-0.5).then_inc(sC.h, 1)
            s.wait_ge(vC.h, b_vC + 2 * TT + 6)
            s.activation(out=stat[:, 8 * TT:9 * TT], in_=stat[:, 7 * TT:8 * TT],
                         func=AF.Ln, bias=stat[:, 17 * TT + 1:17 * TT + 2]).then_inc(sC.h, 1)
            s.wait_ge(sC.h, b_sC + TT + 3)
            s.activation(out=stat[:, 9 * TT:10 * TT], in_=stat[:, 8 * TT:9 * TT],
                         func=AF.Exp, scale=-0.5).then_inc(sC.h, 1)
        sC.n += TT + 4

        @block.vector
        def _(v):
            v.memset(stat[:, 17 * TT:17 * TT + 1], LN_EPS)
            v.memset(stat[:, 17 * TT + 1:17 * TT + 2], RMS_EPS)
            for j in range(TT):
                _wait(v, xtm_t[j])
                v.reduce_sum(stat[:, j:j + 1], xtm_buf[:, (j % 2) * D:(j % 2 + 1) * D],
                             axis=mybir.AxisListType.X).then_inc(vC.h, 1)
                v.wait_ge(sC.h, b_sC + j + 1)
                v.reduce_sum(stat[:, TT + j:TT + j + 1], xsq_buf[:, (j % 2) * D:(j % 2 + 1) * D],
                             axis=mybir.AxisListType.X).then_inc(vC.h, 1)
            ecnt = [2 * TT]

            def edge(ins):
                ecnt[0] += 1
                ins.then_inc(vC.h, 1)
                v.wait_ge(vC.h, b_vC + ecnt[0])

            sm, sq = stat[:, 0:TT], stat[:, TT:2 * TT]
            m, var, tmp = stat[:, 2 * TT:3 * TT], stat[:, 3 * TT:4 * TT], stat[:, 4 * TT:5 * TT]
            edge(v.tensor_scalar_mul(m, sm, 1.0 / D))
            edge(v.tensor_scalar_mul(var, sq, 1.0 / D))
            edge(v.tensor_mul(tmp, m, m))
            edge(v.tensor_sub(var, var, tmp))
            assert ecnt[0] == 2 * TT + 4
            r1 = stat[:, 6 * TT:7 * TT]
            u = stat[:, 7 * TT:8 * TT]
            v.wait_ge(sC.h, b_sC + TT + 2)
            edge(v.tensor_mul(u, var, r1))
            edge(v.tensor_mul(u, u, r1))
            v.wait_ge(sC.h, b_sC + TT + 4)
            r2s = stat[:, 9 * TT:10 * TT]
            r2 = stat[:, 10 * TT:11 * TT]
            mr1 = stat[:, 11 * TT:12 * TT]
            mr2 = stat[:, 12 * TT:13 * TT]
            edge(v.tensor_mul(r2, r1, r2s))
            edge(v.tensor_mul(mr1, m, r1))
            edge(v.tensor_mul(mr2, m, r2))
            edge(v.tensor_copy(out=stat[:, 13 * TT:14 * TT], in_=r1))
            edge(v.tensor_copy(out=stat[:, 14 * TT:15 * TT], in_=mr1))
            edge(v.tensor_copy(out=stat[:, 15 * TT:16 * TT], in_=r2))
            edge(v.tensor_copy(out=stat[:, 16 * TT:17 * TT], in_=mr2))
        vC.n += 2 * TT + 13

        b_tC = tC.n

        @block.tensor
        def _(t):
            t.wait_ge(vC.h, b_vC + 2 * TT + 13)
            _wait(t, toks["idf"])
            t.transpose(pb[4][0:4 * TT, 0:128], stat[:, 13 * TT:17 * TT],
                        identf_sb[:, :]).then_inc(tC.h, 1)
        tC.n += 1

        @block.vector
        def _(v):
            v.wait_ge(tC.h, b_tC + 1)
            v.tensor_copy(out=packT_sb[:, :], in_=pb[4][0:4 * TT, 0:128]).then_inc(vC.h, 1)
        vC.n += 1

        @block.scalar
        def _(s):
            s.wait_ge(vC.h, vC.n)
            toks["vecs"] = sS.issue(s, vecs_scr[:, :], packT_sb[:, :])

        @block.sync
        def _(sy):
            _wait(sy, toks["vecs"])
            toks["bsrc"] = sX.issue(
                sy, bsrc_sb[0:1, :], vecs_scr.rearrange("a p -> (a p)").unsqueeze(0))

        nch = TC // 512
        b_tC2 = tC.n
        b_vC2 = vC.n

        @block.tensor
        def _(t):
            _wait(t, toks["bsrc"])
            _wait(t, toks["ones"])
            idx = 0
            for vv in range(4):
                for ch in range(nch):
                    if idx >= 4:
                        t.wait_ge(vC.h, b_vC2 + idx - 3)
                    mmi = t.matmul(pb[idx % 4][:, :], onesb[0:1, :],
                             bsrc_sb[0:1, vv * TC + ch * 512: vv * TC + (ch + 1) * 512],
                             start=True, stop=True).then_inc(tC.h, 1)
                    idx += 1
        tC.n += 4 * nch

        @block.vector
        def _(v):
            idx = 0
            for vv in range(4):
                for ch in range(nch):
                    v.wait_ge(tC.h, b_tC2 + idx + 1)
                    v.tensor_copy(out=bcast[:, vv * TC + ch * 512: vv * TC + (ch + 1) * 512],
                                  in_=pb[idx % 4][:, :]).then_inc(vC.h, 1)
                    idx += 1
        vC.n += 4 * nch

        b_vC3 = vC.n
        b_gC3 = gC.n

        @block.sync
        def _(sy):
            for d in range(DT):
                if d >= 2:
                    sy.wait_ge(vC.h, b_vC3 + 2 * (d - 1))
                    sy.wait_ge(gC.h, b_gC3 + 2 * (d - 1))
                xf_t[d] = sX.issue(sy, xf_buf[:, (d % 2) * TC:(d % 2 + 1) * TC],
                                   x_fm[d * 128:(d + 1) * 128, :])

        @block.gpsimd
        def _(g):
            g.wait_ge(vC.h, b_vC3)
            for d in range(DT):
                _wait(g, xf_t[d])
                xf = xf_buf[:, (d % 2) * TC:(d % 2 + 1) * TC]
                g.tensor_mul(tmp_g[:, :], xf, bcast[:, 2 * TC:3 * TC]).then_inc(gC.h, 1)
                g.wait_ge(gC.h, b_gC3 + 2 * d + 1)
                g.tensor_tensor(out=rn_sb[:, d * TC:(d + 1) * TC],
                                in0=tmp_g[:, :], in1=bcast[:, 3 * TC:4 * TC],
                                op=ALU.subtract).then_inc(gC.h, 1)
        gC.n += 2 * DT

        @block.scalar
        def _(s):
            for d in range(DT):
                s.wait_ge(vC.h, b_vC3 + 2 * (d + 1))
                ln_t[d] = sS.issue(s, ln_scr[d, :, :],
                                   lnt_buf[:, (d % 2) * TC:(d % 2 + 1) * TC])
            _wait_all(s, sS.totals())

        @block.vector
        def _(v):
            for d in range(DT):
                _wait(v, xf_t[d])
                if d >= 2:
                    _wait(v, ln_t[d - 2])
                xf = xf_buf[:, (d % 2) * TC:(d % 2 + 1) * TC]
                v.tensor_mul(tmp_v[:, :], xf, bcast[:, 0:TC]).then_inc(vC.h, 1)
                v.wait_ge(vC.h, b_vC3 + 2 * d + 1)
                v.tensor_tensor(out=lnt_buf[:, (d % 2) * TC:(d % 2 + 1) * TC],
                                in0=tmp_v[:, :], in1=bcast[:, TC:2 * TC],
                                op=ALU.subtract).then_inc(vC.h, 1)
        vC.n += 2 * DT


    pc = {}

    def precast(items):
        @block.gpsimd
        def _(g):
            for name, src, dst, npc in items:
                step = src.shape[0] // npc
                toks_ = pc.setdefault(name, {})
                for i in range(npc):
                    toks_[i] = gW.issue(g, dst[i * step:(i + 1) * step, :, :],
                                        src[i * step:(i + 1) * step, :, :])

    def proj_fm(wt_ap, act_fn, kind, pcname, dest=None):
        nch = TC // 512
        b_tC, b_sC, b_vC, b_gC = tC.n, sC.n, vC.n, gC.n
        per_o_t = nch + (TT if kind == "strips_ktm" else 0)
        w_t = {}
        mm_mil = {}
        tp_mil = {}

        @block.sync
        def _(sy):
            for o in range(DT):
                if o < 2:
                    sy.wait_ge(tC.h, b_tC)
                else:
                    sy.wait_ge(tC.h, b_tC + per_o_t * (o - 1))
                _wait(sy, pc[pcname][o // 4])
                w_t[o] = sX.issue(
                    sy,
                    w_buf[:, (o % 2) * DT * 128:(o % 2 + 1) * DT * 128],
                    wt_ap[o, :, :])

        @block.tensor
        def _(t):
            mil = b_tC

            def emit_tp(s_, j):
                nonlocal mil
                lin = s_ * TT + j
                if lin >= 4:
                    t.wait_ge(vC.h, b_vC + lin - 3)
                t.transpose(pb[4 + (j % 4)].bitcast(BF16)[:, 0:128],
                            strip_buf[:, (s_ % 2) * TC + j * 128:(s_ % 2) * TC + (j + 1) * 128],
                            identb_sb[:, :]).then_inc(tC.h, 1)
                mil += 1
                tp_mil[(s_, j)] = mil

            for o in range(DT):
                _wait(t, w_t[o])
                if o == 0:
                    t.wait_ge(sC.h, b_sC)
                    t.wait_ge(gC.h, b_gC)
                for tci in range(nch):
                    if o >= 1:
                        t.wait_ge(sC.h, b_sC + nch * (o - 1) + tci + 1)
                    for a in range(DT):
                        mmi = t.matmul(pb[tci % 4][:, :],
                                 w_buf[:, (o % 2) * DT * 128 + a * 128:(o % 2) * DT * 128 + (a + 1) * 128],
                                 rn_sb[:, a * TC + tci * 512:a * TC + (tci + 1) * 512],
                                 start=(a == 0), stop=(a == DT - 1))
                    mmi.then_inc(tC.h, 1)
                    mil += 1
                    mm_mil[(o, tci)] = mil
                if kind == "strips_ktm" and o >= 1:
                    t.wait_ge(sC.h, b_sC + nch * o)
                    for j in range(TT):
                        emit_tp(o - 1, j)
            if kind == "strips_ktm":
                t.wait_ge(sC.h, b_sC + nch * DT)
                for j in range(TT):
                    emit_tp(DT - 1, j)
        tC.n += nch * DT + (TT * DT if kind == "strips_ktm" else 0)

        st_toks = {}

        @block.scalar
        def _(s):
            _wait_all(s, sS.totals())
            for o in range(DT):
                if kind == "strips_ktm" and o >= 2:
                    s.wait_ge(tC.h, tp_mil[(o - 2, TT - 1)])
                if kind != "assemble" and o >= 2:
                    _wait(s, st_toks[o - 2])
                for tci in range(nch):
                    s.wait_ge(tC.h, mm_mil[(o, tci)])
                    if kind == "assemble":
                        dst = ktm_sb[:, o * TC + tci * 512:o * TC + (tci + 1) * 512]
                    else:
                        dst = strip_buf[:, (o % 2) * TC + tci * 512:(o % 2) * TC + (tci + 1) * 512]
                    s.activation(out=dst, in_=pb[tci % 4][:, :], func=act_fn).then_inc(sC.h, 1)
                if kind != "assemble":
                    s.wait_ge(sC.h, b_sC + nch * (o + 1))
                    st_toks[o] = sS.issue(s, dest[o, :, :], strip_buf[:, (o % 2) * TC:(o % 2 + 1) * TC])
        sC.n += nch * DT

        if kind == "strips_ktm":
            @block.vector
            def _(v):
                for s_ in range(DT):
                    for j in range(TT):
                        v.wait_ge(tC.h, tp_mil[(s_, j)])
                        v.tensor_copy(out=ktm_sb[:, j * D + s_ * 128:j * D + (s_ + 1) * 128],
                                      in_=pb[4 + (j % 4)].bitcast(BF16)[:, 0:128]).then_inc(vC.h, 1)
            vC.n += DT * TT

    def p1():
        proj_fm(wk_bf, AF.Silu, "strips_ktm", "wk", dest=kfm_scr)
        b_vC = vC.n

        @block.scalar
        def _(s):
            s.wait_ge(vC.h, b_vC)
            for l in range(NCC):
                sS.issue(s, ktm_pub_b[l, :, :, :].rearrange("t p d -> p t d"),
                         ktm_sb.rearrange("p (t d) -> p t d", t=TT)[:, TPC * l:TPC * (l + 1), :])
            _wait_all(s, sS.totals())

        b_tC = tC.n
        b_sC = sC.n
        wv_tok = {}
        vst_t = {}

        @block.sync
        def _(sy):
            for ev in range(4):
                if ev >= 2:
                    sy.wait_ge(tC.h, b_tC + TT * (ev - 1))
                _wait(sy, pc["wv"][ev])
                wv_tok[ev] = sX.issue(sy, wvq_buf[:, (ev % 2) * DT * EQ:(ev % 2 + 1) * DT * EQ],
                                      wv_bf[ev, :, :])

        @block.tensor
        def _(t):
            for ev in range(4):
                _wait(t, wv_tok[ev])
                for j in range(TT):
                    m = ev * TT + j
                    if m == 0:
                        t.wait_ge(sC.h, b_sC)
                    elif m >= 4:
                        t.wait_ge(sC.h, b_sC + m - 3)
                    for a in range(DT):
                        mmi = t.matmul(pb[m % 4][:, 0:EQ],
                                 rn_sb[:, a * TC + j * 128:a * TC + (j + 1) * 128],
                                 wvq_buf[:, (ev % 2) * DT * EQ + a * EQ:(ev % 2) * DT * EQ + (a + 1) * EQ],
                                 start=(a == 0), stop=(a == DT - 1))
                    mmi.then_inc(tC.h, 1)
        tC.n += 4 * TT

        @block.scalar
        def _(s):
            _wait_all(s, sS.totals())
            for ev in range(4):
                for j in range(TT):
                    m = ev * TT + j
                    if m >= 4:
                        _wait(s, vst_t[m - 4])
                    s.wait_ge(tC.h, b_tC + m + 1)
                    s.activation(out=strip_buf[:, (m % 4) * EQ:(m % 4 + 1) * EQ],
                                 in_=pb[m % 4][:, 0:EQ], func=AF.Silu).then_inc(sC.h, 1)
                    s.wait_ge(sC.h, b_sC + m + 1)
                    vst_t[m] = sS.issue(s, v_pub_b[j // TPC, j % TPC, :, ev * EQ:(ev + 1) * EQ],
                                        strip_buf[:, (m % 4) * EQ:(m % 4 + 1) * EQ])
        sC.n += 4 * TT

        proj_fm(wg_bf, AF.Sigmoid, "strips", "wg", dest=g_scr)

        proj_fm(wq_bf, AF.Silu, "assemble", "wq")
        b_sC3 = sC.n

        @block.scalar
        def _(s):
            s.wait_ge(sC.h, b_sC3)
            for l in range(NCC):
                sS.issue(s, q_pub_b[l, :, :, :].rearrange("a p t -> p a t"),
                         ktm_sb.rearrange("p (a t) -> p a t", a=DT)[:, :, l * BS:(l + 1) * BS])
            _wait_all(s, sS.totals())

        misc["g_end_tC"] = tC.n

        @block.gpsimd
        def _(g):
            _wait_all(g, sS.totals())
            g.collective_compute("AllGather", ALU.bypass, replica_groups=RG4,
                                 ins=[mg_pub2d.opt()], outs=[mg_ag2d.opt()]).then_inc(ccm.h, 1)
        ccm.n += 1

    def p25():
        b_tC, b_vC = tC.n, vC.n
        nr = (DT + 7) // 8
        per_t = TPC + nr
        per_v = TPC + nr
        ld_t = {}
        oist_t = {}

        @block.scalar
        def _(s):
            _wait_all(s, sS.totals())
            for l in range(NCC):
                if l >= 2:
                    s.wait_ge(tC.h, b_tC + per_t * (l - 1))
                h0 = (l % 2) * TPC * D
                t1 = sL.issue(s, qc_buf[:, (l % 2) * DT * BS:(l % 2 + 1) * DT * BS].rearrange("p (a f) -> p a f", a=DT),
                              q_pub_b[l, :, :, :].rearrange("a p f -> p a f"))
                t2 = sL.issue(s, ktc_buf[:, h0:h0 + DT * BS].rearrange("p (a f) -> p a f", a=DT),
                              kfm_scr[:, :, l * BS:(l + 1) * BS].rearrange("a p f -> p a f"))
                t3 = sL.issue(s, vv_buf[:, h0:h0 + TPC * D].rearrange("p (t d) -> p t d", t=TPC),
                              v_pub_b[l, :, :, :].rearrange("t p d -> p t d"))
                ld_t[l] = [t1, t2, t3]
                if l >= 1:
                    s.wait_ge(vC.h, b_vC + l * per_v)
                    oist_t[l - 1] = sS.issue(s, oi_scr[l - 1, :, :, :].rearrange("a p f -> p a f"),
                                             oi_sb.rearrange("p (a f) -> p a f", a=DT))
            s.wait_ge(vC.h, b_vC + NCC * per_v)
            oist_t[NCC - 1] = sS.issue(s, oi_scr[NCC - 1, :, :, :].rearrange("a p f -> p a f"),
                                       oi_sb.rearrange("p (a f) -> p a f", a=DT))
            _wait_all(s, sS.totals())

        @block.tensor
        def _(t):
            cnt = 0
            for l in range(NCC):
                _wait_all(t, ld_t[l])
                h0 = (l % 2) * TPC * D
                qc0 = (l % 2) * DT * BS
                for tp in range(TPC):
                    t.wait_ge(vC.h, b_vC + cnt)
                    for a in range(DT):
                        mmi = t.matmul(pb[6][:, 0:BS],
                                 ktc_buf[:, h0 + a * BS + tp * 128:h0 + a * BS + (tp + 1) * 128],
                                 qc_buf[:, qc0 + a * BS:qc0 + (a + 1) * BS],
                                 start=(a == 0), stop=(a == DT - 1))
                    mmi.then_inc(tC.h, 1)
                    cnt += 1
                for r_ in range(nr):
                    t.wait_ge(vC.h, b_vC + cnt)
                    for a in range(8 * r_, min(8 * r_ + 8, DT)):
                        for tp in range(TPC):
                            mmi = t.matmul(pb[a % 8][:, 0:BS],
                                     vv_buf[:, h0 + tp * D + a * 128:h0 + tp * D + (a + 1) * 128],
                                     sT_sb[:, tp * BS:(tp + 1) * BS],
                                     start=(tp == 0), stop=(tp == TPC - 1))
                    mmi.then_inc(tC.h, 1)
                    cnt += 1
        tC.n += per_t * NCC

        @block.vector
        def _(v):
            _wait(v, toks["mask"])
            for l in range(NCC):
                for tp in range(TPC):
                    v.wait_ge(tC.h, b_tC + l * per_t + tp + 1)
                    v.tensor_tensor(out=sT_sb[:, tp * BS:(tp + 1) * BS],
                                    in0=pb[6][:, 0:BS],
                                    in1=mask_sb[:, tp * BS:(tp + 1) * BS],
                                    op=ALU.mult).then_inc(vC.h, 1)
                for r_ in range(nr):
                    v.wait_ge(tC.h, b_tC + l * per_t + TPC + r_ + 1)
                    if r_ == 0 and l >= 1:
                        _wait(v, oist_t[l - 1])
                    ins = None
                    for a in range(8 * r_, min(8 * r_ + 8, DT)):
                        ins = v.tensor_copy(out=oi_sb[:, a * BS:(a + 1) * BS],
                                            in_=pb[a % 8][:, 0:BS])
                    ins.then_inc(vC.h, 1)
        vC.n += per_v * NCC
        misc["p25_end_tC"] = tC.n
        misc["p25_oist"] = sS.totals()

        @block.sync
        def _(sy):
            sy.wait_ge(tC.h, misc["g_end_tC"])
            lnh_toks = []
            for qa in range(4):
                lnh_toks.append(
                    tT.issue(sy, lnh_sb[:, qa * 4 * TC:(qa + 1) * 4 * TC].rearrange("p (a f) -> p a f", a=4),
                             ln_scr[qa * 4:(qa + 1) * 4, :, :].rearrange("a p f -> p a f")))
            misc["lnh_toks"] = lnh_toks

    def group_copies():
        @block.gpsimd
        def _(g):
            g.wait_ge(ccm.h, ccm.n)
            s0 = gW.sems[0]
            KR = NCC * TPC * 128
            for arm in g.switch_core_id(n=128):
                ej = arm.logical % 4
                for r in range(4):
                    g.dma_start(out=v_loc[r * KR:(r + 1) * KR, :],
                                in_=v_ag_v[r][:, ej * E:(ej + 1) * E]).then_inc(s0.h, 16)
            s0.n += 64
        misc["vloc_tok"] = (gW.sems[0].h, gW.sems[0].n)

    def p3():
        @block.vector
        def _(v):
            v.memset(kv_sb[:, :], 0.0).then_inc(vC.h, 1)
        vC.n += 1

        b_sC, b_tC, b_vC = sC.n, tC.n, vC.n
        per_t = ET + 4
        per_v = 4
        ld_t = {}
        ost_t = {}

        @block.sync
        def _(sy):
            sy.wait_ge(ccm.h, ccm.n)
            _wait(sy, misc["vloc_tok"])
            sy.wait_ge(tC.h, misc["p25_end_tC"])
            sy.wait_ge(p5aT.h, p5aT.n)
            for ch in range(NCB):
                if ch >= 2:
                    sy.wait_ge(tC.h, b_tC + per_t * (ch - 1))
                r, l = ch // NCC, ch % NCC
                h0 = (ch % 2) * TPC * D
                t1 = sX.issue(sy, qc_buf[:, (ch % 2) * DT * BS:(ch % 2 + 1) * DT * BS].rearrange("p (a f) -> p a f", a=DT),
                              q_ag_v[r][l, :, :, :].rearrange("a p f -> p a f"))
                t2 = sX.issue(sy, ktc_buf[:, h0:h0 + TPC * D].rearrange("p (t d) -> p t d", t=TPC),
                              ktm_ag_v[r][l, :, :, :].rearrange("t p d -> p t d"))
                t3 = sX.issue(sy, vte_buf[:, (ch % 2) * TPC * E:(ch % 2 + 1) * TPC * E].rearrange("p (t e) -> p t e", t=TPC),
                              v_loc_b[r, l, :, :, :].rearrange("t p e -> p t e"))
                ld_t[ch] = [t1, t2, t3]

        @block.scalar
        def _(s):
            for ch in range(NCB):
                s.wait_ge(vC.h, b_vC + per_v * ch)
                s.copy(out=kv_bf[:, :], in_=kv_sb[:, :]).then_inc(sC.h, 1)
                if ch >= 1:
                    ost_t[ch - 1] = sS.issue(s, oint_pub_b[ch - 1, :, :, :].rearrange("e p f -> p e f"),
                                             oi3_sb[:, :].rearrange("p (e f) -> p e f", e=ET))
            s.wait_ge(vC.h, b_vC + per_v * NCB)
            ost_t[NCB - 1] = sS.issue(s, oint_pub_b[NCB - 1, :, :, :].rearrange("e p f -> p e f"),
                                      oi3_sb[:, :].rearrange("p (e f) -> p e f", e=ET))
            _wait_all(s, sS.totals())
        sC.n += NCB

        @block.tensor
        def _(t):
            t.wait_ge(p5aS.h, p5aS.n)
            t.wait_ge(p5aV.h, p5aV.n)
            for ch in range(NCB):
                t.wait_ge(sC.h, b_sC + ch + 1)
                _wait_all(t, ld_t[ch])
                if ch >= 1:
                    t.wait_ge(vC.h, b_vC + per_v * (ch - 1) + 1)
                qc0 = (ch % 2) * DT * BS
                h0 = (ch % 2) * TPC * D
                vt0 = (ch % 2) * TPC * E
                for et in range(ET):
                    for a in range(DT):
                        mmi = t.matmul(pb[[6, 7, 0, 1][et]][:, 0:BS],
                                 kv_bf[:, a * E + et * 128:a * E + (et + 1) * 128],
                                 qc_buf[:, qc0 + a * BS:qc0 + (a + 1) * BS],
                                 start=(a == 0), stop=(a == DT - 1))
                    mmi.then_inc(tC.h, 1)
                nq = max(1, DT // 4)
                for qd in range(4):
                    t.wait_ge(vC.h, b_vC + per_v * ch + qd)
                    for ai in range(nq):
                        a = qd * nq + ai
                        if a >= DT:
                            continue
                        for tp in range(TPC):
                            mmi = t.matmul(pb[2 + (a % 4)][:, 0:E],
                                     ktc_buf[:, h0 + tp * D + a * 128:h0 + tp * D + (a + 1) * 128],
                                     vte_buf[:, vt0 + tp * E:vt0 + (tp + 1) * E],
                                     start=(tp == 0), stop=(tp == TPC - 1))
                    mmi.then_inc(tC.h, 1)
        tC.n += per_t * NCB

        @block.vector
        def _(v):
            _wait_all(v, misc["p25_oist"])
            for ch in range(NCB):
                v.wait_ge(tC.h, b_tC + ch * per_t + ET)
                if ch >= 1:
                    _wait(v, ost_t[ch - 1])
                ins = None
                for et in range(ET):
                    ins = v.tensor_copy(out=oi3_sb[:, et * BS:(et + 1) * BS],
                                        in_=pb[[6, 7, 0, 1][et]][:, 0:BS])
                nq = max(1, DT // 4)
                for qd in range(4):
                    v.wait_ge(tC.h, b_tC + ch * per_t + ET + qd + 1)
                    for ai in range(nq):
                        a = qd * nq + ai
                        if a >= DT:
                            continue
                        ins = v.tensor_tensor(out=kv_sb[:, a * E:(a + 1) * E],
                                              in0=kv_sb[:, a * E:(a + 1) * E],
                                              in1=pb[2 + (a % 4)][:, 0:E],
                                              op=ALU.add)
                    ins.then_inc(vC.h, 1)
        vC.n += per_v * NCB

        misc["p3_end_tC"] = tC.n
        misc["p3_sS"] = sS.totals()

    def oint_gather():
        @block.gpsimd
        def _(g):
            _wait_all(g, misc["p3_sS"])
            g.collective_compute("AllGather", ALU.bypass, replica_groups=RG4,
                                 ins=[oint_pub.opt()], outs=[oint_ag.opt()]).then_inc(cco.h, 1)
        cco.n += 1

    def oint_copy():
        @block.gpsimd
        def _(g):
            g.wait_ge(cco.h, cco.n)
            s1 = gW.sems[1]
            for arm in g.switch_core_id(n=128):
                q4 = arm.logical % 4
                for r in range(4):
                    base = r * NCB * E + q4 * NCC * E
                    g.dma_start(out=oint_loc[r * NCC * E:(r + 1) * NCC * E, :],
                                in_=oint_ag[base:base + NCC * E, :]).then_inc(s1.h, 16)
            s1.n += 64
        misc["oint_tok"] = (gW.sems[1].h, gW.sems[1].n)

    def emit_p5a(h0, h1):
        ntc = TC // 512
        w_t = misc.setdefault("p5a_wt", {})
        gst_t = misc.setdefault("p5a_gst", {})
        pre_sS = sS.totals()
        wg1, wg2 = (wbA1, wbA2) if h0 == 0 else (w_buf, w_buf2)

        @block.sync
        def _(sy):
            if h0 > 0:
                _wait_all(sy, misc["p25_oist"])
            for h in range(h0, h1):
                if h >= 2:
                    sy.wait_ge(p5aT.h, 2 * ntc * (h - 1))
                elif h0 == 0:
                    sy.wait_ge(tC.h, misc["p25_end_tC"])
                _wait(sy, pc["wgate"][h // 8])
                _wait(sy, pc["wup"][h // 8])
                w_t[(h, 0)] = sX.issue(
                    sy, wg1[:, (h % 2) * DT * 128:(h % 2 + 1) * DT * 128],
                    wgate_bf[h, :, :])
                w_t[(h, 1)] = sX.issue(
                    sy, wg2[:, (h % 2) * DT * 128:(h % 2 + 1) * DT * 128],
                    wup_bf[h, :, :])

        @block.tensor
        def _(t):
            if h0 == 0:
                _wait_all(t, misc["lnh_toks"])
            t.wait_ge(sC.h, sC.n)
            t.wait_ge(vC.h, vC.n)
            for h in range(h0, h1):
                _wait(t, w_t[(h, 0)])
                _wait(t, w_t[(h, 1)])
                for tci in range(ntc):
                    if h >= 1:
                        t.wait_ge(p5aS.h, ntc * (h - 1) + tci + 1)
                    for a in range(DT):
                        mmi = t.matmul(pb[tci][:, :],
                                 wg1[:, (h % 2) * DT * 128 + a * 128:(h % 2) * DT * 128 + (a + 1) * 128],
                                 lnh_sb[:, a * TC + tci * 512:a * TC + (tci + 1) * 512],
                                 start=(a == 0), stop=(a == DT - 1))
                    mmi.then_inc(p5aT.h, 1)
                for tci in range(ntc):
                    if h >= 1:
                        t.wait_ge(p5aV.h, ntc * (h - 1) + tci + 1)
                    for a in range(DT):
                        mmi = t.matmul(pb[4 + tci][:, :],
                                 wg2[:, (h % 2) * DT * 128 + a * 128:(h % 2) * DT * 128 + (a + 1) * 128],
                                 lnh_sb[:, a * TC + tci * 512:a * TC + (tci + 1) * 512],
                                 start=(a == 0), stop=(a == DT - 1))
                    mmi.then_inc(p5aT.h, 1)
        p5aT.n += 2 * ntc * (h1 - h0)

        @block.scalar
        def _(s):
            for h in range(h0, h1):
                if h >= 2:
                    s.wait_ge(p5aV.h, ntc * (h - 1))
                for tci in range(ntc):
                    s.wait_ge(p5aT.h, h * 2 * ntc + tci + 1)
                    s.activation(out=silB_sb[:, (h % 2) * TC + tci * 512:(h % 2) * TC + (tci + 1) * 512],
                                 in_=pb[tci][:, :], func=AF.Silu).then_inc(p5aS.h, 1)
                if h >= 1:
                    s.wait_ge(p5aV.h, ntc * h)
                    gst_t[h - 1] = sS.issue(s, gated_scr[h - 1, :, :],
                                            gatB_sb[:, ((h - 1) % 4) * TC:((h - 1) % 4) * TC + TC])
            if h1 == HT:
                s.wait_ge(p5aV.h, ntc * HT)
                gst_t[HT - 1] = sS.issue(s, gated_scr[HT - 1, :, :],
                                         gatB_sb[:, ((HT - 1) % 4) * TC:((HT - 1) % 4) * TC + TC])
        p5aS.n += ntc * (h1 - h0)

        @block.vector
        def _(v):
            if h0 == 0:
                _wait_all(v, pre_sS)
            for h in range(h0, h1):
                if h >= 4:
                    _wait(v, gst_t[h - 4])
                for tci in range(ntc):
                    v.wait_ge(p5aT.h, h * 2 * ntc + ntc + tci + 1)
                    v.wait_ge(p5aS.h, ntc * h + tci + 1)
                    v.tensor_tensor(out=gatB_sb[:, (h % 4) * TC + tci * 512:(h % 4) * TC + (tci + 1) * 512],
                                    in0=silB_sb[:, (h % 2) * TC + tci * 512:(h % 2) * TC + (tci + 1) * 512],
                                    in1=pb[4 + tci][:, :], op=ALU.mult).then_inc(p5aV.h, 1)
        p5aV.n += ntc * (h1 - h0)

    def p4():
        nhc = NCC // 2
        ntc = TH // 512
        b_sC_em = sC.n
        for th in range(2):
            b_tC0 = tC.n
            b_vC = vC.n
            ld_t = {}
            x_t = {}
            xst_t = {}
            w_t = {}

            @block.sync
            def _(sy):
                if th == 0:
                    _wait_all(sy, misc["p3_sS"])
                    _wait(sy, misc["oint_tok"])
                    sy.wait_ge(p5aT.h, p5aT.n)
                else:
                    sy.wait_ge(tC.h, b_tC0)
                ld_t["gh"] = sX.issue(sy, gh_sb[:, :].rearrange("p (a f) -> p a f", a=DT),
                                      g_scr[:, :, th * TH:(th + 1) * TH].rearrange("a p f -> p a f"))
                for li in range(nhc):
                    l = th * nhc + li
                    if li >= 1:
                        sy.wait_ge(vC.h, b_vC + 2 * li)
                    tokens = [sX.issue(sy, oi4_sb[:, :].rearrange("p (a f) -> p a f", a=DT),
                                       oi_scr[l, :, :, :].rearrange("a p f -> p a f"))]
                    for r in range(4):
                        tokens.append(sX.issue(
                            sy, oint_buf[:, r * ET * BS:(r + 1) * ET * BS].rearrange("p (e f) -> p e f", e=ET),
                            oint_loc_b[r, l, :, :, :].rearrange("e p f -> p e f")))
                    ld_t[li] = tokens

            @block.vector
            def _(v):
                if th == 0:
                    v.wait_ge(p5aT.h, p5aT.n)
                else:
                    v.wait_ge(tC.h, b_tC0)
                _wait(v, ld_t["gh"])
                for li in range(nhc):
                    _wait_all(v, ld_t[li])
                    v.tensor_add(oi4_sb[:, :], oi4_sb[:, :], oint_buf[:, :]).then_inc(vC.h, 1)
                    v.wait_ge(vC.h, b_vC + 2 * li + 1)
                    v.tensor_tensor(
                        out=og_sb.rearrange("p (a f) -> p a f", a=DT)[:, :, li * BS:(li + 1) * BS],
                        in0=oi4_sb.rearrange("p (a f) -> p a f", a=DT),
                        in1=gh_sb.rearrange("p (a f) -> p a f", a=DT)[:, :, li * BS:(li + 1) * BS],
                        op=ALU.mult).then_inc(vC.h, 1)
            vC.n += 2 * nhc

            b_tC, b_vC2 = tC.n, vC.n

            @block.scalar
            def _(s):
                if th == 0:
                    _wait_all(s, sS.totals())
                    s.wait_ge(tC.h, misc["p3_end_tC"])
                    _wait(s, pc["wdown"][0])
                    misc["wd_q"] = {0: sL.issue(s, wdB_sb[:, :], wdown_bf[0, :, :])}
                for do in range(DT):
                    if do < 2:
                        if th == 0:
                            s.wait_ge(p5aT.h, p5aT.n)
                        else:
                            s.wait_ge(tC.h, b_tC)
                    else:
                        s.wait_ge(tC.h, b_tC + ntc * (do - 1))
                    _wait(s, pc["wp"][do // 4])
                    w_t[do] = sL.issue(
                        s, w_buf[:, (do % 2) * DT * 128:(do % 2 + 1) * DT * 128],
                        wp_bf[do, :, :])

            @block.sync
            def _(sy):
                for do in range(DT):
                    if do >= 4:
                        sy.wait_ge(vC.h, b_vC2 + ntc * (do - 3))
                    x_t[do] = sX.issue(sy, xf_buf[:, (do % 4) * TH:(do % 4 + 1) * TH],
                                       x_fm[do * 128:(do + 1) * 128, th * TH:(th + 1) * TH])
                    if do >= 1:
                        sy.wait_ge(vC.h, b_vC2 + ntc * do)
                        xst_t[do - 1] = tT.issue(sy, xattn_scr[do - 1, :, th * TH:(th + 1) * TH],
                                                 xa_buf[:, ((do - 1) % 2) * TH:((do - 1) % 2) * TH + TH])
                sy.wait_ge(vC.h, b_vC2 + ntc * DT)
                xst_t[DT - 1] = tT.issue(sy, xattn_scr[DT - 1, :, th * TH:(th + 1) * TH],
                                         xa_buf[:, ((DT - 1) % 2) * TH:((DT - 1) % 2) * TH + TH])
                _wait_all(sy, tT.totals())

            @block.tensor
            def _(t):
                t.wait_ge(vC.h, b_vC + 2 * nhc)
                for do in range(DT):
                    _wait(t, w_t[do])
                    for tci in range(ntc):
                        if do == 0 and tci == 0 and th == 0:
                            t.wait_ge(p5aS.h, p5aS.n)
                        if do >= 2:
                            t.wait_ge(vC.h, b_vC2 + ntc * (do - 2) + tci + 1)
                        for a in range(DT):
                            mmi = t.matmul(pb[(do % 2) * 2 + tci][:, :],
                                     w_buf[:, (do % 2) * DT * 128 + a * 128:(do % 2) * DT * 128 + (a + 1) * 128],
                                     og_sb[:, a * TH + tci * 512:a * TH + (tci + 1) * 512],
                                     start=(a == 0), stop=(a == DT - 1))
                        mmi.then_inc(tC.h, 1)
            tC.n += ntc * DT

            @block.vector
            def _(v):
                for do in range(DT):
                    _wait(v, x_t[do])
                    if do >= 2:
                        _wait(v, xst_t[do - 2])
                    for tci in range(ntc):
                        v.wait_ge(tC.h, b_tC + ntc * do + tci + 1)
                        v.tensor_tensor(out=xa_buf[:, (do % 2) * TH + tci * 512:(do % 2) * TH + (tci + 1) * 512],
                                        in0=pb[(do % 2) * 2 + tci][:, :],
                                        in1=xf_buf[:, (do % 4) * TH + tci * 512:(do % 4) * TH + (tci + 1) * 512],
                                        op=ALU.add).then_inc(vC.h, 1)
            vC.n += ntc * DT
        misc["p4_end_tC"] = tC.n
        misc["p4_tT"] = tT.totals()

    def p5b():
        nq = 4
        ndt = DQ // 128
        ntcf = TC // 512
        NCI = nq * ntcf
        b_tC, b_vC = tC.n, vC.n
        gld_t = {}
        xld_t = {}
        ost_t = {}
        wd_q = misc["wd_q"]
        wdq_sb = [wdB_sb, wdA_sb, wdB_sb, wdA_sb]

        @block.scalar
        def _(s):
            s.wait_ge(tC.h, misc["p4_end_tC"])
            _wait(s, pc["wdown"][1])
            wd_q[1] = sL.issue(s, wdA_sb[:, :], wdown_bf[1, :, :])
            s.wait_ge(p5aV.h, p5aV.n)
            _wait_all(s, misc["p3_sS"])
            for ci in range(NCI):
                q, tci = divmod(ci, ntcf)
                if ci == ntcf:
                    s.wait_ge(tC.h, b_tC + ntcf * HT)
                    _wait(s, pc["wdown"][2])
                    wd_q[2] = sL.issue(s, wdB_sb[:, :], wdown_bf[2, :, :])
                elif ci == 2 * ntcf:
                    s.wait_ge(tC.h, b_tC + 2 * ntcf * HT)
                    _wait(s, pc["wdown"][3])
                    wd_q[3] = sL.issue(s, wdA_sb[:, :], wdown_bf[3, :, :])
                for hp in range(HT // 2):
                    pj = ci * (HT // 2) + hp
                    h0 = 2 * hp
                    if pj >= 4:
                        s.wait_ge(tC.h, b_tC + 2 * pj - 6)
                    _wait(s, misc["p5a_gst"][h0])
                    _wait(s, misc["p5a_gst"][h0 + 1])
                    gld_t[pj] = sL.issue(
                        s, gld_buf[:, (pj % 4) * 1024:(pj % 4) * 1024 + 1024].rearrange("p (h f) -> p h f", h=2),
                        gated_scr[h0:h0 + 2, :, tci * 512:(tci + 1) * 512].rearrange("h p f -> p h f"))
                for a in range(ndt):
                    idx2 = ci * ndt + a
                    if idx2 < 4:
                        _wait_all(s, misc["p4_tT"])
                    else:
                        s.wait_ge(oT.sems[idx2 % 4].h, 16 * ((idx2 - 4) // 4 + 1))
                    xld_t[idx2] = sL.issue(s, xa_buf[:, (idx2 % 4) * 512:(idx2 % 4) * 512 + 512],
                                           xattn_scr[q * ndt + a, :, tci * 512:(tci + 1) * 512])

        @block.tensor
        def _(t):
            for q in range(nq):
                _wait(t, wd_q[q])
                for tci in range(ntcf):
                    ci = q * ntcf + tci
                    if ci == 0:
                        t.wait_ge(vC.h, b_vC)
                        t.wait_ge(p5aV.h, p5aV.n)
                    elif ci >= 2:
                        t.wait_ge(vC.h, b_vC + ndt * (ci - 1))
                    for h in range(HT):
                        pj = ci * (HT // 2) + h // 2
                        if h % 2 == 0:
                            _wait(t, gld_t[pj])
                        for a in range(ndt):
                            mmi = t.matmul(pb[(tci % 2) * 4 + a][:, :],
                                     wdq_sb[q][:, h * DQ + a * 128:h * DQ + (a + 1) * 128],
                                     gld_buf[:, (pj % 4) * 1024 + (h % 2) * 512:(pj % 4) * 1024 + (h % 2) * 512 + 512],
                                     start=(h == 0), stop=(h == HT - 1))
                        mmi.then_inc(tC.h, 1)
        tC.n += NCI * HT

        @block.vector
        def _(v):
            for ci in range(NCI):
                q, tci = divmod(ci, ntcf)
                v.wait_ge(tC.h, b_tC + (ci + 1) * HT)
                for a in range(ndt):
                    idx2 = ci * ndt + a
                    _wait(v, xld_t[idx2])
                    v.tensor_tensor(out=xa_buf[:, (idx2 % 4) * 512:(idx2 % 4) * 512 + 512],
                                    in0=pb[(tci % 2) * 4 + a][:, :],
                                    in1=xa_buf[:, (idx2 % 4) * 512:(idx2 % 4) * 512 + 512],
                                    op=ALU.add).then_inc(vC.h, 1)
        vC.n += NCI * ndt

        @block.sync
        def _(sy):
            for ci in range(NCI):
                q, tci = divmod(ci, ntcf)
                for a in range(ndt):
                    idx2 = ci * ndt + a
                    d_abs = q * ndt + a
                    sy.wait_ge(vC.h, b_vC + idx2 + 1)
                    ost_t[idx2] = oT.issue(sy, out_fm[d_abs * 128:(d_abs + 1) * 128,
                                                      tci * 512:(tci + 1) * 512],
                                           xa_buf[:, (idx2 % 4) * 512:(idx2 % 4) * 512 + 512])
            _wait_all(sy, tT.totals())
            _wait_all(sy, oT.totals())

    precast([("wk", wk_t, wk_bf, 4), ("wv", wv_t, wv_bf, 4)])
    p0()
    barrier()
    precast([("wg", wg_t, wg_bf, 4), ("wq", wq_t, wq_bf, 4),
             ("wgate", wgate_b, wgate_bf, 8), ("wup", wup_b, wup_bf, 8),
             ("wp", wp_t, wp_bf, 4), ("wdown", wdown_b, wdown_bf, 4)])
    if phases == 0.3:
        proj_fm(wk_bf, AF.Silu, "strips", "wk", dest=kfm_scr)
    elif phases == 0.6:
        proj_fm(wk_bf, AF.Silu, "strips_ktm", "wk", dest=kfm_scr)
    elif phases == 0.8:
        proj_fm(wk_bf, AF.Silu, "strips_ktm", "wk", dest=kfm_scr)

        @block.gpsimd
        def _(g):
            _wait_all(g, sS.totals())
            g.collective_compute("AllGather", ALU.bypass, replica_groups=RG4,
                                 ins=[mg_pub2d.opt()], outs=[mg_ag2d.opt()]).then_inc(ccm.h, 1)
            g.wait_ge(ccm.h, 1)
    if phases >= 1:
        p1()
    if phases >= 2:
        p25()
    if phases >= 3:
        group_copies()
    H1 = 48
    if phases >= 6:
        emit_p5a(0, H1)
    if phases >= 4:
        p3()
    if phases >= 6:
        emit_p5a(H1, HT)
    if phases >= 5:
        oint_gather()
        oint_copy()
    if phases >= 7:
        p4()
    if phases >= 8:
        p5b()
    if phases < 8:
        @block.sync
        def _(sy):
            _wait_all(sy, sX.totals())
            _wait_all(sy, tT.totals())
        @block.scalar
        def _(s):
            _wait_all(s, sS.totals())
            _wait_all(s, sL.totals())
        @block.gpsimd
        def _(g):
            _wait_all(g, gW.totals())

    if DEBUG_DUMP:
        scr_map = dict(ln_scr=ln_scr, g_scr=g_scr, kfm_scr=kfm_scr, oi_scr=oi_scr,
                       gated_scr=gated_scr, xattn_scr=xattn_scr, oint_pub=oint_pub,
                       v_loc=v_loc, oint_loc=oint_loc)
        ten = scr_map[DEBUG_DUMP]
        flat = ten.rearrange(" ".join(chr(97 + i) for i in range(len(ten.shape))) + " -> ("
                             + " ".join(chr(97 + i) for i in range(len(ten.shape))) + ")")
        dbg = nc.dram_tensor("dbg_out", list(ten.shape), ten.dtype, kind="ExternalOutput").ap()
        dbgf = dbg.rearrange(" ".join(chr(97 + i) for i in range(len(ten.shape))) + " -> ("
                             + " ".join(chr(97 + i) for i in range(len(ten.shape))) + ")")

        @block.sync
        def _(sy):
            _wait_all(sy, tT.totals())
            sy.dma_start(out=dbgf, in_=flat).then_inc(tT.sems[0].h, 16)
            tT.sems[0].n += 16
            sy.wait_ge(tT.sems[0].h, tT.sems[0].n)

    bg.__exit__(None, None, None)
    for g in reversed(guards):
        g.__exit__(None, None, None)
    return nc



def _stage_inputs(c: Cfg, x, Wq, Wk, Wv, Wg, Wp, w_gate, w_up, w_down):
    asc = np.ascontiguousarray
    DT = c.DT

    def blk(wT):
        a = wT.reshape(DT, 128, DT, 128).transpose(2, 1, 0, 3)
        return asc(a.reshape(DT, 128, DT * 128))

    wq_t = blk(Wq.T)
    wk_t = blk(Wk.T)
    wg_t = blk(Wg.T)
    wp_t = blk(Wp.T)
    EQ = c.D // 4
    wv_t = asc(Wv.T.reshape(DT, 128, 4, EQ).transpose(2, 1, 0, 3).reshape(4, 128, DT * EQ))
    wgate_b = asc(w_gate.T.reshape(DT, 128, c.HT, 128).transpose(2, 1, 0, 3).reshape(c.HT, 128, DT * 128))
    wup_b = asc(w_up.T.reshape(DT, 128, c.HT, 128).transpose(2, 1, 0, 3).reshape(c.HT, 128, DT * 128))
    DQ = c.D // 4
    wdown_b = asc(w_down.T.reshape(c.HT, 128, 4, DQ).transpose(2, 1, 0, 3).reshape(4, 128, c.HT * DQ))
    in_maps = []
    for core in range(8):
        b, q4 = core // 4, core % 4
        sl = x[b, q4 * c.TC:(q4 + 1) * c.TC, :]
        in_maps.append(dict(
            x_fm=asc(sl.T), x_tm=asc(sl),
            wq_t=wq_t, wk_t=wk_t, wv_t=wv_t, wg_t=wg_t, wp_t=wp_t,
            wgate_b=wgate_b, wup_b=wup_b, wdown_b=wdown_b,
        ))
    return in_maps


def run(c: Cfg, x, Wq, Wk, Wv, Wg, Wp, w_gate, w_up, w_down):
    nc = build_program(c)
    in_maps = _stage_inputs(c, x, Wq, Wk, Wv, Wg, Wp, w_gate, w_up, w_down)
    res = run_bass_kernel_spmd(nc, in_maps, core_ids=list(range(8)))
    out = np.empty((2, 4 * c.TC, c.D), np.float32)
    for core in range(8):
        b, q4 = core // 4, core % 4
        out[b, q4 * c.TC:(q4 + 1) * c.TC, :] = res.results[core]["out_fm"].T
    return out


def kernel(L0_attn_x, L0_attn_Wq, L0_attn_Wk, L0_attn_Wv, L0_attn_Wg, L0_attn_Wp,
           L0_ffn_w_gate, L0_ffn_w_up, L0_ffn_w_down):
    c = Cfg(D=2048, TC=2048, H=8192)
    return run(c, np.asarray(L0_attn_x), np.asarray(L0_attn_Wq), np.asarray(L0_attn_Wk),
               np.asarray(L0_attn_Wv), np.asarray(L0_attn_Wg), np.asarray(L0_attn_Wp),
               np.asarray(L0_ffn_w_gate), np.asarray(L0_ffn_w_up), np.asarray(L0_ffn_w_down))


# version 10
# speedup vs baseline: 1.1359x; 1.1359x over previous
import numpy as np
import ml_dtypes
import concourse.bass as bass
import concourse.mybir as mybir
from concourse.bass_utils import run_bass_kernel_spmd

F32 = mybir.dt.float32
BF16 = mybir.dt.bfloat16
AF = mybir.ActivationFunctionType
ALU = mybir.AluOpType

DEBUG_DUMP = None
LN_EPS = 1e-5
RMS_EPS = float(np.finfo(np.float32).eps)


class Cfg:
    def __init__(self, D, TC, H, BS=256):
        self.D = D
        self.TC = TC
        self.H = H
        self.BS = BS
        self.DT = D // 128
        self.TT = TC // 128
        self.E = D // 4
        self.ET = self.E // 128
        self.NCC = TC // BS
        self.NCB = 4 * self.NCC
        self.TH = TC // 2
        self.HT = H // 128
        self.TPC = BS // 128
        assert D % 512 == 0 and TC % 512 == 0 and BS == 256
        assert self.TH % 512 == 0 and H % 128 == 0


class SemS:
    def __init__(self, handle):
        self.h = handle
        self.n = 0


class Stream:

    def __init__(self, mk, name, k=2):
        self.sems = [mk(f"{name}{i}") for i in range(k)]
        self.k = k
        self._rr = 0

    def issue(self, eng, out, in_):
        s = self.sems[self._rr % self.k]
        self._rr += 1
        if s.n > 0:
            eng.wait_ge(s.h, s.n)
        eng.dma_start(out=out, in_=in_).then_inc(s.h, 16)
        s.n += 16
        return (s.h, s.n)

    def totals(self):
        return [(s.h, s.n) for s in self.sems]


def _wait(eng, tok):
    eng.wait_ge(tok[0], tok[1])


def _wait_all(eng, toks):
    for t in toks:
        eng.wait_ge(t[0], t[1])


def build_program(c: Cfg, phases=99):
    global DEBUG_DUMP
    nc = bass.Bass("TRN2", target_bir_lowering=False, detect_race_conditions=False)
    D, TC, DT, TT, BS, ET, E = c.D, c.TC, c.DT, c.TT, c.BS, c.ET, c.E
    NCC, NCB, TH, HT, TPC = c.NCC, c.NCB, c.TH, c.HT, c.TPC

    EQ = D // 4
    DQ = D // 4
    x_fm = nc.dram_tensor("x_fm", [D, TC], F32, kind="ExternalInput").ap()
    x_tm = nc.dram_tensor("x_tm", [TC, D], F32, kind="ExternalInput").ap()
    wq_t = nc.dram_tensor("wq_t", [DT, 128, DT * 128], F32, kind="ExternalInput").ap()
    wk_t = nc.dram_tensor("wk_t", [DT, 128, DT * 128], F32, kind="ExternalInput").ap()
    wv_t = nc.dram_tensor("wv_t", [4, 128, DT * EQ], F32, kind="ExternalInput").ap()
    wg_t = nc.dram_tensor("wg_t", [DT, 128, DT * 128], F32, kind="ExternalInput").ap()
    wp_t = nc.dram_tensor("wp_t", [DT, 128, DT * 128], F32, kind="ExternalInput").ap()
    wgate_b = nc.dram_tensor("wgate_b", [HT, 128, DT * 128], F32, kind="ExternalInput").ap()
    wup_b = nc.dram_tensor("wup_b", [HT, 128, DT * 128], F32, kind="ExternalInput").ap()
    wdown_b = nc.dram_tensor("wdown_b", [4, 128, HT * DQ], F32, kind="ExternalInput").ap()
    out_fm = nc.dram_tensor("out_fm", [D, TC], F32, kind="ExternalOutput").ap()

    ident_f = nc.inline_tensor(np.eye(128, dtype=np.float32), "ident_f").ap()
    ident_b = nc.inline_tensor(np.eye(128, dtype=np.float32).astype(ml_dtypes.bfloat16), "ident_b").ap()
    ones_f = nc.inline_tensor(np.ones((1, 128), np.float32), "ones_f").ap()
    mtri = np.triu(np.ones((BS, BS), np.float32))
    mtri = np.ascontiguousarray(
        mtri.reshape(TPC, 128, BS).transpose(1, 0, 2).reshape(128, TPC * BS)
    ).astype(ml_dtypes.bfloat16)
    mask_c = nc.inline_tensor(mtri, "mask_c").ap()

    def dram(name, shape, dt, **kw):
        return nc.dram_tensor(name, shape, dt, **kw).ap()

    ln_scr = dram("ln_scr", [DT, 128, TC], BF16)
    g_scr = dram("g_scr", [DT, 128, TC], BF16)
    kfm_scr = dram("kfm_scr", [DT, 128, TC], BF16)
    oi_scr = dram("oi_scr", [NCC, DT, 128, BS], BF16)
    gated_scr = dram("gated_scr", [HT, 128, TC], BF16)

    wq_bf = dram("wq_bf", [DT, 128, DT * 128], BF16)
    wk_bf = dram("wk_bf", [DT, 128, DT * 128], BF16)
    wv_bf = dram("wv_bf", [4, 128, DT * EQ], BF16)
    wg_bf = dram("wg_bf", [DT, 128, DT * 128], BF16)
    wp_bf = dram("wp_bf", [DT, 128, DT * 128], BF16)
    wgate_bf = dram("wgate_bf", [HT, 128, DT * 128], BF16)
    wup_bf = dram("wup_bf", [HT, 128, DT * 128], BF16)
    wdown_bf = dram("wdown_bf", [4, 128, HT * DQ], BF16)
    xattn_scr = dram("xattn_scr", [DT, 128, TC], F32)
    vecs_scr = dram("vecs_scr", [4 * TT, 128], F32)

    QSZ = NCC * DT * 128 * BS
    KSZ = NCC * TPC * 128 * D
    MGE = QSZ + 2 * KSZ
    mg_pub = dram("mg_pub", [MGE], BF16)
    mg_ag = dram("mg_ag", [4 * MGE], BF16)
    mg_pub2d = mg_pub.rearrange("(x c) -> x c", c=2048)
    mg_ag2d = mg_ag.rearrange("(x c) -> x c", c=2048)
    oint_pub = dram("oint_pub", [NCB * E, BS], BF16)
    oint_ag = dram("oint_ag", [4 * NCB * E, BS], BF16)

    v_loc = dram("v_loc", [4 * NCC * TPC * 128, E], BF16)
    oint_loc = dram("oint_loc", [4 * NCC * E, BS], BF16)

    q_pub_b = mg_pub[0:QSZ].rearrange("(l a p f) -> l a p f", l=NCC, a=DT, p=128)
    ktm_pub_b = mg_pub[QSZ:QSZ + KSZ].rearrange("(l t p d) -> l t p d", l=NCC, t=TPC, p=128)
    v_pub_b = mg_pub[QSZ + KSZ:MGE].rearrange("(l t p d) -> l t p d", l=NCC, t=TPC, p=128)
    q_ag_v = [mg_ag[r * MGE:r * MGE + QSZ].rearrange("(l a p f) -> l a p f", l=NCC, a=DT, p=128)
              for r in range(4)]
    ktm_ag_v = [mg_ag[r * MGE + QSZ:r * MGE + QSZ + KSZ].rearrange("(l t p d) -> l t p d", l=NCC, t=TPC, p=128)
                for r in range(4)]
    v_ag_v = [mg_ag[r * MGE + QSZ + KSZ:(r + 1) * MGE].rearrange("(x d) -> x d", d=D)
              for r in range(4)]
    v_loc_b = v_loc.rearrange("(r l t p) e -> r l t p e", r=4, l=NCC, t=TPC, p=128)
    oint_pub_b = oint_pub.rearrange("(ch e p) f -> ch e p f", ch=NCB, e=ET, p=128)
    oint_loc_b = oint_loc.rearrange("(r l e p) f -> r l e p f", r=4, l=NCC, e=ET, p=128)

    def sb(name, shape, dtype, off):
        return nc.alloc_sbuf_tensor_at(name, shape, dtype, offset=off).ap()

    AR = 16512
    A0, B0, C0 = AR, AR + DT * TC * 2, AR + 2 * DT * TC * 2
    rn_sb = sb("rn_sb", [128, DT * TC], BF16, A0)
    lnh_sb = sb("lnh_sb", [128, DT * TC], BF16, A0)
    og_sb = sb("og_sb", [128, DT * TH], BF16, A0)
    gh_sb = sb("gh_sb", [128, DT * TH], BF16, A0 + DT * TH * 2)
    wdA_sb = sb("wdA_sb", [128, HT * DQ], BF16, A0)
    ktm_sb = sb("ktm_sb", [128, TT * D], BF16, B0)
    p0need = 4 * TC * 4 + 2 * TC * 2 + 2 * TC * 4
    p0b = B0 if p0need <= DT * TC * 2 else C0 + 77824
    assert p0b + p0need <= 229344
    bcast = sb("bcast", [128, 4 * TC], F32, p0b)
    lnt_buf = sb("lnt_buf", [128, 2 * TC], BF16, p0b + 4 * TC * 4)
    tmp_v = sb("tmp_v", [128, TC], F32, p0b + 4 * TC * 4 + 2 * TC * 2)
    tmp_g = sb("tmp_g", [128, TC], F32, p0b + 4 * TC * 4 + 2 * TC * 2 + TC * 4)
    kv_sb = sb("kv_sb", [128, DT * E], F32, B0)
    kv_bf = sb("kv_bf", [128, DT * E], BF16, B0 + DT * E * 4)
    gatB_sb = sb("gatB_sb", [128, 4 * TC], BF16, B0 + 6 * TC * 4)
    silB_sb = sb("silB_sb", [128, 2 * TC], BF16, C0 + 71680)
    wdB_sb = sb("wdB_sb", [128, HT * DQ], BF16, B0)
    assert HT * DQ * 2 == DT * TC * 2

    xtm_buf = sb("xtm_buf", [128, 2 * D], F32, C0)
    wbA1 = sb("wbA1", [128, 2 * DT * 128], BF16, C0)
    wbA2 = sb("wbA2", [128, 2 * DT * 128], BF16, C0 + 8192)
    qc_buf = sb("qc_buf", [128, 2 * DT * BS], BF16, C0)
    xsq_buf = sb("xsq_buf", [128, 2 * D], F32, C0 + 16384)
    ktc_buf = sb("ktc_buf", [128, 2 * TPC * D], BF16, C0 + 16384)
    oint_buf = sb("oint_buf", [128, DT * BS], BF16, C0 + 16384)
    wvq_buf = sb("wvq_buf", [128, 2 * DT * EQ], BF16, C0 + 16384)
    xf_buf = sb("xf_buf", [128, 2 * TC], F32, C0 + 32768)
    vv_buf = sb("vv_buf", [128, 2 * TPC * D], BF16, C0 + 32768)
    vte_buf = sb("vte_buf", [128, 2 * TPC * E], BF16, C0 + 32768)
    oi4_sb = sb("oi4_sb", [128, DT * BS], BF16, C0 + 40960)
    bsrc_sb = sb("bsrc_sb", [1, 4 * TC], F32, C0)
    stat = sb("stat", [128, 18 * TT], F32, C0 + 49152)
    packT_sb = sb("packT_sb", [4 * TT, 128], F32, C0 + 50304)
    onesb = sb("onesb", [1, 128], F32, C0 + 50816)
    identf_sb = sb("identf_sb", [128, 128], F32, C0 + 51328)
    identb_sb = sb("identb_sb", [128, 128], BF16, C0 + 51840)
    mask_sb = sb("mask_sb", [128, TPC * BS], BF16, C0 + 52096)
    w_buf = sb("w_buf", [128, 2 * DT * 128], BF16, C0 + 53248)
    oi_sb = sb("oi_sb", [128, DT * BS], BF16, C0 + 53248)
    w_buf2 = sb("w_buf2", [128, 2 * DT * 128], BF16, C0 + 61440)
    xa_buf = sb("xa_buf", [128, 2 * 1024], F32, C0 + 61440)
    sT_sb = sb("sT_sb", [128, TPC * BS], BF16, C0 + 61440)
    strip_buf = sb("strip_buf", [128, 2 * TC], BF16, C0 + 69632)
    gld_buf = sb("gld_buf", [128, 8 * 512], BF16, C0 + 69632)
    oi3_sb = sb("oi3_sb", [128, ET * BS], BF16, C0 + 69632)
    assert 18 * TT * 4 <= 1152 and 4 * TT <= 128
    assert 2 * TPC * D * 2 <= 16384 and DT * BS * 2 <= 8192
    assert C0 + 77824 <= 229344, "sbuf overflow"

    pb = [nc.alloc_psum_tensor(f"pb{i}", [128, 512], F32).ap() for i in range(8)]

    guards = []

    def mk(name):
        g = nc.semaphore(name)
        h = g.__enter__()
        guards.append(g)
        return SemS(h)

    tC, vC, sC, gC = mk("tC"), mk("vC"), mk("sC"), mk("gC")
    p5aT, p5aS, p5aV = mk("p5aT"), mk("p5aS"), mk("p5aV")
    ccm, cco = mk("ccm"), mk("cco")
    barS = mk("bar")
    sX = Stream(mk, "sX", 4)
    tT = Stream(mk, "tT", 4)
    gW = Stream(mk, "gW", 4)
    sS = Stream(mk, "sS", 4)
    sL = Stream(mk, "sL", 4)
    oT = Stream(mk, "oT", 4)

    RG8 = [list(range(8))]
    RG4 = [[0, 1, 2, 3], [4, 5, 6, 7]]
    toks = {}
    misc = {}

    bg = nc.Block()
    block = bg.__enter__()

    def barrier():
        barS.n += 5
        tgt = barS.n
        for sec in (block.tensor, block.vector, block.scalar, block.gpsimd, block.sync):
            def _f(eng, tgt=tgt):
                eng.sem_inc(barS.h, 1)
                eng.wait_ge(barS.h, tgt)
            sec(_f)

    def p0():
        xtm_t = {}
        xf_t = {}
        ln_t = {}
        b_sC, b_vC, b_gC = sC.n, vC.n, gC.n

        @block.sync
        def _(sy):
            for nm, t, src in (("ones", onesb, ones_f), ("idf", identf_sb, ident_f),
                               ("idb", identb_sb, ident_b), ("mask", mask_sb, mask_c)):
                toks[nm] = sX.issue(sy, t[:, :], src[:, :])
            for j in range(TT):
                if j >= 2:
                    sy.wait_ge(sC.h, b_sC + j - 1)
                    sy.wait_ge(vC.h, b_vC + 2 * (j - 1))
                xtm_t[j] = sX.issue(sy, xtm_buf[:, (j % 2) * D:(j % 2 + 1) * D],
                                    x_tm[j * 128:(j + 1) * 128, :])

        @block.scalar
        def _(s):
            for j in range(TT):
                _wait(s, xtm_t[j])
                if j >= 2:
                    s.wait_ge(vC.h, b_vC + 2 * (j - 1))
                s.activation(out=xsq_buf[:, (j % 2) * D:(j % 2 + 1) * D],
                             in_=xtm_buf[:, (j % 2) * D:(j % 2 + 1) * D],
                             func=AF.Square).then_inc(sC.h, 1)
            s.wait_ge(vC.h, b_vC + 2 * TT + 4)
            s.activation(out=stat[:, 5 * TT:6 * TT], in_=stat[:, 3 * TT:4 * TT],
                         func=AF.Ln, bias=stat[:, 17 * TT:17 * TT + 1]).then_inc(sC.h, 1)
            s.wait_ge(sC.h, b_sC + TT + 1)
            s.activation(out=stat[:, 6 * TT:7 * TT], in_=stat[:, 5 * TT:6 * TT],
                         func=AF.Exp, scale=-0.5).then_inc(sC.h, 1)
            s.wait_ge(vC.h, b_vC + 2 * TT + 6)
            s.activation(out=stat[:, 8 * TT:9 * TT], in_=stat[:, 7 * TT:8 * TT],
                         func=AF.Ln, bias=stat[:, 17 * TT + 1:17 * TT + 2]).then_inc(sC.h, 1)
            s.wait_ge(sC.h, b_sC + TT + 3)
            s.activation(out=stat[:, 9 * TT:10 * TT], in_=stat[:, 8 * TT:9 * TT],
                         func=AF.Exp, scale=-0.5).then_inc(sC.h, 1)
        sC.n += TT + 4

        @block.vector
        def _(v):
            v.memset(stat[:, 17 * TT:17 * TT + 1], LN_EPS)
            v.memset(stat[:, 17 * TT + 1:17 * TT + 2], RMS_EPS)
            for j in range(TT):
                _wait(v, xtm_t[j])
                v.reduce_sum(stat[:, j:j + 1], xtm_buf[:, (j % 2) * D:(j % 2 + 1) * D],
                             axis=mybir.AxisListType.X).then_inc(vC.h, 1)
                v.wait_ge(sC.h, b_sC + j + 1)
                v.reduce_sum(stat[:, TT + j:TT + j + 1], xsq_buf[:, (j % 2) * D:(j % 2 + 1) * D],
                             axis=mybir.AxisListType.X).then_inc(vC.h, 1)
            ecnt = [2 * TT]

            def edge(ins):
                ecnt[0] += 1
                ins.then_inc(vC.h, 1)
                v.wait_ge(vC.h, b_vC + ecnt[0])

            sm, sq = stat[:, 0:TT], stat[:, TT:2 * TT]
            m, var, tmp = stat[:, 2 * TT:3 * TT], stat[:, 3 * TT:4 * TT], stat[:, 4 * TT:5 * TT]
            edge(v.tensor_scalar_mul(m, sm, 1.0 / D))
            edge(v.tensor_scalar_mul(var, sq, 1.0 / D))
            edge(v.tensor_mul(tmp, m, m))
            edge(v.tensor_sub(var, var, tmp))
            assert ecnt[0] == 2 * TT + 4
            r1 = stat[:, 6 * TT:7 * TT]
            u = stat[:, 7 * TT:8 * TT]
            v.wait_ge(sC.h, b_sC + TT + 2)
            edge(v.tensor_mul(u, var, r1))
            edge(v.tensor_mul(u, u, r1))
            v.wait_ge(sC.h, b_sC + TT + 4)
            r2s = stat[:, 9 * TT:10 * TT]
            r2 = stat[:, 10 * TT:11 * TT]
            mr1 = stat[:, 11 * TT:12 * TT]
            mr2 = stat[:, 12 * TT:13 * TT]
            edge(v.tensor_mul(r2, r1, r2s))
            edge(v.tensor_mul(mr1, m, r1))
            edge(v.tensor_mul(mr2, m, r2))
            edge(v.tensor_copy(out=stat[:, 13 * TT:14 * TT], in_=r1))
            edge(v.tensor_copy(out=stat[:, 14 * TT:15 * TT], in_=mr1))
            edge(v.tensor_copy(out=stat[:, 15 * TT:16 * TT], in_=r2))
            edge(v.tensor_copy(out=stat[:, 16 * TT:17 * TT], in_=mr2))
        vC.n += 2 * TT + 13

        b_tC = tC.n

        @block.tensor
        def _(t):
            t.wait_ge(vC.h, b_vC + 2 * TT + 13)
            _wait(t, toks["idf"])
            t.transpose(pb[4][0:4 * TT, 0:128], stat[:, 13 * TT:17 * TT],
                        identf_sb[:, :]).then_inc(tC.h, 1)
        tC.n += 1

        @block.vector
        def _(v):
            v.wait_ge(tC.h, b_tC + 1)
            v.tensor_copy(out=packT_sb[:, :], in_=pb[4][0:4 * TT, 0:128]).then_inc(vC.h, 1)
        vC.n += 1

        @block.scalar
        def _(s):
            s.wait_ge(vC.h, vC.n)
            toks["vecs"] = sS.issue(s, vecs_scr[:, :], packT_sb[:, :])

        @block.sync
        def _(sy):
            _wait(sy, toks["vecs"])
            toks["bsrc"] = sX.issue(
                sy, bsrc_sb[0:1, :], vecs_scr.rearrange("a p -> (a p)").unsqueeze(0))

        nch = TC // 512
        b_tC2 = tC.n
        b_vC2 = vC.n

        @block.tensor
        def _(t):
            _wait(t, toks["bsrc"])
            _wait(t, toks["ones"])
            idx = 0
            for vv in range(4):
                for ch in range(nch):
                    if idx >= 4:
                        t.wait_ge(vC.h, b_vC2 + idx - 3)
                    mmi = t.matmul(pb[idx % 4][:, :], onesb[0:1, :],
                             bsrc_sb[0:1, vv * TC + ch * 512: vv * TC + (ch + 1) * 512],
                             start=True, stop=True).then_inc(tC.h, 1)
                    idx += 1
        tC.n += 4 * nch

        @block.vector
        def _(v):
            idx = 0
            for vv in range(4):
                for ch in range(nch):
                    v.wait_ge(tC.h, b_tC2 + idx + 1)
                    v.tensor_copy(out=bcast[:, vv * TC + ch * 512: vv * TC + (ch + 1) * 512],
                                  in_=pb[idx % 4][:, :]).then_inc(vC.h, 1)
                    idx += 1
        vC.n += 4 * nch

        b_vC3 = vC.n
        b_gC3 = gC.n

        @block.sync
        def _(sy):
            for d in range(DT):
                if d >= 2:
                    sy.wait_ge(vC.h, b_vC3 + 2 * (d - 1))
                    sy.wait_ge(gC.h, b_gC3 + 2 * (d - 1))
                xf_t[d] = sX.issue(sy, xf_buf[:, (d % 2) * TC:(d % 2 + 1) * TC],
                                   x_fm[d * 128:(d + 1) * 128, :])

        @block.gpsimd
        def _(g):
            g.wait_ge(vC.h, b_vC3)
            for d in range(DT):
                _wait(g, xf_t[d])
                xf = xf_buf[:, (d % 2) * TC:(d % 2 + 1) * TC]
                g.tensor_mul(tmp_g[:, :], xf, bcast[:, 2 * TC:3 * TC]).then_inc(gC.h, 1)
                g.wait_ge(gC.h, b_gC3 + 2 * d + 1)
                g.tensor_tensor(out=rn_sb[:, d * TC:(d + 1) * TC],
                                in0=tmp_g[:, :], in1=bcast[:, 3 * TC:4 * TC],
                                op=ALU.subtract).then_inc(gC.h, 1)
        gC.n += 2 * DT

        @block.scalar
        def _(s):
            for d in range(DT):
                s.wait_ge(vC.h, b_vC3 + 2 * (d + 1))
                ln_t[d] = sS.issue(s, ln_scr[d, :, :],
                                   lnt_buf[:, (d % 2) * TC:(d % 2 + 1) * TC])
            _wait_all(s, sS.totals())

        @block.vector
        def _(v):
            for d in range(DT):
                _wait(v, xf_t[d])
                if d >= 2:
                    _wait(v, ln_t[d - 2])
                xf = xf_buf[:, (d % 2) * TC:(d % 2 + 1) * TC]
                v.tensor_mul(tmp_v[:, :], xf, bcast[:, 0:TC]).then_inc(vC.h, 1)
                v.wait_ge(vC.h, b_vC3 + 2 * d + 1)
                v.tensor_tensor(out=lnt_buf[:, (d % 2) * TC:(d % 2 + 1) * TC],
                                in0=tmp_v[:, :], in1=bcast[:, TC:2 * TC],
                                op=ALU.subtract).then_inc(vC.h, 1)
        vC.n += 2 * DT


    pc = {}
    pcq = []

    def _fill_pcq():
        for i in range(16):
            pcq.append(("wgate", i, wgate_bf[4*i:4*i+4, :, :], wgate_b[4*i:4*i+4, :, :]))
            pcq.append(("wup", i, wup_bf[4*i:4*i+4, :, :], wup_b[4*i:4*i+4, :, :]))
        for i in range(4):
            pcq.append(("wp", i, wp_bf[4*i:4*i+4, :, :], wp_t[4*i:4*i+4, :, :]))
        for q in range(4):
            for i in range(4):
                pcq.append(("wdown", 4*q + i,
                            wdown_bf[q, :, i*8192:(i+1)*8192],
                            wdown_b[q, :, i*8192:(i+1)*8192]))

    def emit_piece(g, n=1):
        for _ in range(n):
            if not pcq:
                return
            name, i, dst, srcap = pcq.pop(0)
            pc.setdefault(name, {})[i] = gW.issue(g, dst, srcap)

    def proj_fm(wt_ap, act_fn, kind, pcname, dest=None):
        nch = TC // 512
        b_tC, b_sC, b_vC, b_gC = tC.n, sC.n, vC.n, gC.n
        per_o_t = nch + (TT if kind == "strips_ktm" else 0)
        w_t = {}
        mm_mil = {}
        tp_mil = {}

        @block.gpsimd
        def _(g):
            for o in range(DT):
                if o < 2:
                    g.wait_ge(tC.h, b_tC)
                else:
                    g.wait_ge(tC.h, b_tC + per_o_t * (o - 1))
                w_t[o] = gW.issue(
                    g,
                    w_buf[:, (o % 2) * DT * 128:(o % 2 + 1) * DT * 128],
                    wt_ap[o, :, :])
                emit_piece(g)

        @block.tensor
        def _(t):
            mil = b_tC

            def emit_tp(s_, j):
                nonlocal mil
                lin = s_ * TT + j
                if lin >= 4:
                    t.wait_ge(vC.h, b_vC + lin - 3)
                t.transpose(pb[4 + (j % 4)].bitcast(BF16)[:, 0:128],
                            strip_buf[:, (s_ % 2) * TC + j * 128:(s_ % 2) * TC + (j + 1) * 128],
                            identb_sb[:, :]).then_inc(tC.h, 1)
                mil += 1
                tp_mil[(s_, j)] = mil

            for o in range(DT):
                _wait(t, w_t[o])
                if o == 0:
                    t.wait_ge(sC.h, b_sC)
                    t.wait_ge(gC.h, b_gC)
                for tci in range(nch):
                    if o >= 1:
                        t.wait_ge(sC.h, b_sC + nch * (o - 1) + tci + 1)
                    for a in range(DT):
                        mmi = t.matmul(pb[tci % 4][:, :],
                                 w_buf[:, (o % 2) * DT * 128 + a * 128:(o % 2) * DT * 128 + (a + 1) * 128],
                                 rn_sb[:, a * TC + tci * 512:a * TC + (tci + 1) * 512],
                                 start=(a == 0), stop=(a == DT - 1))
                    mmi.then_inc(tC.h, 1)
                    mil += 1
                    mm_mil[(o, tci)] = mil
                if kind == "strips_ktm" and o >= 1:
                    t.wait_ge(sC.h, b_sC + nch * o)
                    for j in range(TT):
                        emit_tp(o - 1, j)
            if kind == "strips_ktm":
                t.wait_ge(sC.h, b_sC + nch * DT)
                for j in range(TT):
                    emit_tp(DT - 1, j)
        tC.n += nch * DT + (TT * DT if kind == "strips_ktm" else 0)

        st_toks = {}

        @block.scalar
        def _(s):
            _wait_all(s, sS.totals())
            for o in range(DT):
                if kind == "strips_ktm" and o >= 2:
                    s.wait_ge(tC.h, tp_mil[(o - 2, TT - 1)])
                if kind != "assemble" and o >= 2:
                    _wait(s, st_toks[o - 2])
                for tci in range(nch):
                    s.wait_ge(tC.h, mm_mil[(o, tci)])
                    if kind == "assemble":
                        dst = ktm_sb[:, o * TC + tci * 512:o * TC + (tci + 1) * 512]
                    else:
                        dst = strip_buf[:, (o % 2) * TC + tci * 512:(o % 2) * TC + (tci + 1) * 512]
                    s.activation(out=dst, in_=pb[tci % 4][:, :], func=act_fn).then_inc(sC.h, 1)
                if kind != "assemble":
                    s.wait_ge(sC.h, b_sC + nch * (o + 1))
                    st_toks[o] = sS.issue(s, dest[o, :, :], strip_buf[:, (o % 2) * TC:(o % 2 + 1) * TC])
        sC.n += nch * DT

        if kind == "strips_ktm":
            @block.vector
            def _(v):
                for s_ in range(DT):
                    for j in range(TT):
                        v.wait_ge(tC.h, tp_mil[(s_, j)])
                        v.tensor_copy(out=ktm_sb[:, j * D + s_ * 128:j * D + (s_ + 1) * 128],
                                      in_=pb[4 + (j % 4)].bitcast(BF16)[:, 0:128]).then_inc(vC.h, 1)
            vC.n += DT * TT

    def p1():
        proj_fm(wk_t, AF.Silu, "strips_ktm", "wk", dest=kfm_scr)
        b_vC = vC.n

        @block.scalar
        def _(s):
            s.wait_ge(vC.h, b_vC)
            for l in range(NCC):
                sS.issue(s, ktm_pub_b[l, :, :, :].rearrange("t p d -> p t d"),
                         ktm_sb.rearrange("p (t d) -> p t d", t=TT)[:, TPC * l:TPC * (l + 1), :])
            _wait_all(s, sS.totals())

        b_tC = tC.n
        b_sC = sC.n
        wv_tok = {}
        vst_t = {}

        @block.gpsimd
        def _(g):
            for ev in range(4):
                if ev >= 2:
                    g.wait_ge(tC.h, b_tC + TT * (ev - 1))
                wv_tok[ev] = gW.issue(g, wvq_buf[:, (ev % 2) * DT * EQ:(ev % 2 + 1) * DT * EQ],
                                      wv_t[ev, :, :])
                emit_piece(g)

        @block.tensor
        def _(t):
            for ev in range(4):
                _wait(t, wv_tok[ev])
                for j in range(TT):
                    m = ev * TT + j
                    if m == 0:
                        t.wait_ge(sC.h, b_sC)
                    elif m >= 4:
                        t.wait_ge(sC.h, b_sC + m - 3)
                    for a in range(DT):
                        mmi = t.matmul(pb[m % 4][:, 0:EQ],
                                 rn_sb[:, a * TC + j * 128:a * TC + (j + 1) * 128],
                                 wvq_buf[:, (ev % 2) * DT * EQ + a * EQ:(ev % 2) * DT * EQ + (a + 1) * EQ],
                                 start=(a == 0), stop=(a == DT - 1))
                    mmi.then_inc(tC.h, 1)
        tC.n += 4 * TT

        @block.scalar
        def _(s):
            _wait_all(s, sS.totals())
            for ev in range(4):
                for j in range(TT):
                    m = ev * TT + j
                    if m >= 4:
                        _wait(s, vst_t[m - 4])
                    s.wait_ge(tC.h, b_tC + m + 1)
                    s.activation(out=strip_buf[:, (m % 4) * EQ:(m % 4 + 1) * EQ],
                                 in_=pb[m % 4][:, 0:EQ], func=AF.Silu).then_inc(sC.h, 1)
                    s.wait_ge(sC.h, b_sC + m + 1)
                    vst_t[m] = sS.issue(s, v_pub_b[j // TPC, j % TPC, :, ev * EQ:(ev + 1) * EQ],
                                        strip_buf[:, (m % 4) * EQ:(m % 4 + 1) * EQ])
        sC.n += 4 * TT

        proj_fm(wg_t, AF.Sigmoid, "strips", "wg", dest=g_scr)

        proj_fm(wq_t, AF.Silu, "assemble", "wq")
        b_sC3 = sC.n

        @block.scalar
        def _(s):
            s.wait_ge(sC.h, b_sC3)
            for l in range(NCC):
                sS.issue(s, q_pub_b[l, :, :, :].rearrange("a p t -> p a t"),
                         ktm_sb.rearrange("p (a t) -> p a t", a=DT)[:, :, l * BS:(l + 1) * BS])
            _wait_all(s, sS.totals())

        misc["g_end_tC"] = tC.n

        @block.gpsimd
        def _(g):
            emit_piece(g, n=len(pcq))
            _wait_all(g, sS.totals())
            g.collective_compute("AllGather", ALU.bypass, replica_groups=RG4,
                                 ins=[mg_pub2d.opt()], outs=[mg_ag2d.opt()]).then_inc(ccm.h, 1)
        ccm.n += 1

    def p25():
        b_tC, b_vC = tC.n, vC.n
        nr = (DT + 7) // 8
        per_t = TPC + nr
        per_v = TPC + nr
        ld_t = {}
        oist_t = {}

        @block.scalar
        def _(s):
            _wait_all(s, sS.totals())
            for l in range(NCC):
                if l >= 2:
                    s.wait_ge(tC.h, b_tC + per_t * (l - 1))
                h0 = (l % 2) * TPC * D
                t1 = sL.issue(s, qc_buf[:, (l % 2) * DT * BS:(l % 2 + 1) * DT * BS].rearrange("p (a f) -> p a f", a=DT),
                              q_pub_b[l, :, :, :].rearrange("a p f -> p a f"))
                t2 = sL.issue(s, ktc_buf[:, h0:h0 + DT * BS].rearrange("p (a f) -> p a f", a=DT),
                              kfm_scr[:, :, l * BS:(l + 1) * BS].rearrange("a p f -> p a f"))
                t3 = sL.issue(s, vv_buf[:, h0:h0 + TPC * D].rearrange("p (t d) -> p t d", t=TPC),
                              v_pub_b[l, :, :, :].rearrange("t p d -> p t d"))
                ld_t[l] = [t1, t2, t3]
                if l >= 1:
                    s.wait_ge(vC.h, b_vC + l * per_v)
                    oist_t[l - 1] = sS.issue(s, oi_scr[l - 1, :, :, :].rearrange("a p f -> p a f"),
                                             oi_sb.rearrange("p (a f) -> p a f", a=DT))
            s.wait_ge(vC.h, b_vC + NCC * per_v)
            oist_t[NCC - 1] = sS.issue(s, oi_scr[NCC - 1, :, :, :].rearrange("a p f -> p a f"),
                                       oi_sb.rearrange("p (a f) -> p a f", a=DT))
            _wait_all(s, sS.totals())

        @block.tensor
        def _(t):
            cnt = 0
            for l in range(NCC):
                _wait_all(t, ld_t[l])
                h0 = (l % 2) * TPC * D
                qc0 = (l % 2) * DT * BS
                for tp in range(TPC):
                    t.wait_ge(vC.h, b_vC + cnt)
                    for a in range(DT):
                        mmi = t.matmul(pb[6][:, 0:BS],
                                 ktc_buf[:, h0 + a * BS + tp * 128:h0 + a * BS + (tp + 1) * 128],
                                 qc_buf[:, qc0 + a * BS:qc0 + (a + 1) * BS],
                                 start=(a == 0), stop=(a == DT - 1))
                    mmi.then_inc(tC.h, 1)
                    cnt += 1
                for r_ in range(nr):
                    t.wait_ge(vC.h, b_vC + cnt)
                    for a in range(8 * r_, min(8 * r_ + 8, DT)):
                        for tp in range(TPC):
                            mmi = t.matmul(pb[a % 8][:, 0:BS],
                                     vv_buf[:, h0 + tp * D + a * 128:h0 + tp * D + (a + 1) * 128],
                                     sT_sb[:, tp * BS:(tp + 1) * BS],
                                     start=(tp == 0), stop=(tp == TPC - 1))
                    mmi.then_inc(tC.h, 1)
                    cnt += 1
        tC.n += per_t * NCC

        @block.vector
        def _(v):
            _wait(v, toks["mask"])
            for l in range(NCC):
                for tp in range(TPC):
                    v.wait_ge(tC.h, b_tC + l * per_t + tp + 1)
                    v.tensor_tensor(out=sT_sb[:, tp * BS:(tp + 1) * BS],
                                    in0=pb[6][:, 0:BS],
                                    in1=mask_sb[:, tp * BS:(tp + 1) * BS],
                                    op=ALU.mult).then_inc(vC.h, 1)
                for r_ in range(nr):
                    v.wait_ge(tC.h, b_tC + l * per_t + TPC + r_ + 1)
                    if r_ == 0 and l >= 1:
                        _wait(v, oist_t[l - 1])
                    ins = None
                    for a in range(8 * r_, min(8 * r_ + 8, DT)):
                        ins = v.tensor_copy(out=oi_sb[:, a * BS:(a + 1) * BS],
                                            in_=pb[a % 8][:, 0:BS])
                    ins.then_inc(vC.h, 1)
        vC.n += per_v * NCC
        misc["p25_end_tC"] = tC.n
        misc["p25_oist"] = sS.totals()

        @block.sync
        def _(sy):
            sy.wait_ge(tC.h, misc["g_end_tC"])
            lnh_toks = []
            for qa in range(4):
                lnh_toks.append(
                    tT.issue(sy, lnh_sb[:, qa * 4 * TC:(qa + 1) * 4 * TC].rearrange("p (a f) -> p a f", a=4),
                             ln_scr[qa * 4:(qa + 1) * 4, :, :].rearrange("a p f -> p a f")))
            misc["lnh_toks"] = lnh_toks

    def group_copies():
        @block.gpsimd
        def _(g):
            g.wait_ge(ccm.h, ccm.n)
            s0 = gW.sems[0]
            KR = NCC * TPC * 128
            for arm in g.switch_core_id(n=128):
                ej = arm.logical % 4
                for r in range(4):
                    g.dma_start(out=v_loc[r * KR:(r + 1) * KR, :],
                                in_=v_ag_v[r][:, ej * E:(ej + 1) * E]).then_inc(s0.h, 16)
            s0.n += 64
        misc["vloc_tok"] = (gW.sems[0].h, gW.sems[0].n)

    def p3():
        @block.vector
        def _(v):
            v.memset(kv_sb[:, :], 0.0).then_inc(vC.h, 1)
        vC.n += 1

        b_sC, b_tC, b_vC = sC.n, tC.n, vC.n
        per_t = ET + 4
        per_v = 4
        ld_t = {}
        ost_t = {}

        @block.sync
        def _(sy):
            sy.wait_ge(ccm.h, ccm.n)
            _wait(sy, misc["vloc_tok"])
            sy.wait_ge(tC.h, misc["p25_end_tC"])
            sy.wait_ge(p5aT.h, p5aT.n)
            for ch in range(NCB):
                if ch >= 2:
                    sy.wait_ge(tC.h, b_tC + per_t * (ch - 1))
                r, l = ch // NCC, ch % NCC
                h0 = (ch % 2) * TPC * D
                t1 = sX.issue(sy, qc_buf[:, (ch % 2) * DT * BS:(ch % 2 + 1) * DT * BS].rearrange("p (a f) -> p a f", a=DT),
                              q_ag_v[r][l, :, :, :].rearrange("a p f -> p a f"))
                t2 = sX.issue(sy, ktc_buf[:, h0:h0 + TPC * D].rearrange("p (t d) -> p t d", t=TPC),
                              ktm_ag_v[r][l, :, :, :].rearrange("t p d -> p t d"))
                t3 = sX.issue(sy, vte_buf[:, (ch % 2) * TPC * E:(ch % 2 + 1) * TPC * E].rearrange("p (t e) -> p t e", t=TPC),
                              v_loc_b[r, l, :, :, :].rearrange("t p e -> p t e"))
                ld_t[ch] = [t1, t2, t3]

        @block.scalar
        def _(s):
            for ch in range(NCB):
                s.wait_ge(vC.h, b_vC + per_v * ch)
                s.copy(out=kv_bf[:, :], in_=kv_sb[:, :]).then_inc(sC.h, 1)
                if ch >= 1:
                    ost_t[ch - 1] = sS.issue(s, oint_pub_b[ch - 1, :, :, :].rearrange("e p f -> p e f"),
                                             oi3_sb[:, :].rearrange("p (e f) -> p e f", e=ET))
            s.wait_ge(vC.h, b_vC + per_v * NCB)
            ost_t[NCB - 1] = sS.issue(s, oint_pub_b[NCB - 1, :, :, :].rearrange("e p f -> p e f"),
                                      oi3_sb[:, :].rearrange("p (e f) -> p e f", e=ET))
            _wait_all(s, sS.totals())
        sC.n += NCB

        @block.tensor
        def _(t):
            t.wait_ge(p5aS.h, p5aS.n)
            t.wait_ge(p5aV.h, p5aV.n)
            for ch in range(NCB):
                t.wait_ge(sC.h, b_sC + ch + 1)
                _wait_all(t, ld_t[ch])
                if ch >= 1:
                    t.wait_ge(vC.h, b_vC + per_v * (ch - 1) + 1)
                qc0 = (ch % 2) * DT * BS
                h0 = (ch % 2) * TPC * D
                vt0 = (ch % 2) * TPC * E
                for et in range(ET):
                    for a in range(DT):
                        mmi = t.matmul(pb[[6, 7, 0, 1][et]][:, 0:BS],
                                 kv_bf[:, a * E + et * 128:a * E + (et + 1) * 128],
                                 qc_buf[:, qc0 + a * BS:qc0 + (a + 1) * BS],
                                 start=(a == 0), stop=(a == DT - 1))
                    mmi.then_inc(tC.h, 1)
                nq = max(1, DT // 4)
                for qd in range(4):
                    t.wait_ge(vC.h, b_vC + per_v * ch + qd)
                    for ai in range(nq):
                        a = qd * nq + ai
                        if a >= DT:
                            continue
                        for tp in range(TPC):
                            mmi = t.matmul(pb[2 + (a % 4)][:, 0:E],
                                     ktc_buf[:, h0 + tp * D + a * 128:h0 + tp * D + (a + 1) * 128],
                                     vte_buf[:, vt0 + tp * E:vt0 + (tp + 1) * E],
                                     start=(tp == 0), stop=(tp == TPC - 1))
                    mmi.then_inc(tC.h, 1)
        tC.n += per_t * NCB

        @block.vector
        def _(v):
            _wait_all(v, misc["p25_oist"])
            for ch in range(NCB):
                v.wait_ge(tC.h, b_tC + ch * per_t + ET)
                if ch >= 1:
                    _wait(v, ost_t[ch - 1])
                ins = None
                for et in range(ET):
                    ins = v.tensor_copy(out=oi3_sb[:, et * BS:(et + 1) * BS],
                                        in_=pb[[6, 7, 0, 1][et]][:, 0:BS])
                nq = max(1, DT // 4)
                for qd in range(4):
                    v.wait_ge(tC.h, b_tC + ch * per_t + ET + qd + 1)
                    for ai in range(nq):
                        a = qd * nq + ai
                        if a >= DT:
                            continue
                        ins = v.tensor_tensor(out=kv_sb[:, a * E:(a + 1) * E],
                                              in0=kv_sb[:, a * E:(a + 1) * E],
                                              in1=pb[2 + (a % 4)][:, 0:E],
                                              op=ALU.add)
                    ins.then_inc(vC.h, 1)
        vC.n += per_v * NCB

        misc["p3_end_tC"] = tC.n
        misc["p3_sS"] = sS.totals()

    def oint_gather():
        @block.gpsimd
        def _(g):
            _wait_all(g, misc["p3_sS"])
            g.collective_compute("AllGather", ALU.bypass, replica_groups=RG4,
                                 ins=[oint_pub.opt()], outs=[oint_ag.opt()]).then_inc(cco.h, 1)
        cco.n += 1

    def oint_copy():
        @block.gpsimd
        def _(g):
            g.wait_ge(cco.h, cco.n)
            s1 = gW.sems[1]
            for arm in g.switch_core_id(n=128):
                q4 = arm.logical % 4
                for r in range(4):
                    base = r * NCB * E + q4 * NCC * E
                    g.dma_start(out=oint_loc[r * NCC * E:(r + 1) * NCC * E, :],
                                in_=oint_ag[base:base + NCC * E, :]).then_inc(s1.h, 16)
            s1.n += 64
        misc["oint_tok"] = (gW.sems[1].h, gW.sems[1].n)

    def emit_p5a(h0, h1):
        ntc = TC // 512
        w_t = misc.setdefault("p5a_wt", {})
        gst_t = misc.setdefault("p5a_gst", {})
        pre_sS = sS.totals()
        wg1, wg2 = (wbA1, wbA2) if h0 == 0 else (w_buf, w_buf2)

        @block.sync
        def _(sy):
            if h0 > 0:
                _wait_all(sy, misc["p25_oist"])
            for h in range(h0, h1):
                if h >= 2:
                    sy.wait_ge(p5aT.h, 2 * ntc * (h - 1))
                elif h0 == 0:
                    sy.wait_ge(tC.h, misc["p25_end_tC"])
                _wait(sy, pc["wgate"][h // 4])
                _wait(sy, pc["wup"][h // 4])
                w_t[(h, 0)] = sX.issue(
                    sy, wg1[:, (h % 2) * DT * 128:(h % 2 + 1) * DT * 128],
                    wgate_bf[h, :, :])
                w_t[(h, 1)] = sX.issue(
                    sy, wg2[:, (h % 2) * DT * 128:(h % 2 + 1) * DT * 128],
                    wup_bf[h, :, :])

        @block.tensor
        def _(t):
            if h0 == 0:
                _wait_all(t, misc["lnh_toks"])
            t.wait_ge(sC.h, sC.n)
            t.wait_ge(vC.h, vC.n)
            for h in range(h0, h1):
                _wait(t, w_t[(h, 0)])
                _wait(t, w_t[(h, 1)])
                for tci in range(ntc):
                    if h >= 1:
                        t.wait_ge(p5aS.h, ntc * (h - 1) + tci + 1)
                    for a in range(DT):
                        mmi = t.matmul(pb[tci][:, :],
                                 wg1[:, (h % 2) * DT * 128 + a * 128:(h % 2) * DT * 128 + (a + 1) * 128],
                                 lnh_sb[:, a * TC + tci * 512:a * TC + (tci + 1) * 512],
                                 start=(a == 0), stop=(a == DT - 1))
                    mmi.then_inc(p5aT.h, 1)
                for tci in range(ntc):
                    if h >= 1:
                        t.wait_ge(p5aV.h, ntc * (h - 1) + tci + 1)
                    for a in range(DT):
                        mmi = t.matmul(pb[4 + tci][:, :],
                                 wg2[:, (h % 2) * DT * 128 + a * 128:(h % 2) * DT * 128 + (a + 1) * 128],
                                 lnh_sb[:, a * TC + tci * 512:a * TC + (tci + 1) * 512],
                                 start=(a == 0), stop=(a == DT - 1))
                    mmi.then_inc(p5aT.h, 1)
        p5aT.n += 2 * ntc * (h1 - h0)

        @block.scalar
        def _(s):
            for h in range(h0, h1):
                if h >= 2:
                    s.wait_ge(p5aV.h, ntc * (h - 1))
                for tci in range(ntc):
                    s.wait_ge(p5aT.h, h * 2 * ntc + tci + 1)
                    s.activation(out=silB_sb[:, (h % 2) * TC + tci * 512:(h % 2) * TC + (tci + 1) * 512],
                                 in_=pb[tci][:, :], func=AF.Silu).then_inc(p5aS.h, 1)
                if h >= 1:
                    s.wait_ge(p5aV.h, ntc * h)
                    gst_t[h - 1] = sS.issue(s, gated_scr[h - 1, :, :],
                                            gatB_sb[:, ((h - 1) % 4) * TC:((h - 1) % 4) * TC + TC])
            if h1 == HT:
                s.wait_ge(p5aV.h, ntc * HT)
                gst_t[HT - 1] = sS.issue(s, gated_scr[HT - 1, :, :],
                                         gatB_sb[:, ((HT - 1) % 4) * TC:((HT - 1) % 4) * TC + TC])
        p5aS.n += ntc * (h1 - h0)

        @block.vector
        def _(v):
            if h0 == 0:
                _wait_all(v, pre_sS)
            for h in range(h0, h1):
                if h >= 4:
                    _wait(v, gst_t[h - 4])
                for tci in range(ntc):
                    v.wait_ge(p5aT.h, h * 2 * ntc + ntc + tci + 1)
                    v.wait_ge(p5aS.h, ntc * h + tci + 1)
                    v.tensor_tensor(out=gatB_sb[:, (h % 4) * TC + tci * 512:(h % 4) * TC + (tci + 1) * 512],
                                    in0=silB_sb[:, (h % 2) * TC + tci * 512:(h % 2) * TC + (tci + 1) * 512],
                                    in1=pb[4 + tci][:, :], op=ALU.mult).then_inc(p5aV.h, 1)
        p5aV.n += ntc * (h1 - h0)

    def p4():
        nhc = NCC // 2
        ntc = TH // 512
        b_sC_em = sC.n
        for th in range(2):
            b_tC0 = tC.n
            b_vC = vC.n
            ld_t = {}
            x_t = {}
            xst_t = {}
            w_t = {}

            @block.sync
            def _(sy):
                if th == 0:
                    _wait_all(sy, misc["p3_sS"])
                    _wait(sy, misc["oint_tok"])
                    sy.wait_ge(p5aT.h, p5aT.n)
                else:
                    sy.wait_ge(tC.h, b_tC0)
                ld_t["gh"] = sX.issue(sy, gh_sb[:, :].rearrange("p (a f) -> p a f", a=DT),
                                      g_scr[:, :, th * TH:(th + 1) * TH].rearrange("a p f -> p a f"))
                for li in range(nhc):
                    l = th * nhc + li
                    if li >= 1:
                        sy.wait_ge(vC.h, b_vC + 2 * li)
                    tokens = [sX.issue(sy, oi4_sb[:, :].rearrange("p (a f) -> p a f", a=DT),
                                       oi_scr[l, :, :, :].rearrange("a p f -> p a f"))]
                    for r in range(4):
                        tokens.append(sX.issue(
                            sy, oint_buf[:, r * ET * BS:(r + 1) * ET * BS].rearrange("p (e f) -> p e f", e=ET),
                            oint_loc_b[r, l, :, :, :].rearrange("e p f -> p e f")))
                    ld_t[li] = tokens

            @block.vector
            def _(v):
                if th == 0:
                    v.wait_ge(p5aT.h, p5aT.n)
                else:
                    v.wait_ge(tC.h, b_tC0)
                _wait(v, ld_t["gh"])
                for li in range(nhc):
                    _wait_all(v, ld_t[li])
                    v.tensor_add(oi4_sb[:, :], oi4_sb[:, :], oint_buf[:, :]).then_inc(vC.h, 1)
                    v.wait_ge(vC.h, b_vC + 2 * li + 1)
                    v.tensor_tensor(
                        out=og_sb.rearrange("p (a f) -> p a f", a=DT)[:, :, li * BS:(li + 1) * BS],
                        in0=oi4_sb.rearrange("p (a f) -> p a f", a=DT),
                        in1=gh_sb.rearrange("p (a f) -> p a f", a=DT)[:, :, li * BS:(li + 1) * BS],
                        op=ALU.mult).then_inc(vC.h, 1)
            vC.n += 2 * nhc

            b_tC, b_vC2 = tC.n, vC.n

            @block.scalar
            def _(s):
                if th == 0:
                    _wait_all(s, sS.totals())
                    s.wait_ge(tC.h, misc["p3_end_tC"])
                    _wait_all(s, [pc["wdown"][i] for i in range(4)])
                    misc["wd_q"] = {0: sL.issue(s, wdB_sb[:, :], wdown_bf[0, :, :])}
                for do in range(DT):
                    if do < 2:
                        if th == 0:
                            s.wait_ge(p5aT.h, p5aT.n)
                        else:
                            s.wait_ge(tC.h, b_tC)
                    else:
                        s.wait_ge(tC.h, b_tC + ntc * (do - 1))
                    _wait(s, pc["wp"][do // 4])
                    w_t[do] = sL.issue(
                        s, w_buf[:, (do % 2) * DT * 128:(do % 2 + 1) * DT * 128],
                        wp_bf[do, :, :])

            @block.sync
            def _(sy):
                for do in range(DT):
                    if do >= 4:
                        sy.wait_ge(vC.h, b_vC2 + ntc * (do - 3))
                    x_t[do] = sX.issue(sy, xf_buf[:, (do % 4) * TH:(do % 4 + 1) * TH],
                                       x_fm[do * 128:(do + 1) * 128, th * TH:(th + 1) * TH])
                    if do >= 1:
                        sy.wait_ge(vC.h, b_vC2 + ntc * do)
                        xst_t[do - 1] = tT.issue(sy, xattn_scr[do - 1, :, th * TH:(th + 1) * TH],
                                                 xa_buf[:, ((do - 1) % 2) * TH:((do - 1) % 2) * TH + TH])
                sy.wait_ge(vC.h, b_vC2 + ntc * DT)
                xst_t[DT - 1] = tT.issue(sy, xattn_scr[DT - 1, :, th * TH:(th + 1) * TH],
                                         xa_buf[:, ((DT - 1) % 2) * TH:((DT - 1) % 2) * TH + TH])
                _wait_all(sy, tT.totals())

            @block.tensor
            def _(t):
                t.wait_ge(vC.h, b_vC + 2 * nhc)
                for do in range(DT):
                    _wait(t, w_t[do])
                    for tci in range(ntc):
                        if do == 0 and tci == 0 and th == 0:
                            t.wait_ge(p5aS.h, p5aS.n)
                        if do >= 2:
                            t.wait_ge(vC.h, b_vC2 + ntc * (do - 2) + tci + 1)
                        for a in range(DT):
                            mmi = t.matmul(pb[(do % 2) * 2 + tci][:, :],
                                     w_buf[:, (do % 2) * DT * 128 + a * 128:(do % 2) * DT * 128 + (a + 1) * 128],
                                     og_sb[:, a * TH + tci * 512:a * TH + (tci + 1) * 512],
                                     start=(a == 0), stop=(a == DT - 1))
                        mmi.then_inc(tC.h, 1)
            tC.n += ntc * DT

            @block.vector
            def _(v):
                for do in range(DT):
                    _wait(v, x_t[do])
                    if do >= 2:
                        _wait(v, xst_t[do - 2])
                    for tci in range(ntc):
                        v.wait_ge(tC.h, b_tC + ntc * do + tci + 1)
                        v.tensor_tensor(out=xa_buf[:, (do % 2) * TH + tci * 512:(do % 2) * TH + (tci + 1) * 512],
                                        in0=pb[(do % 2) * 2 + tci][:, :],
                                        in1=xf_buf[:, (do % 4) * TH + tci * 512:(do % 4) * TH + (tci + 1) * 512],
                                        op=ALU.add).then_inc(vC.h, 1)
            vC.n += ntc * DT
        misc["p4_end_tC"] = tC.n
        misc["p4_tT"] = tT.totals()

    def p5b():
        nq = 4
        ndt = DQ // 128
        ntcf = TC // 512
        NCI = nq * ntcf
        b_tC, b_vC = tC.n, vC.n
        gld_t = {}
        xld_t = {}
        ost_t = {}
        wd_q = misc["wd_q"]
        wdq_sb = [wdB_sb, wdA_sb, wdB_sb, wdA_sb]

        @block.scalar
        def _(s):
            s.wait_ge(tC.h, misc["p4_end_tC"])
            _wait_all(s, [pc["wdown"][4 + i] for i in range(4)])
            wd_q[1] = sL.issue(s, wdA_sb[:, :], wdown_bf[1, :, :])
            s.wait_ge(p5aV.h, p5aV.n)
            _wait_all(s, misc["p3_sS"])
            for ci in range(NCI):
                q, tci = divmod(ci, ntcf)
                if ci == ntcf:
                    s.wait_ge(tC.h, b_tC + ntcf * HT)
                    _wait_all(s, [pc["wdown"][8 + i] for i in range(4)])
                    wd_q[2] = sL.issue(s, wdB_sb[:, :], wdown_bf[2, :, :])
                elif ci == 2 * ntcf:
                    s.wait_ge(tC.h, b_tC + 2 * ntcf * HT)
                    _wait_all(s, [pc["wdown"][12 + i] for i in range(4)])
                    wd_q[3] = sL.issue(s, wdA_sb[:, :], wdown_bf[3, :, :])
                for hp in range(HT // 2):
                    pj = ci * (HT // 2) + hp
                    h0 = 2 * hp
                    if pj >= 4:
                        s.wait_ge(tC.h, b_tC + 2 * pj - 6)
                    _wait(s, misc["p5a_gst"][h0])
                    _wait(s, misc["p5a_gst"][h0 + 1])
                    gld_t[pj] = sL.issue(
                        s, gld_buf[:, (pj % 4) * 1024:(pj % 4) * 1024 + 1024].rearrange("p (h f) -> p h f", h=2),
                        gated_scr[h0:h0 + 2, :, tci * 512:(tci + 1) * 512].rearrange("h p f -> p h f"))
                for a in range(ndt):
                    idx2 = ci * ndt + a
                    if idx2 < 4:
                        _wait_all(s, misc["p4_tT"])
                    else:
                        s.wait_ge(oT.sems[idx2 % 4].h, 16 * ((idx2 - 4) // 4 + 1))
                    xld_t[idx2] = sL.issue(s, xa_buf[:, (idx2 % 4) * 512:(idx2 % 4) * 512 + 512],
                                           xattn_scr[q * ndt + a, :, tci * 512:(tci + 1) * 512])

        @block.tensor
        def _(t):
            for q in range(nq):
                _wait(t, wd_q[q])
                for tci in range(ntcf):
                    ci = q * ntcf + tci
                    if ci == 0:
                        t.wait_ge(vC.h, b_vC)
                        t.wait_ge(p5aV.h, p5aV.n)
                    elif ci >= 2:
                        t.wait_ge(vC.h, b_vC + ndt * (ci - 1))
                    for h in range(HT):
                        pj = ci * (HT // 2) + h // 2
                        if h % 2 == 0:
                            _wait(t, gld_t[pj])
                        for a in range(ndt):
                            mmi = t.matmul(pb[(tci % 2) * 4 + a][:, :],
                                     wdq_sb[q][:, h * DQ + a * 128:h * DQ + (a + 1) * 128],
                                     gld_buf[:, (pj % 4) * 1024 + (h % 2) * 512:(pj % 4) * 1024 + (h % 2) * 512 + 512],
                                     start=(h == 0), stop=(h == HT - 1))
                        mmi.then_inc(tC.h, 1)
        tC.n += NCI * HT

        @block.vector
        def _(v):
            for ci in range(NCI):
                q, tci = divmod(ci, ntcf)
                v.wait_ge(tC.h, b_tC + (ci + 1) * HT)
                for a in range(ndt):
                    idx2 = ci * ndt + a
                    _wait(v, xld_t[idx2])
                    v.tensor_tensor(out=xa_buf[:, (idx2 % 4) * 512:(idx2 % 4) * 512 + 512],
                                    in0=pb[(tci % 2) * 4 + a][:, :],
                                    in1=xa_buf[:, (idx2 % 4) * 512:(idx2 % 4) * 512 + 512],
                                    op=ALU.add).then_inc(vC.h, 1)
        vC.n += NCI * ndt

        @block.sync
        def _(sy):
            for ci in range(NCI):
                q, tci = divmod(ci, ntcf)
                for a in range(ndt):
                    idx2 = ci * ndt + a
                    d_abs = q * ndt + a
                    sy.wait_ge(vC.h, b_vC + idx2 + 1)
                    ost_t[idx2] = oT.issue(sy, out_fm[d_abs * 128:(d_abs + 1) * 128,
                                                      tci * 512:(tci + 1) * 512],
                                           xa_buf[:, (idx2 % 4) * 512:(idx2 % 4) * 512 + 512])
            _wait_all(sy, tT.totals())
            _wait_all(sy, oT.totals())

    _fill_pcq()
    p0()
    barrier()
    if phases == 0.3:
        proj_fm(wk_t, AF.Silu, "strips", "wk", dest=kfm_scr)
    elif phases == 0.6:
        proj_fm(wk_t, AF.Silu, "strips_ktm", "wk", dest=kfm_scr)
    elif phases == 0.8:
        proj_fm(wk_t, AF.Silu, "strips_ktm", "wk", dest=kfm_scr)

        @block.gpsimd
        def _(g):
            _wait_all(g, sS.totals())
            g.collective_compute("AllGather", ALU.bypass, replica_groups=RG4,
                                 ins=[mg_pub2d.opt()], outs=[mg_ag2d.opt()]).then_inc(ccm.h, 1)
            g.wait_ge(ccm.h, 1)
    if phases >= 1:
        p1()
    if phases >= 2:
        p25()
    if phases >= 3:
        group_copies()
    H1 = 48
    if phases >= 6:
        emit_p5a(0, H1)
    if phases >= 4:
        p3()
    if phases >= 6:
        emit_p5a(H1, HT)
    if phases >= 5:
        oint_gather()
        oint_copy()
    if phases >= 7:
        p4()
    if phases >= 8:
        p5b()
    if phases < 8:
        @block.sync
        def _(sy):
            _wait_all(sy, sX.totals())
            _wait_all(sy, tT.totals())
        @block.scalar
        def _(s):
            _wait_all(s, sS.totals())
            _wait_all(s, sL.totals())
        @block.gpsimd
        def _(g):
            _wait_all(g, gW.totals())

    if DEBUG_DUMP:
        scr_map = dict(ln_scr=ln_scr, g_scr=g_scr, kfm_scr=kfm_scr, oi_scr=oi_scr,
                       gated_scr=gated_scr, xattn_scr=xattn_scr, oint_pub=oint_pub,
                       v_loc=v_loc, oint_loc=oint_loc)
        ten = scr_map[DEBUG_DUMP]
        flat = ten.rearrange(" ".join(chr(97 + i) for i in range(len(ten.shape))) + " -> ("
                             + " ".join(chr(97 + i) for i in range(len(ten.shape))) + ")")
        dbg = nc.dram_tensor("dbg_out", list(ten.shape), ten.dtype, kind="ExternalOutput").ap()
        dbgf = dbg.rearrange(" ".join(chr(97 + i) for i in range(len(ten.shape))) + " -> ("
                             + " ".join(chr(97 + i) for i in range(len(ten.shape))) + ")")

        @block.sync
        def _(sy):
            _wait_all(sy, tT.totals())
            sy.dma_start(out=dbgf, in_=flat).then_inc(tT.sems[0].h, 16)
            tT.sems[0].n += 16
            sy.wait_ge(tT.sems[0].h, tT.sems[0].n)

    bg.__exit__(None, None, None)
    for g in reversed(guards):
        g.__exit__(None, None, None)
    return nc



def _stage_inputs(c: Cfg, x, Wq, Wk, Wv, Wg, Wp, w_gate, w_up, w_down):
    asc = np.ascontiguousarray
    DT = c.DT

    def blk(wT):
        a = wT.reshape(DT, 128, DT, 128).transpose(2, 1, 0, 3)
        return asc(a.reshape(DT, 128, DT * 128))

    wq_t = blk(Wq.T)
    wk_t = blk(Wk.T)
    wg_t = blk(Wg.T)
    wp_t = blk(Wp.T)
    EQ = c.D // 4
    wv_t = asc(Wv.T.reshape(DT, 128, 4, EQ).transpose(2, 1, 0, 3).reshape(4, 128, DT * EQ))
    wgate_b = asc(w_gate.T.reshape(DT, 128, c.HT, 128).transpose(2, 1, 0, 3).reshape(c.HT, 128, DT * 128))
    wup_b = asc(w_up.T.reshape(DT, 128, c.HT, 128).transpose(2, 1, 0, 3).reshape(c.HT, 128, DT * 128))
    DQ = c.D // 4
    wdown_b = asc(w_down.T.reshape(c.HT, 128, 4, DQ).transpose(2, 1, 0, 3).reshape(4, 128, c.HT * DQ))
    in_maps = []
    for core in range(8):
        b, q4 = core // 4, core % 4
        sl = x[b, q4 * c.TC:(q4 + 1) * c.TC, :]
        in_maps.append(dict(
            x_fm=asc(sl.T), x_tm=asc(sl),
            wq_t=wq_t, wk_t=wk_t, wv_t=wv_t, wg_t=wg_t, wp_t=wp_t,
            wgate_b=wgate_b, wup_b=wup_b, wdown_b=wdown_b,
        ))
    return in_maps


def run(c: Cfg, x, Wq, Wk, Wv, Wg, Wp, w_gate, w_up, w_down):
    nc = build_program(c)
    in_maps = _stage_inputs(c, x, Wq, Wk, Wv, Wg, Wp, w_gate, w_up, w_down)
    res = run_bass_kernel_spmd(nc, in_maps, core_ids=list(range(8)))
    out = np.empty((2, 4 * c.TC, c.D), np.float32)
    for core in range(8):
        b, q4 = core // 4, core % 4
        out[b, q4 * c.TC:(q4 + 1) * c.TC, :] = res.results[core]["out_fm"].T
    return out


def kernel(L0_attn_x, L0_attn_Wq, L0_attn_Wk, L0_attn_Wv, L0_attn_Wg, L0_attn_Wp,
           L0_ffn_w_gate, L0_ffn_w_up, L0_ffn_w_down):
    c = Cfg(D=2048, TC=2048, H=8192)
    return run(c, np.asarray(L0_attn_x), np.asarray(L0_attn_Wq), np.asarray(L0_attn_Wk),
               np.asarray(L0_attn_Wv), np.asarray(L0_attn_Wg), np.asarray(L0_attn_Wp),
               np.asarray(L0_ffn_w_gate), np.asarray(L0_ffn_w_up), np.asarray(L0_ffn_w_down))


# version 11
# speedup vs baseline: 1.2070x; 1.0626x over previous
import numpy as np
import ml_dtypes
import concourse.bass as bass
import concourse.mybir as mybir
from concourse.bass_utils import run_bass_kernel_spmd

F32 = mybir.dt.float32
BF16 = mybir.dt.bfloat16
AF = mybir.ActivationFunctionType
ALU = mybir.AluOpType

DEBUG_DUMP = None
LN_EPS = 1e-5
RMS_EPS = float(np.finfo(np.float32).eps)


class Cfg:
    def __init__(self, D, TC, H, BS=256):
        self.D = D
        self.TC = TC
        self.H = H
        self.BS = BS
        self.DT = D // 128
        self.TT = TC // 128
        self.E = D // 4
        self.ET = self.E // 128
        self.NCC = TC // BS
        self.NCB = 4 * self.NCC
        self.TH = TC // 2
        self.HT = H // 128
        self.TPC = BS // 128
        assert D % 512 == 0 and TC % 512 == 0 and BS == 256
        assert self.TH % 512 == 0 and H % 128 == 0


class SemS:
    def __init__(self, handle):
        self.h = handle
        self.n = 0


class Stream:

    def __init__(self, mk, name, k=2):
        self.sems = [mk(f"{name}{i}") for i in range(k)]
        self.k = k
        self._rr = 0

    def issue(self, eng, out, in_):
        s = self.sems[self._rr % self.k]
        self._rr += 1
        if s.n > 0:
            eng.wait_ge(s.h, s.n)
        eng.dma_start(out=out, in_=in_).then_inc(s.h, 16)
        s.n += 16
        return (s.h, s.n)

    def totals(self):
        return [(s.h, s.n) for s in self.sems]


def _wait(eng, tok):
    eng.wait_ge(tok[0], tok[1])


def _wait_all(eng, toks):
    for t in toks:
        eng.wait_ge(t[0], t[1])


def build_program(c: Cfg, phases=99):
    global DEBUG_DUMP
    nc = bass.Bass("TRN2", target_bir_lowering=False, detect_race_conditions=False)
    D, TC, DT, TT, BS, ET, E = c.D, c.TC, c.DT, c.TT, c.BS, c.ET, c.E
    NCC, NCB, TH, HT, TPC = c.NCC, c.NCB, c.TH, c.HT, c.TPC

    EQ = D // 4
    DQ = D // 4
    x_fm = nc.dram_tensor("x_fm", [D, TC], F32, kind="ExternalInput").ap()
    x_tm = nc.dram_tensor("x_tm", [TC, D], F32, kind="ExternalInput").ap()
    wq_t = nc.dram_tensor("wq_t", [DT, 128, DT * 128], F32, kind="ExternalInput").ap()
    wk_t = nc.dram_tensor("wk_t", [DT, 128, DT * 128], F32, kind="ExternalInput").ap()
    wv_t = nc.dram_tensor("wv_t", [4, 128, DT * EQ], F32, kind="ExternalInput").ap()
    wg_t = nc.dram_tensor("wg_t", [DT, 128, DT * 128], F32, kind="ExternalInput").ap()
    wp_t = nc.dram_tensor("wp_t", [DT, 128, DT * 128], F32, kind="ExternalInput").ap()
    wgate_b = nc.dram_tensor("wgate_b", [HT, 128, DT * 128], F32, kind="ExternalInput").ap()
    wup_b = nc.dram_tensor("wup_b", [HT, 128, DT * 128], F32, kind="ExternalInput").ap()
    wdown_b = nc.dram_tensor("wdown_b", [4, 128, HT * DQ], F32, kind="ExternalInput").ap()
    out_fm = nc.dram_tensor("out_fm", [D, TC], F32, kind="ExternalOutput").ap()

    ident_f = nc.inline_tensor(np.eye(128, dtype=np.float32), "ident_f").ap()
    ident_b = nc.inline_tensor(np.eye(128, dtype=np.float32).astype(ml_dtypes.bfloat16), "ident_b").ap()
    ones_f = nc.inline_tensor(np.ones((1, 128), np.float32), "ones_f").ap()
    mtri = np.triu(np.ones((BS, BS), np.float32))
    mtri = np.ascontiguousarray(
        mtri.reshape(TPC, 128, BS).transpose(1, 0, 2).reshape(128, TPC * BS)
    ).astype(ml_dtypes.bfloat16)
    mask_c = nc.inline_tensor(mtri, "mask_c").ap()

    def dram(name, shape, dt, **kw):
        return nc.dram_tensor(name, shape, dt, **kw).ap()

    ln_scr = dram("ln_scr", [DT, 128, TC], BF16)
    g_scr = dram("g_scr", [DT, 128, TC], BF16)
    kfm_scr = dram("kfm_scr", [DT, 128, TC], BF16)
    oi_scr = dram("oi_scr", [NCC, DT, 128, BS], BF16)
    gated_scr = dram("gated_scr", [HT, 128, TC], BF16)

    wq_bf = dram("wq_bf", [DT, 128, DT * 128], BF16)
    wk_bf = dram("wk_bf", [DT, 128, DT * 128], BF16)
    wv_bf = dram("wv_bf", [4, 128, DT * EQ], BF16)
    wg_bf = dram("wg_bf", [DT, 128, DT * 128], BF16)
    wp_bf = dram("wp_bf", [DT, 128, DT * 128], BF16)
    wgate_bf = dram("wgate_bf", [HT, 128, DT * 128], BF16)
    wup_bf = dram("wup_bf", [HT, 128, DT * 128], BF16)
    wdown_bf = dram("wdown_bf", [4, 128, HT * DQ], BF16)
    xattn_scr = dram("xattn_scr", [DT, 128, TC], F32)
    vecs_scr = dram("vecs_scr", [4 * TT, 128], F32)

    QSZ = NCC * DT * 128 * BS
    KSZ = NCC * TPC * 128 * D
    MGE = QSZ + 2 * KSZ
    mg_pub = dram("mg_pub", [MGE], BF16)
    mg_ag = dram("mg_ag", [4 * MGE], BF16)
    mg_pub2d = mg_pub.rearrange("(x c) -> x c", c=2048)
    mg_ag2d = mg_ag.rearrange("(x c) -> x c", c=2048)
    oint_pub = dram("oint_pub", [NCB * E, BS], BF16)
    oint_ag = dram("oint_ag", [4 * NCB * E, BS], BF16)

    v_loc = dram("v_loc", [4 * NCC * TPC * 128, E], BF16)
    oint_loc = dram("oint_loc", [4 * NCC * E, BS], BF16)

    q_pub_b = mg_pub[0:QSZ].rearrange("(l a p f) -> l a p f", l=NCC, a=DT, p=128)
    ktm_pub_b = mg_pub[QSZ:QSZ + KSZ].rearrange("(l t p d) -> l t p d", l=NCC, t=TPC, p=128)
    v_pub_b = mg_pub[QSZ + KSZ:MGE].rearrange("(l t p d) -> l t p d", l=NCC, t=TPC, p=128)
    q_ag_v = [mg_ag[r * MGE:r * MGE + QSZ].rearrange("(l a p f) -> l a p f", l=NCC, a=DT, p=128)
              for r in range(4)]
    ktm_ag_v = [mg_ag[r * MGE + QSZ:r * MGE + QSZ + KSZ].rearrange("(l t p d) -> l t p d", l=NCC, t=TPC, p=128)
                for r in range(4)]
    v_ag_v = [mg_ag[r * MGE + QSZ + KSZ:(r + 1) * MGE].rearrange("(x d) -> x d", d=D)
              for r in range(4)]
    v_loc_b = v_loc.rearrange("(r l t p) e -> r l t p e", r=4, l=NCC, t=TPC, p=128)
    oint_pub_b = oint_pub.rearrange("(ch e p) f -> ch e p f", ch=NCB, e=ET, p=128)
    oint_loc_b = oint_loc.rearrange("(r l e p) f -> r l e p f", r=4, l=NCC, e=ET, p=128)

    def sb(name, shape, dtype, off):
        return nc.alloc_sbuf_tensor_at(name, shape, dtype, offset=off).ap()

    AR = 16512
    A0, B0, C0 = AR, AR + DT * TC * 2, AR + 2 * DT * TC * 2
    rn_sb = sb("rn_sb", [128, DT * TC], BF16, A0)
    lnh_sb = sb("lnh_sb", [128, DT * TC], BF16, A0)
    og_sb = sb("og_sb", [128, DT * TH], BF16, A0)
    gh_sb = sb("gh_sb", [128, DT * TH], BF16, A0 + DT * TH * 2)
    wdA_sb = sb("wdA_sb", [128, HT * DQ], BF16, A0)
    ktm_sb = sb("ktm_sb", [128, TT * D], BF16, B0)
    p0need = 4 * TC * 4 + 2 * TC * 2 + 2 * TC * 4
    p0b = B0 if p0need <= DT * TC * 2 else C0 + 77824
    assert p0b + p0need <= 229344
    bcast = sb("bcast", [128, 4 * TC], F32, p0b)
    lnt_buf = sb("lnt_buf", [128, 2 * TC], BF16, p0b + 4 * TC * 4)
    tmp_v = sb("tmp_v", [128, TC], F32, p0b + 4 * TC * 4 + 2 * TC * 2)
    tmp_g = sb("tmp_g", [128, TC], F32, p0b + 4 * TC * 4 + 2 * TC * 2 + TC * 4)
    kv_sb = sb("kv_sb", [128, DT * E], F32, B0)
    kv_bf = sb("kv_bf", [128, DT * E], BF16, B0 + DT * E * 4)
    gatB_sb = sb("gatB_sb", [128, 4 * TC], BF16, B0 + 6 * TC * 4)
    silB_sb = sb("silB_sb", [128, 2 * TC], BF16, C0 + 71680)
    wdB_sb = sb("wdB_sb", [128, HT * DQ], BF16, B0)
    assert HT * DQ * 2 == DT * TC * 2

    xtm_buf = sb("xtm_buf", [128, 2 * D], F32, C0)
    wbA1 = sb("wbA1", [128, 2 * DT * 128], BF16, C0)
    wbA2 = sb("wbA2", [128, 2 * DT * 128], BF16, C0 + 8192)
    qc_buf = sb("qc_buf", [128, 2 * DT * BS], BF16, C0)
    xsq_buf = sb("xsq_buf", [128, 2 * D], F32, C0 + 16384)
    ktc_buf = sb("ktc_buf", [128, 2 * TPC * D], BF16, C0 + 16384)
    oint_buf = sb("oint_buf", [128, DT * BS], BF16, C0 + 16384)
    wvq_buf = sb("wvq_buf", [128, 2 * DT * EQ], BF16, C0 + 16384)
    xf_buf = sb("xf_buf", [128, 2 * TC], F32, C0 + 32768)
    vv_buf = sb("vv_buf", [128, 2 * TPC * D], BF16, C0 + 32768)
    vte_buf = sb("vte_buf", [128, 2 * TPC * E], BF16, C0 + 32768)
    oi4_sb = sb("oi4_sb", [128, DT * BS], BF16, C0 + 40960)
    bsrc_sb = sb("bsrc_sb", [1, 4 * TC], F32, C0)
    stat = sb("stat", [128, 18 * TT], F32, C0 + 49152)
    packT_sb = sb("packT_sb", [4 * TT, 128], F32, C0 + 50304)
    onesb = sb("onesb", [1, 128], F32, C0 + 50816)
    identf_sb = sb("identf_sb", [128, 128], F32, C0 + 51328)
    identb_sb = sb("identb_sb", [128, 128], BF16, C0 + 51840)
    mask_sb = sb("mask_sb", [128, TPC * BS], BF16, C0 + 52096)
    w_buf = sb("w_buf", [128, 2 * DT * 128], BF16, C0 + 53248)
    oi_sb = sb("oi_sb", [128, DT * BS], BF16, C0 + 53248)
    w_buf2 = sb("w_buf2", [128, 2 * DT * 128], BF16, C0 + 61440)
    xa_buf = sb("xa_buf", [128, 2 * 1024], F32, C0 + 61440)
    sT_sb = sb("sT_sb", [128, TPC * BS], BF16, C0 + 61440)
    strip_buf = sb("strip_buf", [128, 2 * TC], BF16, C0 + 69632)
    gld_buf = sb("gld_buf", [128, 8 * 512], BF16, C0 + 69632)
    oi3_sb = sb("oi3_sb", [128, ET * BS], BF16, C0 + 69632)
    assert 18 * TT * 4 <= 1152 and 4 * TT <= 128
    assert 2 * TPC * D * 2 <= 16384 and DT * BS * 2 <= 8192
    assert C0 + 77824 <= 229344, "sbuf overflow"

    pb = [nc.alloc_psum_tensor(f"pb{i}", [128, 512], F32).ap() for i in range(8)]

    guards = []

    def mk(name):
        g = nc.semaphore(name)
        h = g.__enter__()
        guards.append(g)
        return SemS(h)

    tC, vC, sC, gC = mk("tC"), mk("vC"), mk("sC"), mk("gC")
    p5aT, p5aS, p5aV = mk("p5aT"), mk("p5aS"), mk("p5aV")
    ccm, cco = mk("ccm"), mk("cco")
    barS = mk("bar")
    sX = Stream(mk, "sX", 4)
    tT = Stream(mk, "tT", 4)
    gW = Stream(mk, "gW", 4)
    sS = Stream(mk, "sS", 4)
    sL = Stream(mk, "sL", 4)
    oT = Stream(mk, "oT", 4)

    RG8 = [list(range(8))]
    RG4 = [[0, 1, 2, 3], [4, 5, 6, 7]]
    toks = {}
    misc = {}

    bg = nc.Block()
    block = bg.__enter__()

    def barrier():
        barS.n += 5
        tgt = barS.n
        for sec in (block.tensor, block.vector, block.scalar, block.gpsimd, block.sync):
            def _f(eng, tgt=tgt):
                eng.sem_inc(barS.h, 1)
                eng.wait_ge(barS.h, tgt)
            sec(_f)

    def p0():
        xtm_t = {}
        xf_t = {}
        ln_t = {}
        b_sC, b_vC, b_gC = sC.n, vC.n, gC.n

        @block.sync
        def _(sy):
            for nm, t, src in (("ones", onesb, ones_f), ("idf", identf_sb, ident_f),
                               ("idb", identb_sb, ident_b), ("mask", mask_sb, mask_c)):
                toks[nm] = sX.issue(sy, t[:, :], src[:, :])
            for j in range(TT):
                if j >= 2:
                    sy.wait_ge(sC.h, b_sC + j - 1)
                    sy.wait_ge(vC.h, b_vC + 2 * (j - 1))
                xtm_t[j] = sX.issue(sy, xtm_buf[:, (j % 2) * D:(j % 2 + 1) * D],
                                    x_tm[j * 128:(j + 1) * 128, :])

        @block.scalar
        def _(s):
            for j in range(TT):
                _wait(s, xtm_t[j])
                if j >= 2:
                    s.wait_ge(vC.h, b_vC + 2 * (j - 1))
                s.activation(out=xsq_buf[:, (j % 2) * D:(j % 2 + 1) * D],
                             in_=xtm_buf[:, (j % 2) * D:(j % 2 + 1) * D],
                             func=AF.Square).then_inc(sC.h, 1)
            s.wait_ge(vC.h, b_vC + 2 * TT + 4)
            s.activation(out=stat[:, 5 * TT:6 * TT], in_=stat[:, 3 * TT:4 * TT],
                         func=AF.Ln, bias=stat[:, 17 * TT:17 * TT + 1]).then_inc(sC.h, 1)
            s.wait_ge(sC.h, b_sC + TT + 1)
            s.activation(out=stat[:, 6 * TT:7 * TT], in_=stat[:, 5 * TT:6 * TT],
                         func=AF.Exp, scale=-0.5).then_inc(sC.h, 1)
            s.wait_ge(vC.h, b_vC + 2 * TT + 6)
            s.activation(out=stat[:, 8 * TT:9 * TT], in_=stat[:, 7 * TT:8 * TT],
                         func=AF.Ln, bias=stat[:, 17 * TT + 1:17 * TT + 2]).then_inc(sC.h, 1)
            s.wait_ge(sC.h, b_sC + TT + 3)
            s.activation(out=stat[:, 9 * TT:10 * TT], in_=stat[:, 8 * TT:9 * TT],
                         func=AF.Exp, scale=-0.5).then_inc(sC.h, 1)
        sC.n += TT + 4

        @block.vector
        def _(v):
            v.memset(stat[:, 17 * TT:17 * TT + 1], LN_EPS)
            v.memset(stat[:, 17 * TT + 1:17 * TT + 2], RMS_EPS)
            for j in range(TT):
                _wait(v, xtm_t[j])
                v.reduce_sum(stat[:, j:j + 1], xtm_buf[:, (j % 2) * D:(j % 2 + 1) * D],
                             axis=mybir.AxisListType.X).then_inc(vC.h, 1)
                v.wait_ge(sC.h, b_sC + j + 1)
                v.reduce_sum(stat[:, TT + j:TT + j + 1], xsq_buf[:, (j % 2) * D:(j % 2 + 1) * D],
                             axis=mybir.AxisListType.X).then_inc(vC.h, 1)
            ecnt = [2 * TT]

            def edge(ins):
                ecnt[0] += 1
                ins.then_inc(vC.h, 1)
                v.wait_ge(vC.h, b_vC + ecnt[0])

            sm, sq = stat[:, 0:TT], stat[:, TT:2 * TT]
            m, var, tmp = stat[:, 2 * TT:3 * TT], stat[:, 3 * TT:4 * TT], stat[:, 4 * TT:5 * TT]
            edge(v.tensor_scalar_mul(m, sm, 1.0 / D))
            edge(v.tensor_scalar_mul(var, sq, 1.0 / D))
            edge(v.tensor_mul(tmp, m, m))
            edge(v.tensor_sub(var, var, tmp))
            assert ecnt[0] == 2 * TT + 4
            r1 = stat[:, 6 * TT:7 * TT]
            u = stat[:, 7 * TT:8 * TT]
            v.wait_ge(sC.h, b_sC + TT + 2)
            edge(v.tensor_mul(u, var, r1))
            edge(v.tensor_mul(u, u, r1))
            v.wait_ge(sC.h, b_sC + TT + 4)
            r2s = stat[:, 9 * TT:10 * TT]
            r2 = stat[:, 10 * TT:11 * TT]
            mr1 = stat[:, 11 * TT:12 * TT]
            mr2 = stat[:, 12 * TT:13 * TT]
            edge(v.tensor_mul(r2, r1, r2s))
            edge(v.tensor_mul(mr1, m, r1))
            edge(v.tensor_mul(mr2, m, r2))
            edge(v.tensor_copy(out=stat[:, 13 * TT:14 * TT], in_=r1))
            edge(v.tensor_copy(out=stat[:, 14 * TT:15 * TT], in_=mr1))
            edge(v.tensor_copy(out=stat[:, 15 * TT:16 * TT], in_=r2))
            edge(v.tensor_copy(out=stat[:, 16 * TT:17 * TT], in_=mr2))
        vC.n += 2 * TT + 13

        b_tC = tC.n

        @block.tensor
        def _(t):
            t.wait_ge(vC.h, b_vC + 2 * TT + 13)
            _wait(t, toks["idf"])
            t.transpose(pb[4][0:4 * TT, 0:128], stat[:, 13 * TT:17 * TT],
                        identf_sb[:, :]).then_inc(tC.h, 1)
        tC.n += 1

        @block.vector
        def _(v):
            v.wait_ge(tC.h, b_tC + 1)
            v.tensor_copy(out=packT_sb[:, :], in_=pb[4][0:4 * TT, 0:128]).then_inc(vC.h, 1)
        vC.n += 1

        @block.scalar
        def _(s):
            s.wait_ge(vC.h, vC.n)
            toks["vecs"] = sS.issue(s, vecs_scr[:, :], packT_sb[:, :])

        @block.sync
        def _(sy):
            _wait(sy, toks["vecs"])
            toks["bsrc"] = sX.issue(
                sy, bsrc_sb[0:1, :], vecs_scr.rearrange("a p -> (a p)").unsqueeze(0))

        nch = TC // 512
        b_tC2 = tC.n
        b_vC2 = vC.n

        @block.tensor
        def _(t):
            _wait(t, toks["bsrc"])
            _wait(t, toks["ones"])
            idx = 0
            for vv in range(4):
                for ch in range(nch):
                    if idx >= 4:
                        t.wait_ge(vC.h, b_vC2 + idx - 3)
                    mmi = t.matmul(pb[idx % 4][:, :], onesb[0:1, :],
                             bsrc_sb[0:1, vv * TC + ch * 512: vv * TC + (ch + 1) * 512],
                             start=True, stop=True).then_inc(tC.h, 1)
                    idx += 1
        tC.n += 4 * nch

        @block.vector
        def _(v):
            idx = 0
            for vv in range(4):
                for ch in range(nch):
                    v.wait_ge(tC.h, b_tC2 + idx + 1)
                    v.tensor_copy(out=bcast[:, vv * TC + ch * 512: vv * TC + (ch + 1) * 512],
                                  in_=pb[idx % 4][:, :]).then_inc(vC.h, 1)
                    idx += 1
        vC.n += 4 * nch

        b_vC3 = vC.n
        b_gC3 = gC.n

        @block.sync
        def _(sy):
            for d in range(DT):
                if d >= 2:
                    sy.wait_ge(vC.h, b_vC3 + 2 * (d - 1))
                    sy.wait_ge(gC.h, b_gC3 + 2 * (d - 1))
                xf_t[d] = sX.issue(sy, xf_buf[:, (d % 2) * TC:(d % 2 + 1) * TC],
                                   x_fm[d * 128:(d + 1) * 128, :])

        @block.gpsimd
        def _(g):
            g.wait_ge(vC.h, b_vC3)
            for d in range(DT):
                _wait(g, xf_t[d])
                xf = xf_buf[:, (d % 2) * TC:(d % 2 + 1) * TC]
                g.tensor_mul(tmp_g[:, :], xf, bcast[:, 2 * TC:3 * TC]).then_inc(gC.h, 1)
                g.wait_ge(gC.h, b_gC3 + 2 * d + 1)
                g.tensor_tensor(out=rn_sb[:, d * TC:(d + 1) * TC],
                                in0=tmp_g[:, :], in1=bcast[:, 3 * TC:4 * TC],
                                op=ALU.subtract).then_inc(gC.h, 1)
        gC.n += 2 * DT

        @block.scalar
        def _(s):
            for d in range(DT):
                s.wait_ge(vC.h, b_vC3 + 2 * (d + 1))
                ln_t[d] = sS.issue(s, ln_scr[d, :, :],
                                   lnt_buf[:, (d % 2) * TC:(d % 2 + 1) * TC])
            _wait_all(s, sS.totals())

        @block.vector
        def _(v):
            for d in range(DT):
                _wait(v, xf_t[d])
                if d >= 2:
                    _wait(v, ln_t[d - 2])
                xf = xf_buf[:, (d % 2) * TC:(d % 2 + 1) * TC]
                v.tensor_mul(tmp_v[:, :], xf, bcast[:, 0:TC]).then_inc(vC.h, 1)
                v.wait_ge(vC.h, b_vC3 + 2 * d + 1)
                v.tensor_tensor(out=lnt_buf[:, (d % 2) * TC:(d % 2 + 1) * TC],
                                in0=tmp_v[:, :], in1=bcast[:, TC:2 * TC],
                                op=ALU.subtract).then_inc(vC.h, 1)
        vC.n += 2 * DT


    pc = {}
    pcq = []

    def _fill_pcq():
        for i in range(16):
            pcq.append(("wgate", i, wgate_bf[4*i:4*i+4, :, :], wgate_b[4*i:4*i+4, :, :]))
            pcq.append(("wup", i, wup_bf[4*i:4*i+4, :, :], wup_b[4*i:4*i+4, :, :]))
        for i in range(4):
            pcq.append(("wp", i, wp_bf[4*i:4*i+4, :, :], wp_t[4*i:4*i+4, :, :]))
        for q in range(4):
            for i in range(4):
                pcq.append(("wdown", 4*q + i,
                            wdown_bf[q, :, i*8192:(i+1)*8192],
                            wdown_b[q, :, i*8192:(i+1)*8192]))

    def emit_piece(g, n=1):
        for _ in range(n):
            if not pcq:
                return
            name, i, dst, srcap = pcq.pop(0)
            pc.setdefault(name, {})[i] = gW.issue(g, dst, srcap)

    def proj_fm(wt_ap, act_fn, kind, pcname, dest=None):
        nch = TC // 512
        b_tC, b_sC, b_vC, b_gC = tC.n, sC.n, vC.n, gC.n
        per_o_t = nch + (TT if kind == "strips_ktm" else 0)
        w_t = {}
        mm_mil = {}
        tp_mil = {}

        @block.gpsimd
        def _(g):
            for o in range(DT):
                if o < 2:
                    g.wait_ge(tC.h, b_tC)
                else:
                    g.wait_ge(tC.h, b_tC + per_o_t * (o - 1))
                w_t[o] = gW.issue(
                    g,
                    w_buf[:, (o % 2) * DT * 128:(o % 2 + 1) * DT * 128],
                    wt_ap[o, :, :])
                emit_piece(g)

        @block.tensor
        def _(t):
            mil = b_tC

            def emit_tp(s_, j):
                nonlocal mil
                lin = s_ * TT + j
                if lin >= 4:
                    t.wait_ge(vC.h, b_vC + lin - 3)
                t.transpose(pb[4 + (j % 4)].bitcast(BF16)[:, 0:128],
                            strip_buf[:, (s_ % 2) * TC + j * 128:(s_ % 2) * TC + (j + 1) * 128],
                            identb_sb[:, :]).then_inc(tC.h, 1)
                mil += 1
                tp_mil[(s_, j)] = mil

            for o in range(DT):
                _wait(t, w_t[o])
                if o == 0:
                    t.wait_ge(sC.h, b_sC)
                    t.wait_ge(gC.h, b_gC)
                for tci in range(nch):
                    if o >= 1:
                        t.wait_ge(sC.h, b_sC + nch * (o - 1) + tci + 1)
                    for a in range(DT):
                        mmi = t.matmul(pb[tci % 4][:, :],
                                 w_buf[:, (o % 2) * DT * 128 + a * 128:(o % 2) * DT * 128 + (a + 1) * 128],
                                 rn_sb[:, a * TC + tci * 512:a * TC + (tci + 1) * 512],
                                 start=(a == 0), stop=(a == DT - 1))
                    mmi.then_inc(tC.h, 1)
                    mil += 1
                    mm_mil[(o, tci)] = mil
                if kind == "strips_ktm" and o >= 1:
                    t.wait_ge(sC.h, b_sC + nch * o)
                    for j in range(TT):
                        emit_tp(o - 1, j)
            if kind == "strips_ktm":
                t.wait_ge(sC.h, b_sC + nch * DT)
                for j in range(TT):
                    emit_tp(DT - 1, j)
        tC.n += nch * DT + (TT * DT if kind == "strips_ktm" else 0)

        st_toks = {}

        @block.scalar
        def _(s):
            _wait_all(s, sS.totals())
            for o in range(DT):
                if kind == "strips_ktm" and o >= 2:
                    s.wait_ge(tC.h, tp_mil[(o - 2, TT - 1)])
                if kind != "assemble" and o >= 2:
                    _wait(s, st_toks[o - 2])
                for tci in range(nch):
                    s.wait_ge(tC.h, mm_mil[(o, tci)])
                    if kind == "assemble":
                        dst = ktm_sb[:, o * TC + tci * 512:o * TC + (tci + 1) * 512]
                    else:
                        dst = strip_buf[:, (o % 2) * TC + tci * 512:(o % 2) * TC + (tci + 1) * 512]
                    s.activation(out=dst, in_=pb[tci % 4][:, :], func=act_fn).then_inc(sC.h, 1)
                if kind != "assemble":
                    s.wait_ge(sC.h, b_sC + nch * (o + 1))
                    st_toks[o] = sS.issue(s, dest[o, :, :], strip_buf[:, (o % 2) * TC:(o % 2 + 1) * TC])
        sC.n += nch * DT

        if kind == "strips_ktm":
            @block.vector
            def _(v):
                for s_ in range(DT):
                    for j in range(TT):
                        v.wait_ge(tC.h, tp_mil[(s_, j)])
                        v.tensor_copy(out=ktm_sb[:, j * D + s_ * 128:j * D + (s_ + 1) * 128],
                                      in_=pb[4 + (j % 4)].bitcast(BF16)[:, 0:128]).then_inc(vC.h, 1)
            vC.n += DT * TT

    def p1():
        proj_fm(wk_t, AF.Silu, "strips_ktm", "wk", dest=kfm_scr)
        b_vC = vC.n

        @block.scalar
        def _(s):
            s.wait_ge(vC.h, b_vC)
            for l in range(NCC):
                sS.issue(s, ktm_pub_b[l, :, :, :].rearrange("t p d -> p t d"),
                         ktm_sb.rearrange("p (t d) -> p t d", t=TT)[:, TPC * l:TPC * (l + 1), :])
            _wait_all(s, sS.totals())

        b_tC = tC.n
        b_sC = sC.n
        wv_tok = {}
        vst_t = {}

        @block.gpsimd
        def _(g):
            for ev in range(4):
                if ev >= 2:
                    g.wait_ge(tC.h, b_tC + TT * (ev - 1))
                wv_tok[ev] = gW.issue(g, wvq_buf[:, (ev % 2) * DT * EQ:(ev % 2 + 1) * DT * EQ],
                                      wv_t[ev, :, :])
                emit_piece(g)

        @block.tensor
        def _(t):
            for ev in range(4):
                _wait(t, wv_tok[ev])
                for j in range(TT):
                    m = ev * TT + j
                    if m == 0:
                        t.wait_ge(sC.h, b_sC)
                    elif m >= 4:
                        t.wait_ge(sC.h, b_sC + m - 3)
                    for a in range(DT):
                        mmi = t.matmul(pb[m % 4][:, 0:EQ],
                                 rn_sb[:, a * TC + j * 128:a * TC + (j + 1) * 128],
                                 wvq_buf[:, (ev % 2) * DT * EQ + a * EQ:(ev % 2) * DT * EQ + (a + 1) * EQ],
                                 start=(a == 0), stop=(a == DT - 1))
                    mmi.then_inc(tC.h, 1)
        tC.n += 4 * TT

        @block.scalar
        def _(s):
            _wait_all(s, sS.totals())
            for ev in range(4):
                for j in range(TT):
                    m = ev * TT + j
                    if m >= 4:
                        _wait(s, vst_t[m - 4])
                    s.wait_ge(tC.h, b_tC + m + 1)
                    s.activation(out=strip_buf[:, (m % 4) * EQ:(m % 4 + 1) * EQ],
                                 in_=pb[m % 4][:, 0:EQ], func=AF.Silu).then_inc(sC.h, 1)
                    s.wait_ge(sC.h, b_sC + m + 1)
                    vst_t[m] = sS.issue(s, v_pub_b[j // TPC, j % TPC, :, ev * EQ:(ev + 1) * EQ],
                                        strip_buf[:, (m % 4) * EQ:(m % 4 + 1) * EQ])
        sC.n += 4 * TT

        proj_fm(wg_t, AF.Sigmoid, "strips", "wg", dest=g_scr)

        proj_fm(wq_t, AF.Silu, "assemble", "wq")
        b_sC3 = sC.n

        @block.scalar
        def _(s):
            s.wait_ge(sC.h, b_sC3)
            for l in range(NCC):
                sS.issue(s, q_pub_b[l, :, :, :].rearrange("a p t -> p a t"),
                         ktm_sb.rearrange("p (a t) -> p a t", a=DT)[:, :, l * BS:(l + 1) * BS])
            _wait_all(s, sS.totals())

        misc["g_end_tC"] = tC.n

        @block.gpsimd
        def _(g):
            emit_piece(g, n=len(pcq))
            _wait_all(g, sS.totals())
            g.collective_compute("AllGather", ALU.bypass, replica_groups=RG4,
                                 ins=[mg_pub2d.opt()], outs=[mg_ag2d.opt()]).then_inc(ccm.h, 1)
        ccm.n += 1

    def p25():
        b_tC, b_vC = tC.n, vC.n
        nr = (DT + 7) // 8
        per_t = TPC + nr
        per_v = TPC + nr
        ld_t = {}
        oist_t = {}

        @block.scalar
        def _(s):
            _wait_all(s, sS.totals())
            for l in range(NCC):
                if l >= 2:
                    s.wait_ge(tC.h, b_tC + per_t * (l - 1))
                h0 = (l % 2) * TPC * D
                t2 = sL.issue(s, ktc_buf[:, h0:h0 + DT * BS].rearrange("p (a f) -> p a f", a=DT),
                              kfm_scr[:, :, l * BS:(l + 1) * BS].rearrange("a p f -> p a f"))
                t3 = sL.issue(s, vv_buf[:, h0:h0 + TPC * D].rearrange("p (t d) -> p t d", t=TPC),
                              v_pub_b[l, :, :, :].rearrange("t p d -> p t d"))
                ld_t[l] = [t2, t3]
                if l >= 1:
                    s.wait_ge(vC.h, b_vC + l * per_v)
                    oist_t[l - 1] = sS.issue(s, oi_scr[l - 1, :, :, :].rearrange("a p f -> p a f"),
                                             oi_sb.rearrange("p (a f) -> p a f", a=DT))
            s.wait_ge(vC.h, b_vC + NCC * per_v)
            oist_t[NCC - 1] = sS.issue(s, oi_scr[NCC - 1, :, :, :].rearrange("a p f -> p a f"),
                                       oi_sb.rearrange("p (a f) -> p a f", a=DT))
            _wait_all(s, sS.totals())

        @block.tensor
        def _(t):
            t.wait_ge(sC.h, sC.n)
            cnt = 0
            for l in range(NCC):
                _wait_all(t, ld_t[l])
                h0 = (l % 2) * TPC * D
                for tp in range(TPC):
                    t.wait_ge(vC.h, b_vC + cnt)
                    for a in range(DT):
                        mmi = t.matmul(pb[6][:, 0:BS],
                                 ktc_buf[:, h0 + a * BS + tp * 128:h0 + a * BS + (tp + 1) * 128],
                                 ktm_sb[:, a * TC + l * BS:a * TC + (l + 1) * BS],
                                 start=(a == 0), stop=(a == DT - 1))
                    mmi.then_inc(tC.h, 1)
                    cnt += 1
                for r_ in range(nr):
                    t.wait_ge(vC.h, b_vC + cnt)
                    for a in range(8 * r_, min(8 * r_ + 8, DT)):
                        for tp in range(TPC):
                            mmi = t.matmul(pb[a % 8][:, 0:BS],
                                     vv_buf[:, h0 + tp * D + a * 128:h0 + tp * D + (a + 1) * 128],
                                     sT_sb[:, tp * BS:(tp + 1) * BS],
                                     start=(tp == 0), stop=(tp == TPC - 1))
                    mmi.then_inc(tC.h, 1)
                    cnt += 1
        tC.n += per_t * NCC

        @block.vector
        def _(v):
            _wait(v, toks["mask"])
            for l in range(NCC):
                for tp in range(TPC):
                    v.wait_ge(tC.h, b_tC + l * per_t + tp + 1)
                    v.tensor_tensor(out=sT_sb[:, tp * BS:(tp + 1) * BS],
                                    in0=pb[6][:, 0:BS],
                                    in1=mask_sb[:, tp * BS:(tp + 1) * BS],
                                    op=ALU.mult).then_inc(vC.h, 1)
                for r_ in range(nr):
                    v.wait_ge(tC.h, b_tC + l * per_t + TPC + r_ + 1)
                    if r_ == 0 and l >= 1:
                        _wait(v, oist_t[l - 1])
                    ins = None
                    for a in range(8 * r_, min(8 * r_ + 8, DT)):
                        ins = v.tensor_copy(out=oi_sb[:, a * BS:(a + 1) * BS],
                                            in_=pb[a % 8][:, 0:BS])
                    ins.then_inc(vC.h, 1)
        vC.n += per_v * NCC
        misc["p25_end_tC"] = tC.n
        misc["p25_oist"] = sS.totals()

        @block.sync
        def _(sy):
            sy.wait_ge(tC.h, misc["g_end_tC"])
            lnh_toks = []
            for qa in range(4):
                lnh_toks.append(
                    tT.issue(sy, lnh_sb[:, qa * 4 * TC:(qa + 1) * 4 * TC].rearrange("p (a f) -> p a f", a=4),
                             ln_scr[qa * 4:(qa + 1) * 4, :, :].rearrange("a p f -> p a f")))
            misc["lnh_toks"] = lnh_toks

    def group_copies():
        @block.gpsimd
        def _(g):
            g.wait_ge(ccm.h, ccm.n)
            s0 = gW.sems[0]
            KR = NCC * TPC * 128
            for arm in g.switch_core_id(n=128):
                ej = arm.logical % 4
                for r in range(4):
                    g.dma_start(out=v_loc[r * KR:(r + 1) * KR, :],
                                in_=v_ag_v[r][:, ej * E:(ej + 1) * E]).then_inc(s0.h, 16)
            s0.n += 64
        misc["vloc_tok"] = (gW.sems[0].h, gW.sems[0].n)

    def p3():
        @block.vector
        def _(v):
            v.memset(kv_sb[:, :], 0.0).then_inc(vC.h, 1)
        vC.n += 1

        b_sC, b_tC, b_vC = sC.n, tC.n, vC.n
        per_t = ET + 4
        per_v = 4
        ld_t = {}
        ost_t = {}

        @block.sync
        def _(sy):
            sy.wait_ge(ccm.h, ccm.n)
            _wait(sy, misc["vloc_tok"])
            sy.wait_ge(tC.h, misc["p25_end_tC"])
            sy.wait_ge(p5aT.h, p5aT.n)
            for ch in range(NCB):
                if ch >= 2:
                    sy.wait_ge(tC.h, b_tC + per_t * (ch - 1))
                r, l = ch // NCC, ch % NCC
                h0 = (ch % 2) * TPC * D
                t1 = sX.issue(sy, qc_buf[:, (ch % 2) * DT * BS:(ch % 2 + 1) * DT * BS].rearrange("p (a f) -> p a f", a=DT),
                              q_ag_v[r][l, :, :, :].rearrange("a p f -> p a f"))
                t2 = sX.issue(sy, ktc_buf[:, h0:h0 + TPC * D].rearrange("p (t d) -> p t d", t=TPC),
                              ktm_ag_v[r][l, :, :, :].rearrange("t p d -> p t d"))
                t3 = sX.issue(sy, vte_buf[:, (ch % 2) * TPC * E:(ch % 2 + 1) * TPC * E].rearrange("p (t e) -> p t e", t=TPC),
                              v_loc_b[r, l, :, :, :].rearrange("t p e -> p t e"))
                ld_t[ch] = [t1, t2, t3]

        @block.scalar
        def _(s):
            for ch in range(NCB):
                for qd in range(4):
                    if ch >= 1:
                        s.wait_ge(vC.h, b_vC + per_v * (ch - 1) + qd + 1)
                    elif qd == 0:
                        s.wait_ge(vC.h, b_vC)
                    s.copy(out=kv_bf[:, qd * 4 * E:(qd + 1) * 4 * E],
                           in_=kv_sb[:, qd * 4 * E:(qd + 1) * 4 * E]).then_inc(sC.h, 1)
                if ch >= 1:
                    ost_t[ch - 1] = sS.issue(s, oint_pub_b[ch - 1, :, :, :].rearrange("e p f -> p e f"),
                                             oi3_sb[:, :].rearrange("p (e f) -> p e f", e=ET))
            s.wait_ge(vC.h, b_vC + per_v * NCB)
            ost_t[NCB - 1] = sS.issue(s, oint_pub_b[NCB - 1, :, :, :].rearrange("e p f -> p e f"),
                                      oi3_sb[:, :].rearrange("p (e f) -> p e f", e=ET))
            _wait_all(s, sS.totals())
        sC.n += 4 * NCB

        @block.tensor
        def _(t):
            t.wait_ge(p5aS.h, p5aS.n)
            t.wait_ge(p5aV.h, p5aV.n)
            for ch in range(NCB):
                _wait_all(t, ld_t[ch])
                if ch >= 1:
                    t.wait_ge(vC.h, b_vC + per_v * (ch - 1) + 1)
                qc0 = (ch % 2) * DT * BS
                h0 = (ch % 2) * TPC * D
                vt0 = (ch % 2) * TPC * E
                for a in range(DT):
                    if a % 4 == 0:
                        t.wait_ge(sC.h, b_sC + 4 * ch + a // 4 + 1)
                    for et in range(ET):
                        mmi = t.matmul(pb[[6, 7, 0, 1][et]][:, 0:BS],
                                 kv_bf[:, a * E + et * 128:a * E + (et + 1) * 128],
                                 qc_buf[:, qc0 + a * BS:qc0 + (a + 1) * BS],
                                 start=(a == 0), stop=(a == DT - 1))
                    if a % 4 == 3:
                        mmi.then_inc(tC.h, 1)
                nq = max(1, DT // 4)
                for qd in range(4):
                    t.wait_ge(vC.h, b_vC + per_v * ch + qd)
                    for ai in range(nq):
                        a = qd * nq + ai
                        if a >= DT:
                            continue
                        for tp in range(TPC):
                            mmi = t.matmul(pb[2 + (a % 4)][:, 0:E],
                                     ktc_buf[:, h0 + tp * D + a * 128:h0 + tp * D + (a + 1) * 128],
                                     vte_buf[:, vt0 + tp * E:vt0 + (tp + 1) * E],
                                     start=(tp == 0), stop=(tp == TPC - 1))
                    mmi.then_inc(tC.h, 1)
        tC.n += per_t * NCB

        @block.vector
        def _(v):
            _wait_all(v, misc["p25_oist"])
            for ch in range(NCB):
                v.wait_ge(tC.h, b_tC + ch * per_t + ET)
                if ch >= 1:
                    _wait(v, ost_t[ch - 1])
                ins = None
                for et in range(ET):
                    ins = v.tensor_copy(out=oi3_sb[:, et * BS:(et + 1) * BS],
                                        in_=pb[[6, 7, 0, 1][et]][:, 0:BS])
                nq = max(1, DT // 4)
                for qd in range(4):
                    v.wait_ge(tC.h, b_tC + ch * per_t + ET + qd + 1)
                    for ai in range(nq):
                        a = qd * nq + ai
                        if a >= DT:
                            continue
                        ins = v.tensor_tensor(out=kv_sb[:, a * E:(a + 1) * E],
                                              in0=kv_sb[:, a * E:(a + 1) * E],
                                              in1=pb[2 + (a % 4)][:, 0:E],
                                              op=ALU.add)
                    ins.then_inc(vC.h, 1)
        vC.n += per_v * NCB

        misc["p3_end_tC"] = tC.n
        misc["p3_sS"] = sS.totals()

    def oint_gather():
        @block.gpsimd
        def _(g):
            _wait_all(g, misc["p3_sS"])
            g.collective_compute("AllGather", ALU.bypass, replica_groups=RG4,
                                 ins=[oint_pub.opt()], outs=[oint_ag.opt()]).then_inc(cco.h, 1)
        cco.n += 1

    def oint_copy():
        @block.gpsimd
        def _(g):
            g.wait_ge(cco.h, cco.n)
            s1 = gW.sems[1]
            for arm in g.switch_core_id(n=128):
                q4 = arm.logical % 4
                for r in range(4):
                    base = r * NCB * E + q4 * NCC * E
                    g.dma_start(out=oint_loc[r * NCC * E:(r + 1) * NCC * E, :],
                                in_=oint_ag[base:base + NCC * E, :]).then_inc(s1.h, 16)
            s1.n += 64
        misc["oint_tok"] = (gW.sems[1].h, gW.sems[1].n)

    def emit_p5a(h0, h1):
        ntc = TC // 512
        w_t = misc.setdefault("p5a_wt", {})
        gst_t = misc.setdefault("p5a_gst", {})
        pre_sS = sS.totals()
        wg1, wg2 = (wbA1, wbA2) if h0 == 0 else (w_buf, w_buf2)

        @block.sync
        def _(sy):
            if h0 > 0:
                _wait_all(sy, misc["p25_oist"])
            for h in range(h0, h1):
                if h >= 2:
                    sy.wait_ge(p5aT.h, 2 * ntc * (h - 1))
                elif h0 == 0:
                    sy.wait_ge(tC.h, misc["p25_end_tC"])
                _wait(sy, pc["wgate"][h // 4])
                _wait(sy, pc["wup"][h // 4])
                w_t[(h, 0)] = sX.issue(
                    sy, wg1[:, (h % 2) * DT * 128:(h % 2 + 1) * DT * 128],
                    wgate_bf[h, :, :])
                w_t[(h, 1)] = sX.issue(
                    sy, wg2[:, (h % 2) * DT * 128:(h % 2 + 1) * DT * 128],
                    wup_bf[h, :, :])

        @block.tensor
        def _(t):
            if h0 == 0:
                _wait_all(t, misc["lnh_toks"])
            t.wait_ge(sC.h, sC.n)
            t.wait_ge(vC.h, vC.n)
            for h in range(h0, h1):
                _wait(t, w_t[(h, 0)])
                _wait(t, w_t[(h, 1)])
                for tci in range(ntc):
                    if h >= 1:
                        t.wait_ge(p5aS.h, ntc * (h - 1) + tci + 1)
                    for a in range(DT):
                        mmi = t.matmul(pb[tci][:, :],
                                 wg1[:, (h % 2) * DT * 128 + a * 128:(h % 2) * DT * 128 + (a + 1) * 128],
                                 lnh_sb[:, a * TC + tci * 512:a * TC + (tci + 1) * 512],
                                 start=(a == 0), stop=(a == DT - 1))
                    mmi.then_inc(p5aT.h, 1)
                for tci in range(ntc):
                    if h >= 1:
                        t.wait_ge(p5aV.h, ntc * (h - 1) + tci + 1)
                    for a in range(DT):
                        mmi = t.matmul(pb[4 + tci][:, :],
                                 wg2[:, (h % 2) * DT * 128 + a * 128:(h % 2) * DT * 128 + (a + 1) * 128],
                                 lnh_sb[:, a * TC + tci * 512:a * TC + (tci + 1) * 512],
                                 start=(a == 0), stop=(a == DT - 1))
                    mmi.then_inc(p5aT.h, 1)
        p5aT.n += 2 * ntc * (h1 - h0)

        @block.scalar
        def _(s):
            for h in range(h0, h1):
                if h >= 2:
                    s.wait_ge(p5aV.h, ntc * (h - 1))
                for tci in range(ntc):
                    s.wait_ge(p5aT.h, h * 2 * ntc + tci + 1)
                    s.activation(out=silB_sb[:, (h % 2) * TC + tci * 512:(h % 2) * TC + (tci + 1) * 512],
                                 in_=pb[tci][:, :], func=AF.Silu).then_inc(p5aS.h, 1)
                if h >= 1:
                    s.wait_ge(p5aV.h, ntc * h)
                    gst_t[h - 1] = sS.issue(s, gated_scr[h - 1, :, :],
                                            gatB_sb[:, ((h - 1) % 4) * TC:((h - 1) % 4) * TC + TC])
            if h1 == HT:
                s.wait_ge(p5aV.h, ntc * HT)
                gst_t[HT - 1] = sS.issue(s, gated_scr[HT - 1, :, :],
                                         gatB_sb[:, ((HT - 1) % 4) * TC:((HT - 1) % 4) * TC + TC])
        p5aS.n += ntc * (h1 - h0)

        @block.vector
        def _(v):
            if h0 == 0:
                _wait_all(v, pre_sS)
            for h in range(h0, h1):
                if h >= 4:
                    _wait(v, gst_t[h - 4])
                for tci in range(ntc):
                    v.wait_ge(p5aT.h, h * 2 * ntc + ntc + tci + 1)
                    v.wait_ge(p5aS.h, ntc * h + tci + 1)
                    v.tensor_tensor(out=gatB_sb[:, (h % 4) * TC + tci * 512:(h % 4) * TC + (tci + 1) * 512],
                                    in0=silB_sb[:, (h % 2) * TC + tci * 512:(h % 2) * TC + (tci + 1) * 512],
                                    in1=pb[4 + tci][:, :], op=ALU.mult).then_inc(p5aV.h, 1)
        p5aV.n += ntc * (h1 - h0)

    def p4():
        nhc = NCC // 2
        ntc = TH // 512
        b_sC_em = sC.n
        for th in range(2):
            b_tC0 = tC.n
            b_vC = vC.n
            ld_t = {}
            x_t = {}
            xst_t = {}
            w_t = {}

            @block.sync
            def _(sy):
                if th == 0:
                    _wait_all(sy, misc["p3_sS"])
                    _wait(sy, misc["oint_tok"])
                    sy.wait_ge(p5aT.h, p5aT.n)
                else:
                    sy.wait_ge(tC.h, b_tC0)
                ld_t["gh"] = sX.issue(sy, gh_sb[:, :].rearrange("p (a f) -> p a f", a=DT),
                                      g_scr[:, :, th * TH:(th + 1) * TH].rearrange("a p f -> p a f"))
                for li in range(nhc):
                    l = th * nhc + li
                    if li >= 1:
                        sy.wait_ge(vC.h, b_vC + 2 * li)
                    tokens = [sX.issue(sy, oi4_sb[:, :].rearrange("p (a f) -> p a f", a=DT),
                                       oi_scr[l, :, :, :].rearrange("a p f -> p a f"))]
                    for r in range(4):
                        tokens.append(sX.issue(
                            sy, oint_buf[:, r * ET * BS:(r + 1) * ET * BS].rearrange("p (e f) -> p e f", e=ET),
                            oint_loc_b[r, l, :, :, :].rearrange("e p f -> p e f")))
                    ld_t[li] = tokens

            @block.vector
            def _(v):
                if th == 0:
                    v.wait_ge(p5aT.h, p5aT.n)
                else:
                    v.wait_ge(tC.h, b_tC0)
                _wait(v, ld_t["gh"])
                for li in range(nhc):
                    _wait_all(v, ld_t[li])
                    v.tensor_add(oi4_sb[:, :], oi4_sb[:, :], oint_buf[:, :]).then_inc(vC.h, 1)
                    v.wait_ge(vC.h, b_vC + 2 * li + 1)
                    v.tensor_tensor(
                        out=og_sb.rearrange("p (a f) -> p a f", a=DT)[:, :, li * BS:(li + 1) * BS],
                        in0=oi4_sb.rearrange("p (a f) -> p a f", a=DT),
                        in1=gh_sb.rearrange("p (a f) -> p a f", a=DT)[:, :, li * BS:(li + 1) * BS],
                        op=ALU.mult).then_inc(vC.h, 1)
            vC.n += 2 * nhc

            b_tC, b_vC2 = tC.n, vC.n

            @block.scalar
            def _(s):
                if th == 0:
                    _wait_all(s, sS.totals())
                    s.wait_ge(tC.h, misc["p3_end_tC"])
                    _wait_all(s, [pc["wdown"][i] for i in range(4)])
                    misc["wd_q"] = {0: sL.issue(s, wdB_sb[:, :], wdown_bf[0, :, :])}
                for do in range(DT):
                    if do < 2:
                        if th == 0:
                            s.wait_ge(p5aT.h, p5aT.n)
                        else:
                            s.wait_ge(tC.h, b_tC)
                    else:
                        s.wait_ge(tC.h, b_tC + ntc * (do - 1))
                    _wait(s, pc["wp"][do // 4])
                    w_t[do] = sL.issue(
                        s, w_buf[:, (do % 2) * DT * 128:(do % 2 + 1) * DT * 128],
                        wp_bf[do, :, :])

            @block.sync
            def _(sy):
                for do in range(DT):
                    if do >= 4:
                        sy.wait_ge(vC.h, b_vC2 + ntc * (do - 3))
                    x_t[do] = sX.issue(sy, xf_buf[:, (do % 4) * TH:(do % 4 + 1) * TH],
                                       x_fm[do * 128:(do + 1) * 128, th * TH:(th + 1) * TH])
                    if do >= 1:
                        sy.wait_ge(vC.h, b_vC2 + ntc * do)
                        xst_t[do - 1] = tT.issue(sy, xattn_scr[do - 1, :, th * TH:(th + 1) * TH],
                                                 xa_buf[:, ((do - 1) % 2) * TH:((do - 1) % 2) * TH + TH])
                sy.wait_ge(vC.h, b_vC2 + ntc * DT)
                xst_t[DT - 1] = tT.issue(sy, xattn_scr[DT - 1, :, th * TH:(th + 1) * TH],
                                         xa_buf[:, ((DT - 1) % 2) * TH:((DT - 1) % 2) * TH + TH])
                _wait_all(sy, tT.totals())

            @block.tensor
            def _(t):
                t.wait_ge(vC.h, b_vC + 2 * nhc)
                for do in range(DT):
                    _wait(t, w_t[do])
                    for tci in range(ntc):
                        if do == 0 and tci == 0 and th == 0:
                            t.wait_ge(p5aS.h, p5aS.n)
                        if do >= 2:
                            t.wait_ge(vC.h, b_vC2 + ntc * (do - 2) + tci + 1)
                        for a in range(DT):
                            mmi = t.matmul(pb[(do % 2) * 2 + tci][:, :],
                                     w_buf[:, (do % 2) * DT * 128 + a * 128:(do % 2) * DT * 128 + (a + 1) * 128],
                                     og_sb[:, a * TH + tci * 512:a * TH + (tci + 1) * 512],
                                     start=(a == 0), stop=(a == DT - 1))
                        mmi.then_inc(tC.h, 1)
            tC.n += ntc * DT

            @block.vector
            def _(v):
                for do in range(DT):
                    _wait(v, x_t[do])
                    if do >= 2:
                        _wait(v, xst_t[do - 2])
                    for tci in range(ntc):
                        v.wait_ge(tC.h, b_tC + ntc * do + tci + 1)
                        v.tensor_tensor(out=xa_buf[:, (do % 2) * TH + tci * 512:(do % 2) * TH + (tci + 1) * 512],
                                        in0=pb[(do % 2) * 2 + tci][:, :],
                                        in1=xf_buf[:, (do % 4) * TH + tci * 512:(do % 4) * TH + (tci + 1) * 512],
                                        op=ALU.add).then_inc(vC.h, 1)
            vC.n += ntc * DT
        misc["p4_end_tC"] = tC.n
        misc["p4_tT"] = tT.totals()

    def p5b():
        nq = 4
        ndt = DQ // 128
        ntcf = TC // 512
        NCI = nq * ntcf
        b_tC, b_vC = tC.n, vC.n
        gld_t = {}
        xld_t = {}
        ost_t = {}
        wd_q = misc["wd_q"]
        wdq_sb = [wdB_sb, wdA_sb, wdB_sb, wdA_sb]

        @block.scalar
        def _(s):
            s.wait_ge(tC.h, misc["p4_end_tC"])
            _wait_all(s, [pc["wdown"][4 + i] for i in range(4)])
            wd_q[1] = sL.issue(s, wdA_sb[:, :], wdown_bf[1, :, :])
            s.wait_ge(p5aV.h, p5aV.n)
            _wait_all(s, misc["p3_sS"])
            for ci in range(NCI):
                q, tci = divmod(ci, ntcf)
                if ci == ntcf:
                    s.wait_ge(tC.h, b_tC + ntcf * HT)
                    _wait_all(s, [pc["wdown"][8 + i] for i in range(4)])
                    wd_q[2] = sL.issue(s, wdB_sb[:, :], wdown_bf[2, :, :])
                elif ci == 2 * ntcf:
                    s.wait_ge(tC.h, b_tC + 2 * ntcf * HT)
                    _wait_all(s, [pc["wdown"][12 + i] for i in range(4)])
                    wd_q[3] = sL.issue(s, wdA_sb[:, :], wdown_bf[3, :, :])
                for hp in range(HT // 2):
                    pj = ci * (HT // 2) + hp
                    h0 = 2 * hp
                    if pj >= 4:
                        s.wait_ge(tC.h, b_tC + 2 * pj - 6)
                    _wait(s, misc["p5a_gst"][h0])
                    _wait(s, misc["p5a_gst"][h0 + 1])
                    gld_t[pj] = sL.issue(
                        s, gld_buf[:, (pj % 4) * 1024:(pj % 4) * 1024 + 1024].rearrange("p (h f) -> p h f", h=2),
                        gated_scr[h0:h0 + 2, :, tci * 512:(tci + 1) * 512].rearrange("h p f -> p h f"))
                for a in range(ndt):
                    idx2 = ci * ndt + a
                    if idx2 < 4:
                        _wait_all(s, misc["p4_tT"])
                    else:
                        s.wait_ge(oT.sems[idx2 % 4].h, 16 * ((idx2 - 4) // 4 + 1))
                    xld_t[idx2] = sL.issue(s, xa_buf[:, (idx2 % 4) * 512:(idx2 % 4) * 512 + 512],
                                           xattn_scr[q * ndt + a, :, tci * 512:(tci + 1) * 512])

        @block.tensor
        def _(t):
            for q in range(nq):
                _wait(t, wd_q[q])
                for tci in range(ntcf):
                    ci = q * ntcf + tci
                    if ci == 0:
                        t.wait_ge(vC.h, b_vC)
                        t.wait_ge(p5aV.h, p5aV.n)
                    elif ci >= 2:
                        t.wait_ge(vC.h, b_vC + ndt * (ci - 1))
                    for h in range(HT):
                        pj = ci * (HT // 2) + h // 2
                        if h % 2 == 0:
                            _wait(t, gld_t[pj])
                        for a in range(ndt):
                            mmi = t.matmul(pb[(tci % 2) * 4 + a][:, :],
                                     wdq_sb[q][:, h * DQ + a * 128:h * DQ + (a + 1) * 128],
                                     gld_buf[:, (pj % 4) * 1024 + (h % 2) * 512:(pj % 4) * 1024 + (h % 2) * 512 + 512],
                                     start=(h == 0), stop=(h == HT - 1))
                        mmi.then_inc(tC.h, 1)
        tC.n += NCI * HT

        @block.vector
        def _(v):
            for ci in range(NCI):
                q, tci = divmod(ci, ntcf)
                v.wait_ge(tC.h, b_tC + (ci + 1) * HT)
                for a in range(ndt):
                    idx2 = ci * ndt + a
                    _wait(v, xld_t[idx2])
                    v.tensor_tensor(out=xa_buf[:, (idx2 % 4) * 512:(idx2 % 4) * 512 + 512],
                                    in0=pb[(tci % 2) * 4 + a][:, :],
                                    in1=xa_buf[:, (idx2 % 4) * 512:(idx2 % 4) * 512 + 512],
                                    op=ALU.add).then_inc(vC.h, 1)
        vC.n += NCI * ndt

        @block.sync
        def _(sy):
            for ci in range(NCI):
                q, tci = divmod(ci, ntcf)
                for a in range(ndt):
                    idx2 = ci * ndt + a
                    d_abs = q * ndt + a
                    sy.wait_ge(vC.h, b_vC + idx2 + 1)
                    ost_t[idx2] = oT.issue(sy, out_fm[d_abs * 128:(d_abs + 1) * 128,
                                                      tci * 512:(tci + 1) * 512],
                                           xa_buf[:, (idx2 % 4) * 512:(idx2 % 4) * 512 + 512])
            _wait_all(sy, tT.totals())
            _wait_all(sy, oT.totals())

    _fill_pcq()
    p0()
    barrier()
    if phases == 0.3:
        proj_fm(wk_t, AF.Silu, "strips", "wk", dest=kfm_scr)
    elif phases == 0.6:
        proj_fm(wk_t, AF.Silu, "strips_ktm", "wk", dest=kfm_scr)
    elif phases == 0.8:
        proj_fm(wk_t, AF.Silu, "strips_ktm", "wk", dest=kfm_scr)

        @block.gpsimd
        def _(g):
            _wait_all(g, sS.totals())
            g.collective_compute("AllGather", ALU.bypass, replica_groups=RG4,
                                 ins=[mg_pub2d.opt()], outs=[mg_ag2d.opt()]).then_inc(ccm.h, 1)
            g.wait_ge(ccm.h, 1)
    if phases >= 1:
        p1()
    if phases >= 2:
        p25()
    if phases >= 3:
        group_copies()
    H1 = 48
    if phases >= 6:
        emit_p5a(0, H1)
    if phases >= 4:
        p3()
    if phases >= 6:
        emit_p5a(H1, HT)
    if phases >= 5:
        oint_gather()
        oint_copy()
    if phases >= 7:
        p4()
    if phases >= 8:
        p5b()
    if phases < 8:
        @block.sync
        def _(sy):
            _wait_all(sy, sX.totals())
            _wait_all(sy, tT.totals())
        @block.scalar
        def _(s):
            _wait_all(s, sS.totals())
            _wait_all(s, sL.totals())
        @block.gpsimd
        def _(g):
            _wait_all(g, gW.totals())

    if DEBUG_DUMP:
        scr_map = dict(ln_scr=ln_scr, g_scr=g_scr, kfm_scr=kfm_scr, oi_scr=oi_scr,
                       gated_scr=gated_scr, xattn_scr=xattn_scr, oint_pub=oint_pub,
                       v_loc=v_loc, oint_loc=oint_loc)
        ten = scr_map[DEBUG_DUMP]
        flat = ten.rearrange(" ".join(chr(97 + i) for i in range(len(ten.shape))) + " -> ("
                             + " ".join(chr(97 + i) for i in range(len(ten.shape))) + ")")
        dbg = nc.dram_tensor("dbg_out", list(ten.shape), ten.dtype, kind="ExternalOutput").ap()
        dbgf = dbg.rearrange(" ".join(chr(97 + i) for i in range(len(ten.shape))) + " -> ("
                             + " ".join(chr(97 + i) for i in range(len(ten.shape))) + ")")

        @block.sync
        def _(sy):
            _wait_all(sy, tT.totals())
            sy.dma_start(out=dbgf, in_=flat).then_inc(tT.sems[0].h, 16)
            tT.sems[0].n += 16
            sy.wait_ge(tT.sems[0].h, tT.sems[0].n)

    bg.__exit__(None, None, None)
    for g in reversed(guards):
        g.__exit__(None, None, None)
    return nc



def _stage_inputs(c: Cfg, x, Wq, Wk, Wv, Wg, Wp, w_gate, w_up, w_down):
    asc = np.ascontiguousarray
    DT = c.DT

    def blk(wT):
        a = wT.reshape(DT, 128, DT, 128).transpose(2, 1, 0, 3)
        return asc(a.reshape(DT, 128, DT * 128))

    wq_t = blk(Wq.T)
    wk_t = blk(Wk.T)
    wg_t = blk(Wg.T)
    wp_t = blk(Wp.T)
    EQ = c.D // 4
    wv_t = asc(Wv.T.reshape(DT, 128, 4, EQ).transpose(2, 1, 0, 3).reshape(4, 128, DT * EQ))
    wgate_b = asc(w_gate.T.reshape(DT, 128, c.HT, 128).transpose(2, 1, 0, 3).reshape(c.HT, 128, DT * 128))
    wup_b = asc(w_up.T.reshape(DT, 128, c.HT, 128).transpose(2, 1, 0, 3).reshape(c.HT, 128, DT * 128))
    DQ = c.D // 4
    wdown_b = asc(w_down.T.reshape(c.HT, 128, 4, DQ).transpose(2, 1, 0, 3).reshape(4, 128, c.HT * DQ))
    in_maps = []
    for core in range(8):
        b, q4 = core // 4, core % 4
        sl = x[b, q4 * c.TC:(q4 + 1) * c.TC, :]
        in_maps.append(dict(
            x_fm=asc(sl.T), x_tm=asc(sl),
            wq_t=wq_t, wk_t=wk_t, wv_t=wv_t, wg_t=wg_t, wp_t=wp_t,
            wgate_b=wgate_b, wup_b=wup_b, wdown_b=wdown_b,
        ))
    return in_maps


def run(c: Cfg, x, Wq, Wk, Wv, Wg, Wp, w_gate, w_up, w_down):
    nc = build_program(c)
    in_maps = _stage_inputs(c, x, Wq, Wk, Wv, Wg, Wp, w_gate, w_up, w_down)
    res = run_bass_kernel_spmd(nc, in_maps, core_ids=list(range(8)))
    out = np.empty((2, 4 * c.TC, c.D), np.float32)
    for core in range(8):
        b, q4 = core // 4, core % 4
        out[b, q4 * c.TC:(q4 + 1) * c.TC, :] = res.results[core]["out_fm"].T
    return out


def kernel(L0_attn_x, L0_attn_Wq, L0_attn_Wk, L0_attn_Wv, L0_attn_Wg, L0_attn_Wp,
           L0_ffn_w_gate, L0_ffn_w_up, L0_ffn_w_down):
    c = Cfg(D=2048, TC=2048, H=8192)
    return run(c, np.asarray(L0_attn_x), np.asarray(L0_attn_Wq), np.asarray(L0_attn_Wk),
               np.asarray(L0_attn_Wv), np.asarray(L0_attn_Wg), np.asarray(L0_attn_Wp),
               np.asarray(L0_ffn_w_gate), np.asarray(L0_ffn_w_up), np.asarray(L0_ffn_w_down))


# version 12
# speedup vs baseline: 1.2536x; 1.0386x over previous
import numpy as np
import ml_dtypes
import concourse.bass as bass
import concourse.mybir as mybir
from concourse.bass_utils import run_bass_kernel_spmd

F32 = mybir.dt.float32
BF16 = mybir.dt.bfloat16
AF = mybir.ActivationFunctionType
ALU = mybir.AluOpType

DEBUG_DUMP = None
LN_EPS = 1e-5
RMS_EPS = float(np.finfo(np.float32).eps)


class Cfg:
    def __init__(self, D, TC, H, BS=256):
        self.D = D
        self.TC = TC
        self.H = H
        self.BS = BS
        self.DT = D // 128
        self.TT = TC // 128
        self.E = D // 4
        self.ET = self.E // 128
        self.NCC = TC // BS
        self.NCB = 4 * self.NCC
        self.TH = TC // 2
        self.HT = H // 128
        self.TPC = BS // 128
        assert D % 512 == 0 and TC % 512 == 0 and BS == 256
        assert self.TH % 512 == 0 and H % 128 == 0


class SemS:
    def __init__(self, handle):
        self.h = handle
        self.n = 0


class Stream:

    def __init__(self, mk, name, k=2):
        self.sems = [mk(f"{name}{i}") for i in range(k)]
        self.k = k
        self._rr = 0

    def issue(self, eng, out, in_):
        s = self.sems[self._rr % self.k]
        self._rr += 1
        if s.n > 0:
            eng.wait_ge(s.h, s.n)
        eng.dma_start(out=out, in_=in_).then_inc(s.h, 16)
        s.n += 16
        return (s.h, s.n)

    def totals(self):
        return [(s.h, s.n) for s in self.sems]


def _wait(eng, tok):
    eng.wait_ge(tok[0], tok[1])


def _wait_all(eng, toks):
    for t in toks:
        eng.wait_ge(t[0], t[1])


def build_program(c: Cfg, phases=99):
    global DEBUG_DUMP
    nc = bass.Bass("TRN2", target_bir_lowering=False, detect_race_conditions=False)
    D, TC, DT, TT, BS, ET, E = c.D, c.TC, c.DT, c.TT, c.BS, c.ET, c.E
    NCC, NCB, TH, HT, TPC = c.NCC, c.NCB, c.TH, c.HT, c.TPC

    EQ = D // 4
    DQ = D // 4
    x_fm = nc.dram_tensor("x_fm", [D, TC], F32, kind="ExternalInput").ap()
    x_tm = nc.dram_tensor("x_tm", [TC, D], F32, kind="ExternalInput").ap()
    wq_t = nc.dram_tensor("wq_t", [DT, 128, DT * 128], F32, kind="ExternalInput").ap()
    wk_t = nc.dram_tensor("wk_t", [DT, 128, DT * 128], F32, kind="ExternalInput").ap()
    wv_t = nc.dram_tensor("wv_t", [4, 128, DT * EQ], F32, kind="ExternalInput").ap()
    wg_t = nc.dram_tensor("wg_t", [DT, 128, DT * 128], F32, kind="ExternalInput").ap()
    wp_t = nc.dram_tensor("wp_t", [DT, 128, DT * 128], F32, kind="ExternalInput").ap()
    wgate_b = nc.dram_tensor("wgate_b", [HT, 128, DT * 128], F32, kind="ExternalInput").ap()
    wup_b = nc.dram_tensor("wup_b", [HT, 128, DT * 128], F32, kind="ExternalInput").ap()
    wdown_b = nc.dram_tensor("wdown_b", [4, 128, HT * DQ], F32, kind="ExternalInput").ap()
    out_fm = nc.dram_tensor("out_fm", [D, TC], F32, kind="ExternalOutput").ap()

    ident_f = nc.inline_tensor(np.eye(128, dtype=np.float32), "ident_f").ap()
    ident_b = nc.inline_tensor(np.eye(128, dtype=np.float32).astype(ml_dtypes.bfloat16), "ident_b").ap()
    ones_f = nc.inline_tensor(np.ones((1, 128), np.float32), "ones_f").ap()
    mtri = np.triu(np.ones((BS, BS), np.float32))
    mtri = np.ascontiguousarray(
        mtri.reshape(TPC, 128, BS).transpose(1, 0, 2).reshape(128, TPC * BS)
    ).astype(ml_dtypes.bfloat16)
    mask_c = nc.inline_tensor(mtri, "mask_c").ap()

    def dram(name, shape, dt, **kw):
        return nc.dram_tensor(name, shape, dt, **kw).ap()

    ln_scr = dram("ln_scr", [DT, 128, TC], BF16)
    g_scr = dram("g_scr", [DT, 128, TC], BF16)
    kfm_scr = dram("kfm_scr", [DT, 128, TC], BF16)
    oi_scr = dram("oi_scr", [NCC, DT, 128, BS], BF16)
    gated_scr = dram("gated_scr", [HT, 128, TC], BF16)

    wq_bf = dram("wq_bf", [DT, 128, DT * 128], BF16)
    wk_bf = dram("wk_bf", [DT, 128, DT * 128], BF16)
    wv_bf = dram("wv_bf", [4, 128, DT * EQ], BF16)
    wg_bf = dram("wg_bf", [DT, 128, DT * 128], BF16)
    wp_bf = dram("wp_bf", [DT, 128, DT * 128], BF16)
    wgate_bf = dram("wgate_bf", [HT, 128, DT * 128], BF16)
    wup_bf = dram("wup_bf", [HT, 128, DT * 128], BF16)
    wdown_bf = dram("wdown_bf", [4, 128, HT * DQ], BF16)
    xattn_scr = dram("xattn_scr", [DT, 128, TC], F32)
    vecs_scr = dram("vecs_scr", [4 * TT, 128], F32)

    QSZ = NCC * DT * 128 * BS
    KSZ = NCC * TPC * 128 * D
    MGE = QSZ + 2 * KSZ
    mg_pub = dram("mg_pub", [MGE], BF16)
    mg_ag = dram("mg_ag", [4 * MGE], BF16)
    mg_pub2d = mg_pub.rearrange("(x c) -> x c", c=2048)
    mg_ag2d = mg_ag.rearrange("(x c) -> x c", c=2048)
    oint_pub = dram("oint_pub", [NCB * E, BS], BF16)
    oint_ag = dram("oint_ag", [4 * NCB * E, BS], BF16)

    v_loc = dram("v_loc", [4 * NCC * TPC * 128, E], BF16)
    oint_loc = dram("oint_loc", [4 * NCC * E, BS], BF16)

    q_pub_b = mg_pub[0:QSZ].rearrange("(l a p f) -> l a p f", l=NCC, a=DT, p=128)
    ktm_pub_b = mg_pub[QSZ:QSZ + KSZ].rearrange("(l t p d) -> l t p d", l=NCC, t=TPC, p=128)
    v_pub_b = mg_pub[QSZ + KSZ:MGE].rearrange("(l t p d) -> l t p d", l=NCC, t=TPC, p=128)
    q_ag_v = [mg_ag[r * MGE:r * MGE + QSZ].rearrange("(l a p f) -> l a p f", l=NCC, a=DT, p=128)
              for r in range(4)]
    ktm_ag_v = [mg_ag[r * MGE + QSZ:r * MGE + QSZ + KSZ].rearrange("(l t p d) -> l t p d", l=NCC, t=TPC, p=128)
                for r in range(4)]
    v_ag_v = [mg_ag[r * MGE + QSZ + KSZ:(r + 1) * MGE].rearrange("(x d) -> x d", d=D)
              for r in range(4)]
    v_loc_b = v_loc.rearrange("(r l t p) e -> r l t p e", r=4, l=NCC, t=TPC, p=128)
    oint_pub_b = oint_pub.rearrange("(ch e p) f -> ch e p f", ch=NCB, e=ET, p=128)
    oint_loc_b = oint_loc.rearrange("(r l e p) f -> r l e p f", r=4, l=NCC, e=ET, p=128)

    def sb(name, shape, dtype, off):
        return nc.alloc_sbuf_tensor_at(name, shape, dtype, offset=off).ap()

    AR = 16512
    A0, B0, C0 = AR, AR + DT * TC * 2, AR + 2 * DT * TC * 2
    rn_sb = sb("rn_sb", [128, DT * TC], BF16, A0)
    lnh_sb = sb("lnh_sb", [128, DT * TC], BF16, A0)
    og_sb = sb("og_sb", [128, DT * TH], BF16, A0)
    gh_sb = sb("gh_sb", [128, DT * TH], BF16, A0 + DT * TH * 2)
    wdA_sb = sb("wdA_sb", [128, HT * DQ], BF16, A0)
    ktm_sb = sb("ktm_sb", [128, TT * D], BF16, B0)
    p0need = 4 * TC * 4 + 2 * TC * 2 + 2 * TC * 4
    p0b = B0 if p0need <= DT * TC * 2 else C0 + 77824
    assert p0b + p0need <= 229344
    bcast = sb("bcast", [128, 4 * TC], F32, p0b)
    lnt_buf = sb("lnt_buf", [128, 2 * TC], BF16, p0b + 4 * TC * 4)
    tmp_v = sb("tmp_v", [128, TC], F32, p0b + 4 * TC * 4 + 2 * TC * 2)
    tmp_g = sb("tmp_g", [128, TC], F32, p0b + 4 * TC * 4 + 2 * TC * 2 + TC * 4)
    kv_sb = sb("kv_sb", [128, DT * E], F32, B0)
    kv_bf = sb("kv_bf", [128, DT * E], BF16, B0 + DT * E * 4)
    gatB_sb = sb("gatB_sb", [128, 4 * TC], BF16, B0 + 6 * TC * 4)
    silB_sb = sb("silB_sb", [128, 2 * TC], BF16, C0 + 71680)
    wdB_sb = sb("wdB_sb", [128, HT * DQ], BF16, B0)
    assert HT * DQ * 2 == DT * TC * 2

    xtm_buf = sb("xtm_buf", [128, 2 * D], F32, C0)
    wbA1 = sb("wbA1", [128, 2 * DT * 128], BF16, C0)
    wbA2 = sb("wbA2", [128, 2 * DT * 128], BF16, C0 + 8192)
    qc_buf = sb("qc_buf", [128, 2 * DT * BS], BF16, C0)
    xsq_buf = sb("xsq_buf", [128, 2 * D], F32, C0 + 16384)
    ktc_buf = sb("ktc_buf", [128, 2 * TPC * D], BF16, C0 + 16384)
    oint_buf = sb("oint_buf", [128, DT * BS], BF16, C0 + 16384)
    wvq_buf = sb("wvq_buf", [128, 2 * DT * EQ], BF16, C0 + 16384)
    xf_buf = sb("xf_buf", [128, 2 * TC], F32, C0 + 32768)
    vv_buf = sb("vv_buf", [128, 2 * TPC * D], BF16, C0 + 32768)
    vte_buf = sb("vte_buf", [128, 2 * TPC * E], BF16, C0 + 32768)
    oi4_sb = sb("oi4_sb", [128, DT * BS], BF16, C0 + 40960)
    bsrc_sb = sb("bsrc_sb", [1, 4 * TC], F32, C0)
    stat = sb("stat", [128, 18 * TT], F32, C0 + 49152)
    packT_sb = sb("packT_sb", [4 * TT, 128], F32, C0 + 50304)
    onesb = sb("onesb", [1, 128], F32, C0 + 50816)
    identf_sb = sb("identf_sb", [128, 128], F32, C0 + 51328)
    identb_sb = sb("identb_sb", [128, 128], BF16, C0 + 51840)
    mask_sb = sb("mask_sb", [128, TPC * BS], BF16, C0 + 52096)
    w_buf = sb("w_buf", [128, 2 * DT * 128], BF16, C0 + 53248)
    oi_sb = sb("oi_sb", [128, DT * BS], BF16, C0 + 53248)
    w_buf2 = sb("w_buf2", [128, 2 * DT * 128], BF16, C0 + 61440)
    xa_buf = sb("xa_buf", [128, 2 * 1024], F32, C0 + 61440)
    sT_sb = sb("sT_sb", [128, TPC * BS], BF16, C0 + 61440)
    strip_buf = sb("strip_buf", [128, 2 * TC], BF16, C0 + 69632)
    gld_buf = sb("gld_buf", [128, 8 * 512], BF16, C0 + 69632)
    oi3_sb = sb("oi3_sb", [128, ET * BS], BF16, C0 + 69632)
    assert 18 * TT * 4 <= 1152 and 4 * TT <= 128
    assert 2 * TPC * D * 2 <= 16384 and DT * BS * 2 <= 8192
    assert C0 + 77824 <= 229344, "sbuf overflow"

    pb = [nc.alloc_psum_tensor(f"pb{i}", [128, 512], F32).ap() for i in range(8)]

    guards = []

    def mk(name):
        g = nc.semaphore(name)
        h = g.__enter__()
        guards.append(g)
        return SemS(h)

    tC, vC, sC, gC = mk("tC"), mk("vC"), mk("sC"), mk("gC")
    p5aT, p5aS, p5aV = mk("p5aT"), mk("p5aS"), mk("p5aV")
    ccm, cco = mk("ccm"), mk("cco")
    barS = mk("bar")
    sX = Stream(mk, "sX", 4)
    tT = Stream(mk, "tT", 4)
    gW = Stream(mk, "gW", 4)
    sS = Stream(mk, "sS", 4)
    sL = Stream(mk, "sL", 4)
    oT = Stream(mk, "oT", 4)

    RG8 = [list(range(8))]
    RG4 = [[0, 1, 2, 3], [4, 5, 6, 7]]
    toks = {}
    misc = {}

    bg = nc.Block()
    block = bg.__enter__()

    def barrier():
        barS.n += 5
        tgt = barS.n
        for sec in (block.tensor, block.vector, block.scalar, block.gpsimd, block.sync):
            def _f(eng, tgt=tgt):
                eng.sem_inc(barS.h, 1)
                eng.wait_ge(barS.h, tgt)
            sec(_f)

    def p0():
        xtm_t = {}
        xf_t = {}
        ln_t = {}
        b_sC, b_vC, b_gC = sC.n, vC.n, gC.n

        @block.sync
        def _(sy):
            for nm, t, src in (("ones", onesb, ones_f), ("idf", identf_sb, ident_f),
                               ("idb", identb_sb, ident_b), ("mask", mask_sb, mask_c)):
                toks[nm] = sX.issue(sy, t[:, :], src[:, :])
            for j in range(TT):
                if j >= 2:
                    sy.wait_ge(sC.h, b_sC + j - 1)
                    sy.wait_ge(vC.h, b_vC + 2 * (j - 1))
                xtm_t[j] = sX.issue(sy, xtm_buf[:, (j % 2) * D:(j % 2 + 1) * D],
                                    x_tm[j * 128:(j + 1) * 128, :])

        @block.scalar
        def _(s):
            for j in range(TT):
                _wait(s, xtm_t[j])
                if j >= 2:
                    s.wait_ge(vC.h, b_vC + 2 * (j - 1))
                s.activation(out=xsq_buf[:, (j % 2) * D:(j % 2 + 1) * D],
                             in_=xtm_buf[:, (j % 2) * D:(j % 2 + 1) * D],
                             func=AF.Square).then_inc(sC.h, 1)
            s.wait_ge(vC.h, b_vC + 2 * TT + 4)
            s.activation(out=stat[:, 5 * TT:6 * TT], in_=stat[:, 3 * TT:4 * TT],
                         func=AF.Ln, bias=stat[:, 17 * TT:17 * TT + 1]).then_inc(sC.h, 1)
            s.wait_ge(sC.h, b_sC + TT + 1)
            s.activation(out=stat[:, 6 * TT:7 * TT], in_=stat[:, 5 * TT:6 * TT],
                         func=AF.Exp, scale=-0.5).then_inc(sC.h, 1)
            s.wait_ge(vC.h, b_vC + 2 * TT + 6)
            s.activation(out=stat[:, 8 * TT:9 * TT], in_=stat[:, 7 * TT:8 * TT],
                         func=AF.Ln, bias=stat[:, 17 * TT + 1:17 * TT + 2]).then_inc(sC.h, 1)
            s.wait_ge(sC.h, b_sC + TT + 3)
            s.activation(out=stat[:, 9 * TT:10 * TT], in_=stat[:, 8 * TT:9 * TT],
                         func=AF.Exp, scale=-0.5).then_inc(sC.h, 1)
        sC.n += TT + 4

        @block.vector
        def _(v):
            v.memset(stat[:, 17 * TT:17 * TT + 1], LN_EPS)
            v.memset(stat[:, 17 * TT + 1:17 * TT + 2], RMS_EPS)
            for j in range(TT):
                _wait(v, xtm_t[j])
                v.reduce_sum(stat[:, j:j + 1], xtm_buf[:, (j % 2) * D:(j % 2 + 1) * D],
                             axis=mybir.AxisListType.X).then_inc(vC.h, 1)
                v.wait_ge(sC.h, b_sC + j + 1)
                v.reduce_sum(stat[:, TT + j:TT + j + 1], xsq_buf[:, (j % 2) * D:(j % 2 + 1) * D],
                             axis=mybir.AxisListType.X).then_inc(vC.h, 1)
            ecnt = [2 * TT]

            def edge(ins):
                ecnt[0] += 1
                ins.then_inc(vC.h, 1)
                v.wait_ge(vC.h, b_vC + ecnt[0])

            sm, sq = stat[:, 0:TT], stat[:, TT:2 * TT]
            m, var, tmp = stat[:, 2 * TT:3 * TT], stat[:, 3 * TT:4 * TT], stat[:, 4 * TT:5 * TT]
            edge(v.tensor_scalar_mul(m, sm, 1.0 / D))
            edge(v.tensor_scalar_mul(var, sq, 1.0 / D))
            edge(v.tensor_mul(tmp, m, m))
            edge(v.tensor_sub(var, var, tmp))
            assert ecnt[0] == 2 * TT + 4
            r1 = stat[:, 6 * TT:7 * TT]
            u = stat[:, 7 * TT:8 * TT]
            v.wait_ge(sC.h, b_sC + TT + 2)
            edge(v.tensor_mul(u, var, r1))
            edge(v.tensor_mul(u, u, r1))
            v.wait_ge(sC.h, b_sC + TT + 4)
            r2s = stat[:, 9 * TT:10 * TT]
            r2 = stat[:, 10 * TT:11 * TT]
            mr1 = stat[:, 11 * TT:12 * TT]
            mr2 = stat[:, 12 * TT:13 * TT]
            edge(v.tensor_mul(r2, r1, r2s))
            edge(v.tensor_mul(mr1, m, r1))
            edge(v.tensor_mul(mr2, m, r2))
            edge(v.tensor_copy(out=stat[:, 13 * TT:14 * TT], in_=r1))
            edge(v.tensor_copy(out=stat[:, 14 * TT:15 * TT], in_=mr1))
            edge(v.tensor_copy(out=stat[:, 15 * TT:16 * TT], in_=r2))
            edge(v.tensor_copy(out=stat[:, 16 * TT:17 * TT], in_=mr2))
        vC.n += 2 * TT + 13

        b_tC = tC.n

        @block.tensor
        def _(t):
            t.wait_ge(vC.h, b_vC + 2 * TT + 13)
            _wait(t, toks["idf"])
            t.transpose(pb[4][0:4 * TT, 0:128], stat[:, 13 * TT:17 * TT],
                        identf_sb[:, :]).then_inc(tC.h, 1)
        tC.n += 1

        @block.vector
        def _(v):
            v.wait_ge(tC.h, b_tC + 1)
            v.tensor_copy(out=packT_sb[:, :], in_=pb[4][0:4 * TT, 0:128]).then_inc(vC.h, 1)
        vC.n += 1

        @block.scalar
        def _(s):
            s.wait_ge(vC.h, vC.n)
            toks["vecs"] = sS.issue(s, vecs_scr[:, :], packT_sb[:, :])

        @block.sync
        def _(sy):
            _wait(sy, toks["vecs"])
            toks["bsrc"] = sX.issue(
                sy, bsrc_sb[0:1, :], vecs_scr.rearrange("a p -> (a p)").unsqueeze(0))

        nch = TC // 512
        b_tC2 = tC.n
        b_vC2 = vC.n

        @block.tensor
        def _(t):
            _wait(t, toks["bsrc"])
            _wait(t, toks["ones"])
            idx = 0
            for vv in range(4):
                for ch in range(nch):
                    if idx >= 4:
                        t.wait_ge(vC.h, b_vC2 + idx - 3)
                    mmi = t.matmul(pb[idx % 4][:, :], onesb[0:1, :],
                             bsrc_sb[0:1, vv * TC + ch * 512: vv * TC + (ch + 1) * 512],
                             start=True, stop=True).then_inc(tC.h, 1)
                    idx += 1
        tC.n += 4 * nch

        @block.vector
        def _(v):
            idx = 0
            for vv in range(4):
                for ch in range(nch):
                    v.wait_ge(tC.h, b_tC2 + idx + 1)
                    v.tensor_copy(out=bcast[:, vv * TC + ch * 512: vv * TC + (ch + 1) * 512],
                                  in_=pb[idx % 4][:, :]).then_inc(vC.h, 1)
                    idx += 1
        vC.n += 4 * nch

        b_vC3 = vC.n
        b_gC3 = gC.n

        @block.sync
        def _(sy):
            for d in range(DT):
                if d >= 2:
                    sy.wait_ge(vC.h, b_vC3 + 2 * (d - 1))
                    sy.wait_ge(gC.h, b_gC3 + 2 * (d - 1))
                xf_t[d] = sX.issue(sy, xf_buf[:, (d % 2) * TC:(d % 2 + 1) * TC],
                                   x_fm[d * 128:(d + 1) * 128, :])

        @block.gpsimd
        def _(g):
            g.wait_ge(vC.h, b_vC3)
            for d in range(DT):
                _wait(g, xf_t[d])
                xf = xf_buf[:, (d % 2) * TC:(d % 2 + 1) * TC]
                g.tensor_mul(tmp_g[:, :], xf, bcast[:, 2 * TC:3 * TC]).then_inc(gC.h, 1)
                g.wait_ge(gC.h, b_gC3 + 2 * d + 1)
                g.tensor_tensor(out=rn_sb[:, d * TC:(d + 1) * TC],
                                in0=tmp_g[:, :], in1=bcast[:, 3 * TC:4 * TC],
                                op=ALU.subtract).then_inc(gC.h, 1)
        gC.n += 2 * DT

        @block.scalar
        def _(s):
            for d in range(DT):
                s.wait_ge(vC.h, b_vC3 + 2 * (d + 1))
                ln_t[d] = sS.issue(s, ln_scr[d, :, :],
                                   lnt_buf[:, (d % 2) * TC:(d % 2 + 1) * TC])
            _wait_all(s, sS.totals())

        @block.vector
        def _(v):
            for d in range(DT):
                _wait(v, xf_t[d])
                if d >= 2:
                    _wait(v, ln_t[d - 2])
                xf = xf_buf[:, (d % 2) * TC:(d % 2 + 1) * TC]
                v.tensor_mul(tmp_v[:, :], xf, bcast[:, 0:TC]).then_inc(vC.h, 1)
                v.wait_ge(vC.h, b_vC3 + 2 * d + 1)
                v.tensor_tensor(out=lnt_buf[:, (d % 2) * TC:(d % 2 + 1) * TC],
                                in0=tmp_v[:, :], in1=bcast[:, TC:2 * TC],
                                op=ALU.subtract).then_inc(vC.h, 1)
        vC.n += 2 * DT


    pc = {}
    pcq = []

    def _fill_pcq():
        for i in range(16):
            pcq.append(("wgate", i, wgate_bf[4*i:4*i+4, :, :], wgate_b[4*i:4*i+4, :, :]))
            pcq.append(("wup", i, wup_bf[4*i:4*i+4, :, :], wup_b[4*i:4*i+4, :, :]))
        for i in range(4):
            pcq.append(("wp", i, wp_bf[4*i:4*i+4, :, :], wp_t[4*i:4*i+4, :, :]))
        for q in range(4):
            for i in range(4):
                pcq.append(("wdown", 4*q + i,
                            wdown_bf[q, :, i*8192:(i+1)*8192],
                            wdown_b[q, :, i*8192:(i+1)*8192]))

    def emit_piece(g, n=1):
        for _ in range(n):
            if not pcq:
                return
            name, i, dst, srcap = pcq.pop(0)
            pc.setdefault(name, {})[i] = gW.issue(g, dst, srcap)

    def proj_fm(wt_ap, act_fn, kind, pcname, dest=None):
        nch = TC // 512
        b_tC, b_sC, b_vC, b_gC = tC.n, sC.n, vC.n, gC.n
        per_o_t = nch + (TT if kind == "strips_ktm" else 0)
        w_t = {}
        mm_mil = {}
        tp_mil = {}

        @block.gpsimd
        def _(g):
            for o in range(DT):
                if o < 2:
                    g.wait_ge(tC.h, b_tC)
                else:
                    g.wait_ge(tC.h, b_tC + per_o_t * (o - 1))
                w_t[o] = gW.issue(
                    g,
                    w_buf[:, (o % 2) * DT * 128:(o % 2 + 1) * DT * 128],
                    wt_ap[o, :, :])
                emit_piece(g)

        @block.tensor
        def _(t):
            mil = b_tC

            def emit_tp(s_, j):
                nonlocal mil
                lin = s_ * TT + j
                if lin >= 4:
                    t.wait_ge(vC.h, b_vC + lin - 3)
                t.transpose(pb[4 + (j % 4)].bitcast(BF16)[:, 0:128],
                            strip_buf[:, (s_ % 2) * TC + j * 128:(s_ % 2) * TC + (j + 1) * 128],
                            identb_sb[:, :]).then_inc(tC.h, 1)
                mil += 1
                tp_mil[(s_, j)] = mil

            for o in range(DT):
                _wait(t, w_t[o])
                if o == 0:
                    t.wait_ge(sC.h, b_sC)
                    t.wait_ge(gC.h, b_gC)
                for tci in range(nch):
                    if o >= 1:
                        t.wait_ge(sC.h, b_sC + nch * (o - 1) + tci + 1)
                    for a in range(DT):
                        mmi = t.matmul(pb[tci % 4][:, :],
                                 w_buf[:, (o % 2) * DT * 128 + a * 128:(o % 2) * DT * 128 + (a + 1) * 128],
                                 rn_sb[:, a * TC + tci * 512:a * TC + (tci + 1) * 512],
                                 start=(a == 0), stop=(a == DT - 1))
                    mmi.then_inc(tC.h, 1)
                    mil += 1
                    mm_mil[(o, tci)] = mil
                if kind == "strips_ktm" and o >= 1:
                    t.wait_ge(sC.h, b_sC + nch * o)
                    for j in range(TT):
                        emit_tp(o - 1, j)
            if kind == "strips_ktm":
                t.wait_ge(sC.h, b_sC + nch * DT)
                for j in range(TT):
                    emit_tp(DT - 1, j)
        tC.n += nch * DT + (TT * DT if kind == "strips_ktm" else 0)

        st_toks = {}

        @block.scalar
        def _(s):
            _wait_all(s, sS.totals())
            for o in range(DT):
                if kind == "strips_ktm" and o >= 2:
                    s.wait_ge(tC.h, tp_mil[(o - 2, TT - 1)])
                if kind != "assemble" and o >= 2:
                    _wait(s, st_toks[o - 2])
                for tci in range(nch):
                    s.wait_ge(tC.h, mm_mil[(o, tci)])
                    if kind == "assemble":
                        dst = ktm_sb[:, o * TC + tci * 512:o * TC + (tci + 1) * 512]
                    else:
                        dst = strip_buf[:, (o % 2) * TC + tci * 512:(o % 2) * TC + (tci + 1) * 512]
                    s.activation(out=dst, in_=pb[tci % 4][:, :], func=act_fn).then_inc(sC.h, 1)
                if kind != "assemble":
                    s.wait_ge(sC.h, b_sC + nch * (o + 1))
                    st_toks[o] = sS.issue(s, dest[o, :, :], strip_buf[:, (o % 2) * TC:(o % 2 + 1) * TC])
        sC.n += nch * DT

        if kind == "strips_ktm":
            @block.vector
            def _(v):
                for s_ in range(DT):
                    for j in range(TT):
                        v.wait_ge(tC.h, tp_mil[(s_, j)])
                        v.tensor_copy(out=ktm_sb[:, j * D + s_ * 128:j * D + (s_ + 1) * 128],
                                      in_=pb[4 + (j % 4)].bitcast(BF16)[:, 0:128]).then_inc(vC.h, 1)
            vC.n += DT * TT

    def p1():
        proj_fm(wk_t, AF.Silu, "strips_ktm", "wk", dest=kfm_scr)
        b_vC = vC.n

        @block.scalar
        def _(s):
            s.wait_ge(vC.h, b_vC)
            for l in range(NCC):
                sS.issue(s, ktm_pub_b[l, :, :, :].rearrange("t p d -> p t d"),
                         ktm_sb.rearrange("p (t d) -> p t d", t=TT)[:, TPC * l:TPC * (l + 1), :])
            _wait_all(s, sS.totals())

        b_tC = tC.n
        b_sC = sC.n
        wv_tok = {}
        vst_t = {}

        @block.gpsimd
        def _(g):
            for ev in range(4):
                if ev >= 2:
                    g.wait_ge(tC.h, b_tC + TT * (ev - 1))
                wv_tok[ev] = gW.issue(g, wvq_buf[:, (ev % 2) * DT * EQ:(ev % 2 + 1) * DT * EQ],
                                      wv_t[ev, :, :])
                emit_piece(g)

        @block.tensor
        def _(t):
            for ev in range(4):
                _wait(t, wv_tok[ev])
                for j in range(TT):
                    m = ev * TT + j
                    if m == 0:
                        t.wait_ge(sC.h, b_sC)
                    elif m >= 4:
                        t.wait_ge(sC.h, b_sC + m - 3)
                    for a in range(DT):
                        mmi = t.matmul(pb[m % 4][:, 0:EQ],
                                 rn_sb[:, a * TC + j * 128:a * TC + (j + 1) * 128],
                                 wvq_buf[:, (ev % 2) * DT * EQ + a * EQ:(ev % 2) * DT * EQ + (a + 1) * EQ],
                                 start=(a == 0), stop=(a == DT - 1))
                    mmi.then_inc(tC.h, 1)
        tC.n += 4 * TT

        @block.scalar
        def _(s):
            _wait_all(s, sS.totals())
            for ev in range(4):
                for j in range(TT):
                    m = ev * TT + j
                    if m >= 4:
                        _wait(s, vst_t[m - 4])
                    s.wait_ge(tC.h, b_tC + m + 1)
                    s.activation(out=strip_buf[:, (m % 4) * EQ:(m % 4 + 1) * EQ],
                                 in_=pb[m % 4][:, 0:EQ], func=AF.Silu).then_inc(sC.h, 1)
                    s.wait_ge(sC.h, b_sC + m + 1)
                    vst_t[m] = sS.issue(s, v_pub_b[j // TPC, j % TPC, :, ev * EQ:(ev + 1) * EQ],
                                        strip_buf[:, (m % 4) * EQ:(m % 4 + 1) * EQ])
        sC.n += 4 * TT

        proj_fm(wg_t, AF.Sigmoid, "strips", "wg", dest=g_scr)

        proj_fm(wq_t, AF.Silu, "assemble", "wq")
        b_sC3 = sC.n

        @block.scalar
        def _(s):
            s.wait_ge(sC.h, b_sC3)
            for l in range(NCC):
                sS.issue(s, q_pub_b[l, :, :, :].rearrange("a p t -> p a t"),
                         ktm_sb.rearrange("p (a t) -> p a t", a=DT)[:, :, l * BS:(l + 1) * BS])
            _wait_all(s, sS.totals())

        misc["g_end_tC"] = tC.n

        @block.gpsimd
        def _(g):
            emit_piece(g, n=len(pcq))
            _wait_all(g, sS.totals())
            g.wait_ge(tC.h, tC.n + (TPC + (DT + 7) // 8) * NCC)
            g.collective_compute("AllGather", ALU.bypass, replica_groups=RG4,
                                 ins=[mg_pub2d.opt()], outs=[mg_ag2d.opt()]).then_inc(ccm.h, 1)
        ccm.n += 1

    def p25():
        b_tC, b_vC = tC.n, vC.n
        nr = (DT + 7) // 8
        per_t = TPC + nr
        per_v = TPC + nr
        ld_t = {}
        oist_t = {}

        @block.scalar
        def _(s):
            _wait_all(s, sS.totals())
            for l in range(NCC):
                if l >= 2:
                    s.wait_ge(tC.h, b_tC + per_t * (l - 1))
                h0 = (l % 2) * TPC * D
                t2 = sL.issue(s, ktc_buf[:, h0:h0 + DT * BS].rearrange("p (a f) -> p a f", a=DT),
                              kfm_scr[:, :, l * BS:(l + 1) * BS].rearrange("a p f -> p a f"))
                t3 = sL.issue(s, vv_buf[:, h0:h0 + TPC * D].rearrange("p (t d) -> p t d", t=TPC),
                              v_pub_b[l, :, :, :].rearrange("t p d -> p t d"))
                ld_t[l] = [t2, t3]
                if l >= 1:
                    s.wait_ge(vC.h, b_vC + l * per_v)
                    oist_t[l - 1] = sS.issue(s, oi_scr[l - 1, :, :, :].rearrange("a p f -> p a f"),
                                             oi_sb.rearrange("p (a f) -> p a f", a=DT))
            s.wait_ge(vC.h, b_vC + NCC * per_v)
            oist_t[NCC - 1] = sS.issue(s, oi_scr[NCC - 1, :, :, :].rearrange("a p f -> p a f"),
                                       oi_sb.rearrange("p (a f) -> p a f", a=DT))
            _wait_all(s, sS.totals())

        @block.tensor
        def _(t):
            t.wait_ge(sC.h, sC.n)
            cnt = 0
            for l in range(NCC):
                _wait_all(t, ld_t[l])
                h0 = (l % 2) * TPC * D
                for tp in range(TPC):
                    t.wait_ge(vC.h, b_vC + cnt)
                    for a in range(DT):
                        mmi = t.matmul(pb[6][:, 0:BS],
                                 ktc_buf[:, h0 + a * BS + tp * 128:h0 + a * BS + (tp + 1) * 128],
                                 ktm_sb[:, a * TC + l * BS:a * TC + (l + 1) * BS],
                                 start=(a == 0), stop=(a == DT - 1))
                    mmi.then_inc(tC.h, 1)
                    cnt += 1
                for r_ in range(nr):
                    t.wait_ge(vC.h, b_vC + cnt)
                    for a in range(8 * r_, min(8 * r_ + 8, DT)):
                        for tp in range(TPC):
                            mmi = t.matmul(pb[a % 8][:, 0:BS],
                                     vv_buf[:, h0 + tp * D + a * 128:h0 + tp * D + (a + 1) * 128],
                                     sT_sb[:, tp * BS:(tp + 1) * BS],
                                     start=(tp == 0), stop=(tp == TPC - 1))
                    mmi.then_inc(tC.h, 1)
                    cnt += 1
        tC.n += per_t * NCC

        @block.vector
        def _(v):
            _wait(v, toks["mask"])
            for l in range(NCC):
                for tp in range(TPC):
                    v.wait_ge(tC.h, b_tC + l * per_t + tp + 1)
                    v.tensor_tensor(out=sT_sb[:, tp * BS:(tp + 1) * BS],
                                    in0=pb[6][:, 0:BS],
                                    in1=mask_sb[:, tp * BS:(tp + 1) * BS],
                                    op=ALU.mult).then_inc(vC.h, 1)
                for r_ in range(nr):
                    v.wait_ge(tC.h, b_tC + l * per_t + TPC + r_ + 1)
                    if r_ == 0 and l >= 1:
                        _wait(v, oist_t[l - 1])
                    ins = None
                    for a in range(8 * r_, min(8 * r_ + 8, DT)):
                        ins = v.tensor_copy(out=oi_sb[:, a * BS:(a + 1) * BS],
                                            in_=pb[a % 8][:, 0:BS])
                    ins.then_inc(vC.h, 1)
        vC.n += per_v * NCC
        misc["p25_end_tC"] = tC.n
        misc["p25_oist"] = sS.totals()

        @block.sync
        def _(sy):
            sy.wait_ge(tC.h, misc["g_end_tC"])
            lnh_toks = []
            for qa in range(4):
                lnh_toks.append(
                    tT.issue(sy, lnh_sb[:, qa * 4 * TC:(qa + 1) * 4 * TC].rearrange("p (a f) -> p a f", a=4),
                             ln_scr[qa * 4:(qa + 1) * 4, :, :].rearrange("a p f -> p a f")))
            misc["lnh_toks"] = lnh_toks

    def group_copies():
        @block.gpsimd
        def _(g):
            g.wait_ge(ccm.h, ccm.n)
            s0 = gW.sems[0]
            KR = NCC * TPC * 128
            for arm in g.switch_core_id(n=128):
                ej = arm.logical % 4
                for r in range(4):
                    g.dma_start(out=v_loc[r * KR:(r + 1) * KR, :],
                                in_=v_ag_v[r][:, ej * E:(ej + 1) * E]).then_inc(s0.h, 16)
            s0.n += 64
        misc["vloc_tok"] = (gW.sems[0].h, gW.sems[0].n)

    def p3():
        @block.vector
        def _(v):
            v.memset(kv_sb[:, :], 0.0).then_inc(vC.h, 1)
        vC.n += 1

        b_sC, b_tC, b_vC = sC.n, tC.n, vC.n
        per_t = ET + 4
        per_v = 4
        ld_t = {}
        ost_t = {}

        @block.sync
        def _(sy):
            sy.wait_ge(ccm.h, ccm.n)
            _wait(sy, misc["vloc_tok"])
            sy.wait_ge(tC.h, misc["p25_end_tC"])
            sy.wait_ge(p5aT.h, p5aT.n)
            for ch in range(NCB):
                if ch >= 2:
                    sy.wait_ge(tC.h, b_tC + per_t * (ch - 1))
                r, l = ch // NCC, ch % NCC
                h0 = (ch % 2) * TPC * D
                t1 = sX.issue(sy, qc_buf[:, (ch % 2) * DT * BS:(ch % 2 + 1) * DT * BS].rearrange("p (a f) -> p a f", a=DT),
                              q_ag_v[r][l, :, :, :].rearrange("a p f -> p a f"))
                t2 = sX.issue(sy, ktc_buf[:, h0:h0 + TPC * D].rearrange("p (t d) -> p t d", t=TPC),
                              ktm_ag_v[r][l, :, :, :].rearrange("t p d -> p t d"))
                t3 = sX.issue(sy, vte_buf[:, (ch % 2) * TPC * E:(ch % 2 + 1) * TPC * E].rearrange("p (t e) -> p t e", t=TPC),
                              v_loc_b[r, l, :, :, :].rearrange("t p e -> p t e"))
                ld_t[ch] = [t1, t2, t3]

        @block.scalar
        def _(s):
            for ch in range(NCB):
                for qd in range(4):
                    if ch >= 1:
                        s.wait_ge(vC.h, b_vC + per_v * (ch - 1) + qd + 1)
                    elif qd == 0:
                        s.wait_ge(vC.h, b_vC)
                    s.copy(out=kv_bf[:, qd * 4 * E:(qd + 1) * 4 * E],
                           in_=kv_sb[:, qd * 4 * E:(qd + 1) * 4 * E]).then_inc(sC.h, 1)
                if ch >= 1:
                    ost_t[ch - 1] = sS.issue(s, oint_pub_b[ch - 1, :, :, :].rearrange("e p f -> p e f"),
                                             oi3_sb[:, :].rearrange("p (e f) -> p e f", e=ET))
            s.wait_ge(vC.h, b_vC + per_v * NCB)
            ost_t[NCB - 1] = sS.issue(s, oint_pub_b[NCB - 1, :, :, :].rearrange("e p f -> p e f"),
                                      oi3_sb[:, :].rearrange("p (e f) -> p e f", e=ET))
            _wait_all(s, sS.totals())
        sC.n += 4 * NCB

        @block.tensor
        def _(t):
            t.wait_ge(p5aS.h, p5aS.n)
            t.wait_ge(p5aV.h, p5aV.n)
            for ch in range(NCB):
                _wait_all(t, ld_t[ch])
                if ch >= 1:
                    t.wait_ge(vC.h, b_vC + per_v * (ch - 1) + 1)
                qc0 = (ch % 2) * DT * BS
                h0 = (ch % 2) * TPC * D
                vt0 = (ch % 2) * TPC * E
                for a in range(DT):
                    if a % 4 == 0:
                        t.wait_ge(sC.h, b_sC + 4 * ch + a // 4 + 1)
                    for et in range(ET):
                        mmi = t.matmul(pb[[6, 7, 0, 1][et]][:, 0:BS],
                                 kv_bf[:, a * E + et * 128:a * E + (et + 1) * 128],
                                 qc_buf[:, qc0 + a * BS:qc0 + (a + 1) * BS],
                                 start=(a == 0), stop=(a == DT - 1))
                    if a % 4 == 3:
                        mmi.then_inc(tC.h, 1)
                nq = max(1, DT // 4)
                for qd in range(4):
                    t.wait_ge(vC.h, b_vC + per_v * ch + qd)
                    for ai in range(nq):
                        a = qd * nq + ai
                        if a >= DT:
                            continue
                        for tp in range(TPC):
                            mmi = t.matmul(pb[2 + (a % 4)][:, 0:E],
                                     ktc_buf[:, h0 + tp * D + a * 128:h0 + tp * D + (a + 1) * 128],
                                     vte_buf[:, vt0 + tp * E:vt0 + (tp + 1) * E],
                                     start=(tp == 0), stop=(tp == TPC - 1))
                    mmi.then_inc(tC.h, 1)
        tC.n += per_t * NCB

        @block.vector
        def _(v):
            _wait_all(v, misc["p25_oist"])
            for ch in range(NCB):
                v.wait_ge(tC.h, b_tC + ch * per_t + ET)
                if ch >= 1:
                    _wait(v, ost_t[ch - 1])
                ins = None
                for et in range(ET):
                    ins = v.tensor_copy(out=oi3_sb[:, et * BS:(et + 1) * BS],
                                        in_=pb[[6, 7, 0, 1][et]][:, 0:BS])
                nq = max(1, DT // 4)
                for qd in range(4):
                    v.wait_ge(tC.h, b_tC + ch * per_t + ET + qd + 1)
                    for ai in range(nq):
                        a = qd * nq + ai
                        if a >= DT:
                            continue
                        ins = v.tensor_tensor(out=kv_sb[:, a * E:(a + 1) * E],
                                              in0=kv_sb[:, a * E:(a + 1) * E],
                                              in1=pb[2 + (a % 4)][:, 0:E],
                                              op=ALU.add)
                    ins.then_inc(vC.h, 1)
        vC.n += per_v * NCB

        misc["p3_end_tC"] = tC.n
        misc["p3_sS"] = sS.totals()

    def oint_gather():
        @block.gpsimd
        def _(g):
            _wait_all(g, misc["p3_sS"])
            g.collective_compute("AllGather", ALU.bypass, replica_groups=RG4,
                                 ins=[oint_pub.opt()], outs=[oint_ag.opt()]).then_inc(cco.h, 1)
        cco.n += 1

    def oint_copy():
        @block.gpsimd
        def _(g):
            g.wait_ge(cco.h, cco.n)
            s1 = gW.sems[1]
            for arm in g.switch_core_id(n=128):
                q4 = arm.logical % 4
                for r in range(4):
                    base = r * NCB * E + q4 * NCC * E
                    g.dma_start(out=oint_loc[r * NCC * E:(r + 1) * NCC * E, :],
                                in_=oint_ag[base:base + NCC * E, :]).then_inc(s1.h, 16)
            s1.n += 64
        misc["oint_tok"] = (gW.sems[1].h, gW.sems[1].n)

    def emit_p5a(h0, h1):
        ntc = TC // 512
        w_t = misc.setdefault("p5a_wt", {})
        gst_t = misc.setdefault("p5a_gst", {})
        pre_sS = sS.totals()
        wg1, wg2 = (wbA1, wbA2) if h0 == 0 else (w_buf, w_buf2)

        @block.sync
        def _(sy):
            if h0 > 0:
                _wait_all(sy, misc["p25_oist"])
            for h in range(h0, h1):
                if h >= 2:
                    sy.wait_ge(p5aT.h, 2 * ntc * (h - 1))
                elif h0 == 0:
                    sy.wait_ge(tC.h, misc["p25_end_tC"])
                _wait(sy, pc["wgate"][h // 4])
                _wait(sy, pc["wup"][h // 4])
                w_t[(h, 0)] = sX.issue(
                    sy, wg1[:, (h % 2) * DT * 128:(h % 2 + 1) * DT * 128],
                    wgate_bf[h, :, :])
                w_t[(h, 1)] = sX.issue(
                    sy, wg2[:, (h % 2) * DT * 128:(h % 2 + 1) * DT * 128],
                    wup_bf[h, :, :])

        @block.tensor
        def _(t):
            if h0 == 0:
                _wait_all(t, misc["lnh_toks"])
            t.wait_ge(sC.h, sC.n)
            t.wait_ge(vC.h, vC.n)
            for h in range(h0, h1):
                _wait(t, w_t[(h, 0)])
                _wait(t, w_t[(h, 1)])
                for tci in range(ntc):
                    if h >= 1:
                        t.wait_ge(p5aS.h, ntc * (h - 1) + tci + 1)
                    for a in range(DT):
                        mmi = t.matmul(pb[tci][:, :],
                                 wg1[:, (h % 2) * DT * 128 + a * 128:(h % 2) * DT * 128 + (a + 1) * 128],
                                 lnh_sb[:, a * TC + tci * 512:a * TC + (tci + 1) * 512],
                                 start=(a == 0), stop=(a == DT - 1))
                    mmi.then_inc(p5aT.h, 1)
                for tci in range(ntc):
                    if h >= 1:
                        t.wait_ge(p5aV.h, ntc * (h - 1) + tci + 1)
                    for a in range(DT):
                        mmi = t.matmul(pb[4 + tci][:, :],
                                 wg2[:, (h % 2) * DT * 128 + a * 128:(h % 2) * DT * 128 + (a + 1) * 128],
                                 lnh_sb[:, a * TC + tci * 512:a * TC + (tci + 1) * 512],
                                 start=(a == 0), stop=(a == DT - 1))
                    mmi.then_inc(p5aT.h, 1)
        p5aT.n += 2 * ntc * (h1 - h0)

        @block.scalar
        def _(s):
            for h in range(h0, h1):
                if h >= 2:
                    s.wait_ge(p5aV.h, ntc * (h - 1))
                for tci in range(ntc):
                    s.wait_ge(p5aT.h, h * 2 * ntc + tci + 1)
                    s.activation(out=silB_sb[:, (h % 2) * TC + tci * 512:(h % 2) * TC + (tci + 1) * 512],
                                 in_=pb[tci][:, :], func=AF.Silu).then_inc(p5aS.h, 1)
                if h >= 1:
                    s.wait_ge(p5aV.h, ntc * h)
                    gst_t[h - 1] = sS.issue(s, gated_scr[h - 1, :, :],
                                            gatB_sb[:, ((h - 1) % 4) * TC:((h - 1) % 4) * TC + TC])
            if h1 == HT:
                s.wait_ge(p5aV.h, ntc * HT)
                gst_t[HT - 1] = sS.issue(s, gated_scr[HT - 1, :, :],
                                         gatB_sb[:, ((HT - 1) % 4) * TC:((HT - 1) % 4) * TC + TC])
        p5aS.n += ntc * (h1 - h0)

        @block.vector
        def _(v):
            if h0 == 0:
                _wait_all(v, pre_sS)
            for h in range(h0, h1):
                if h >= 4:
                    _wait(v, gst_t[h - 4])
                for tci in range(ntc):
                    v.wait_ge(p5aT.h, h * 2 * ntc + ntc + tci + 1)
                    v.wait_ge(p5aS.h, ntc * h + tci + 1)
                    v.tensor_tensor(out=gatB_sb[:, (h % 4) * TC + tci * 512:(h % 4) * TC + (tci + 1) * 512],
                                    in0=silB_sb[:, (h % 2) * TC + tci * 512:(h % 2) * TC + (tci + 1) * 512],
                                    in1=pb[4 + tci][:, :], op=ALU.mult).then_inc(p5aV.h, 1)
        p5aV.n += ntc * (h1 - h0)

    def p4():
        nhc = NCC // 2
        ntc = TH // 512
        b_sC_em = sC.n
        for th in range(2):
            b_tC0 = tC.n
            b_vC = vC.n
            ld_t = {}
            x_t = {}
            xst_t = {}
            w_t = {}

            @block.sync
            def _(sy):
                if th == 0:
                    _wait_all(sy, misc["p3_sS"])
                    _wait(sy, misc["oint_tok"])
                    sy.wait_ge(p5aT.h, p5aT.n)
                else:
                    sy.wait_ge(tC.h, b_tC0)
                ld_t["gh"] = sX.issue(sy, gh_sb[:, :].rearrange("p (a f) -> p a f", a=DT),
                                      g_scr[:, :, th * TH:(th + 1) * TH].rearrange("a p f -> p a f"))
                for li in range(nhc):
                    l = th * nhc + li
                    if li >= 1:
                        sy.wait_ge(vC.h, b_vC + 2 * li)
                    tokens = [sX.issue(sy, oi4_sb[:, :].rearrange("p (a f) -> p a f", a=DT),
                                       oi_scr[l, :, :, :].rearrange("a p f -> p a f"))]
                    for r in range(4):
                        tokens.append(sX.issue(
                            sy, oint_buf[:, r * ET * BS:(r + 1) * ET * BS].rearrange("p (e f) -> p e f", e=ET),
                            oint_loc_b[r, l, :, :, :].rearrange("e p f -> p e f")))
                    ld_t[li] = tokens

            @block.vector
            def _(v):
                if th == 0:
                    v.wait_ge(p5aT.h, p5aT.n)
                else:
                    v.wait_ge(tC.h, b_tC0)
                _wait(v, ld_t["gh"])
                for li in range(nhc):
                    _wait_all(v, ld_t[li])
                    v.tensor_add(oi4_sb[:, :], oi4_sb[:, :], oint_buf[:, :]).then_inc(vC.h, 1)
                    v.wait_ge(vC.h, b_vC + 2 * li + 1)
                    v.tensor_tensor(
                        out=og_sb.rearrange("p (a f) -> p a f", a=DT)[:, :, li * BS:(li + 1) * BS],
                        in0=oi4_sb.rearrange("p (a f) -> p a f", a=DT),
                        in1=gh_sb.rearrange("p (a f) -> p a f", a=DT)[:, :, li * BS:(li + 1) * BS],
                        op=ALU.mult).then_inc(vC.h, 1)
            vC.n += 2 * nhc

            b_tC, b_vC2 = tC.n, vC.n

            @block.scalar
            def _(s):
                if th == 0:
                    _wait_all(s, sS.totals())
                    s.wait_ge(tC.h, misc["p3_end_tC"])
                    _wait_all(s, [pc["wdown"][i] for i in range(4)])
                    misc["wd_q"] = {0: sL.issue(s, wdB_sb[:, :], wdown_bf[0, :, :])}
                for do in range(DT):
                    if do < 2:
                        if th == 0:
                            s.wait_ge(p5aT.h, p5aT.n)
                        else:
                            s.wait_ge(tC.h, b_tC)
                    else:
                        s.wait_ge(tC.h, b_tC + ntc * (do - 1))
                    _wait(s, pc["wp"][do // 4])
                    w_t[do] = sL.issue(
                        s, w_buf[:, (do % 2) * DT * 128:(do % 2 + 1) * DT * 128],
                        wp_bf[do, :, :])

            @block.sync
            def _(sy):
                for do in range(DT):
                    if do >= 4:
                        sy.wait_ge(vC.h, b_vC2 + ntc * (do - 3))
                    x_t[do] = sX.issue(sy, xf_buf[:, (do % 4) * TH:(do % 4 + 1) * TH],
                                       x_fm[do * 128:(do + 1) * 128, th * TH:(th + 1) * TH])
                    if do >= 1:
                        sy.wait_ge(vC.h, b_vC2 + ntc * do)
                        xst_t[do - 1] = tT.issue(sy, xattn_scr[do - 1, :, th * TH:(th + 1) * TH],
                                                 xa_buf[:, ((do - 1) % 2) * TH:((do - 1) % 2) * TH + TH])
                sy.wait_ge(vC.h, b_vC2 + ntc * DT)
                xst_t[DT - 1] = tT.issue(sy, xattn_scr[DT - 1, :, th * TH:(th + 1) * TH],
                                         xa_buf[:, ((DT - 1) % 2) * TH:((DT - 1) % 2) * TH + TH])
                _wait_all(sy, tT.totals())

            @block.tensor
            def _(t):
                t.wait_ge(vC.h, b_vC + 2 * nhc)
                for do in range(DT):
                    _wait(t, w_t[do])
                    for tci in range(ntc):
                        if do == 0 and tci == 0 and th == 0:
                            t.wait_ge(p5aS.h, p5aS.n)
                        if do >= 2:
                            t.wait_ge(vC.h, b_vC2 + ntc * (do - 2) + tci + 1)
                        for a in range(DT):
                            mmi = t.matmul(pb[(do % 2) * 2 + tci][:, :],
                                     w_buf[:, (do % 2) * DT * 128 + a * 128:(do % 2) * DT * 128 + (a + 1) * 128],
                                     og_sb[:, a * TH + tci * 512:a * TH + (tci + 1) * 512],
                                     start=(a == 0), stop=(a == DT - 1))
                        mmi.then_inc(tC.h, 1)
            tC.n += ntc * DT

            @block.vector
            def _(v):
                for do in range(DT):
                    _wait(v, x_t[do])
                    if do >= 2:
                        _wait(v, xst_t[do - 2])
                    for tci in range(ntc):
                        v.wait_ge(tC.h, b_tC + ntc * do + tci + 1)
                        v.tensor_tensor(out=xa_buf[:, (do % 2) * TH + tci * 512:(do % 2) * TH + (tci + 1) * 512],
                                        in0=pb[(do % 2) * 2 + tci][:, :],
                                        in1=xf_buf[:, (do % 4) * TH + tci * 512:(do % 4) * TH + (tci + 1) * 512],
                                        op=ALU.add).then_inc(vC.h, 1)
            vC.n += ntc * DT
        misc["p4_end_tC"] = tC.n
        misc["p4_tT"] = tT.totals()

    def p5b():
        nq = 4
        ndt = DQ // 128
        ntcf = TC // 512
        NCI = nq * ntcf
        b_tC, b_vC = tC.n, vC.n
        gld_t = {}
        xld_t = {}
        ost_t = {}
        wd_q = misc["wd_q"]
        wdq_sb = [wdB_sb, wdA_sb, wdB_sb, wdA_sb]

        @block.scalar
        def _(s):
            s.wait_ge(tC.h, misc["p4_end_tC"])
            _wait_all(s, [pc["wdown"][4 + i] for i in range(4)])
            wd_q[1] = sL.issue(s, wdA_sb[:, :], wdown_bf[1, :, :])
            s.wait_ge(p5aV.h, p5aV.n)
            _wait_all(s, misc["p3_sS"])
            for ci in range(NCI):
                q, tci = divmod(ci, ntcf)
                if ci == ntcf:
                    s.wait_ge(tC.h, b_tC + ntcf * HT)
                    _wait_all(s, [pc["wdown"][8 + i] for i in range(4)])
                    wd_q[2] = sL.issue(s, wdB_sb[:, :], wdown_bf[2, :, :])
                elif ci == 2 * ntcf:
                    s.wait_ge(tC.h, b_tC + 2 * ntcf * HT)
                    _wait_all(s, [pc["wdown"][12 + i] for i in range(4)])
                    wd_q[3] = sL.issue(s, wdA_sb[:, :], wdown_bf[3, :, :])
                for hp in range(HT // 2):
                    pj = ci * (HT // 2) + hp
                    h0 = 2 * hp
                    if pj >= 4:
                        s.wait_ge(tC.h, b_tC + 2 * pj - 6)
                    _wait(s, misc["p5a_gst"][h0])
                    _wait(s, misc["p5a_gst"][h0 + 1])
                    gld_t[pj] = sL.issue(
                        s, gld_buf[:, (pj % 4) * 1024:(pj % 4) * 1024 + 1024].rearrange("p (h f) -> p h f", h=2),
                        gated_scr[h0:h0 + 2, :, tci * 512:(tci + 1) * 512].rearrange("h p f -> p h f"))
                for a in range(ndt):
                    idx2 = ci * ndt + a
                    if idx2 < 4:
                        _wait_all(s, misc["p4_tT"])
                    else:
                        s.wait_ge(oT.sems[idx2 % 4].h, 16 * ((idx2 - 4) // 4 + 1))
                    xld_t[idx2] = sL.issue(s, xa_buf[:, (idx2 % 4) * 512:(idx2 % 4) * 512 + 512],
                                           xattn_scr[q * ndt + a, :, tci * 512:(tci + 1) * 512])

        @block.tensor
        def _(t):
            for q in range(nq):
                _wait(t, wd_q[q])
                for tci in range(ntcf):
                    ci = q * ntcf + tci
                    if ci == 0:
                        t.wait_ge(vC.h, b_vC)
                        t.wait_ge(p5aV.h, p5aV.n)
                    elif ci >= 2:
                        t.wait_ge(vC.h, b_vC + ndt * (ci - 1))
                    for h in range(HT):
                        pj = ci * (HT // 2) + h // 2
                        if h % 2 == 0:
                            _wait(t, gld_t[pj])
                        for a in range(ndt):
                            mmi = t.matmul(pb[(tci % 2) * 4 + a][:, :],
                                     wdq_sb[q][:, h * DQ + a * 128:h * DQ + (a + 1) * 128],
                                     gld_buf[:, (pj % 4) * 1024 + (h % 2) * 512:(pj % 4) * 1024 + (h % 2) * 512 + 512],
                                     start=(h == 0), stop=(h == HT - 1))
                        mmi.then_inc(tC.h, 1)
        tC.n += NCI * HT

        @block.vector
        def _(v):
            for ci in range(NCI):
                q, tci = divmod(ci, ntcf)
                v.wait_ge(tC.h, b_tC + (ci + 1) * HT)
                for a in range(ndt):
                    idx2 = ci * ndt + a
                    _wait(v, xld_t[idx2])
                    v.tensor_tensor(out=xa_buf[:, (idx2 % 4) * 512:(idx2 % 4) * 512 + 512],
                                    in0=pb[(tci % 2) * 4 + a][:, :],
                                    in1=xa_buf[:, (idx2 % 4) * 512:(idx2 % 4) * 512 + 512],
                                    op=ALU.add).then_inc(vC.h, 1)
        vC.n += NCI * ndt

        @block.sync
        def _(sy):
            for ci in range(NCI):
                q, tci = divmod(ci, ntcf)
                for a in range(ndt):
                    idx2 = ci * ndt + a
                    d_abs = q * ndt + a
                    sy.wait_ge(vC.h, b_vC + idx2 + 1)
                    ost_t[idx2] = oT.issue(sy, out_fm[d_abs * 128:(d_abs + 1) * 128,
                                                      tci * 512:(tci + 1) * 512],
                                           xa_buf[:, (idx2 % 4) * 512:(idx2 % 4) * 512 + 512])
            _wait_all(sy, tT.totals())
            _wait_all(sy, oT.totals())

    _fill_pcq()
    p0()
    barrier()
    if phases == 0.3:
        proj_fm(wk_t, AF.Silu, "strips", "wk", dest=kfm_scr)
    elif phases == 0.6:
        proj_fm(wk_t, AF.Silu, "strips_ktm", "wk", dest=kfm_scr)
    elif phases == 0.8:
        proj_fm(wk_t, AF.Silu, "strips_ktm", "wk", dest=kfm_scr)

        @block.gpsimd
        def _(g):
            _wait_all(g, sS.totals())
            g.collective_compute("AllGather", ALU.bypass, replica_groups=RG4,
                                 ins=[mg_pub2d.opt()], outs=[mg_ag2d.opt()]).then_inc(ccm.h, 1)
            g.wait_ge(ccm.h, 1)
    if phases >= 1:
        p1()
    if phases >= 2:
        p25()
    if phases >= 3:
        group_copies()
    H1 = 48
    if phases >= 6:
        emit_p5a(0, H1)
    if phases >= 4:
        p3()
    if phases >= 6:
        emit_p5a(H1, HT)
    if phases >= 5:
        oint_gather()
        oint_copy()
    if phases >= 7:
        p4()
    if phases >= 8:
        p5b()
    if phases < 8:
        @block.sync
        def _(sy):
            _wait_all(sy, sX.totals())
            _wait_all(sy, tT.totals())
        @block.scalar
        def _(s):
            _wait_all(s, sS.totals())
            _wait_all(s, sL.totals())
        @block.gpsimd
        def _(g):
            _wait_all(g, gW.totals())

    if DEBUG_DUMP:
        scr_map = dict(ln_scr=ln_scr, g_scr=g_scr, kfm_scr=kfm_scr, oi_scr=oi_scr,
                       gated_scr=gated_scr, xattn_scr=xattn_scr, oint_pub=oint_pub,
                       v_loc=v_loc, oint_loc=oint_loc)
        ten = scr_map[DEBUG_DUMP]
        flat = ten.rearrange(" ".join(chr(97 + i) for i in range(len(ten.shape))) + " -> ("
                             + " ".join(chr(97 + i) for i in range(len(ten.shape))) + ")")
        dbg = nc.dram_tensor("dbg_out", list(ten.shape), ten.dtype, kind="ExternalOutput").ap()
        dbgf = dbg.rearrange(" ".join(chr(97 + i) for i in range(len(ten.shape))) + " -> ("
                             + " ".join(chr(97 + i) for i in range(len(ten.shape))) + ")")

        @block.sync
        def _(sy):
            _wait_all(sy, tT.totals())
            sy.dma_start(out=dbgf, in_=flat).then_inc(tT.sems[0].h, 16)
            tT.sems[0].n += 16
            sy.wait_ge(tT.sems[0].h, tT.sems[0].n)

    bg.__exit__(None, None, None)
    for g in reversed(guards):
        g.__exit__(None, None, None)
    return nc



def _stage_inputs(c: Cfg, x, Wq, Wk, Wv, Wg, Wp, w_gate, w_up, w_down):
    asc = np.ascontiguousarray
    DT = c.DT

    def blk(wT):
        a = wT.reshape(DT, 128, DT, 128).transpose(2, 1, 0, 3)
        return asc(a.reshape(DT, 128, DT * 128))

    wq_t = blk(Wq.T)
    wk_t = blk(Wk.T)
    wg_t = blk(Wg.T)
    wp_t = blk(Wp.T)
    EQ = c.D // 4
    wv_t = asc(Wv.T.reshape(DT, 128, 4, EQ).transpose(2, 1, 0, 3).reshape(4, 128, DT * EQ))
    wgate_b = asc(w_gate.T.reshape(DT, 128, c.HT, 128).transpose(2, 1, 0, 3).reshape(c.HT, 128, DT * 128))
    wup_b = asc(w_up.T.reshape(DT, 128, c.HT, 128).transpose(2, 1, 0, 3).reshape(c.HT, 128, DT * 128))
    DQ = c.D // 4
    wdown_b = asc(w_down.T.reshape(c.HT, 128, 4, DQ).transpose(2, 1, 0, 3).reshape(4, 128, c.HT * DQ))
    in_maps = []
    for core in range(8):
        b, q4 = core // 4, core % 4
        sl = x[b, q4 * c.TC:(q4 + 1) * c.TC, :]
        in_maps.append(dict(
            x_fm=asc(sl.T), x_tm=asc(sl),
            wq_t=wq_t, wk_t=wk_t, wv_t=wv_t, wg_t=wg_t, wp_t=wp_t,
            wgate_b=wgate_b, wup_b=wup_b, wdown_b=wdown_b,
        ))
    return in_maps


def run(c: Cfg, x, Wq, Wk, Wv, Wg, Wp, w_gate, w_up, w_down):
    nc = build_program(c)
    in_maps = _stage_inputs(c, x, Wq, Wk, Wv, Wg, Wp, w_gate, w_up, w_down)
    res = run_bass_kernel_spmd(nc, in_maps, core_ids=list(range(8)))
    out = np.empty((2, 4 * c.TC, c.D), np.float32)
    for core in range(8):
        b, q4 = core // 4, core % 4
        out[b, q4 * c.TC:(q4 + 1) * c.TC, :] = res.results[core]["out_fm"].T
    return out


def kernel(L0_attn_x, L0_attn_Wq, L0_attn_Wk, L0_attn_Wv, L0_attn_Wg, L0_attn_Wp,
           L0_ffn_w_gate, L0_ffn_w_up, L0_ffn_w_down):
    c = Cfg(D=2048, TC=2048, H=8192)
    return run(c, np.asarray(L0_attn_x), np.asarray(L0_attn_Wq), np.asarray(L0_attn_Wk),
               np.asarray(L0_attn_Wv), np.asarray(L0_attn_Wg), np.asarray(L0_attn_Wp),
               np.asarray(L0_ffn_w_gate), np.asarray(L0_ffn_w_up), np.asarray(L0_ffn_w_down))


# version 13 (final)
# speedup vs baseline: 1.2999x; 1.0370x over previous
import numpy as np
import ml_dtypes
import concourse.bass as bass
import concourse.mybir as mybir
from concourse.bass_utils import run_bass_kernel_spmd

F32 = mybir.dt.float32
BF16 = mybir.dt.bfloat16
AF = mybir.ActivationFunctionType
ALU = mybir.AluOpType

DEBUG_DUMP = None
LN_EPS = 1e-5
RMS_EPS = float(np.finfo(np.float32).eps)


class Cfg:
    def __init__(self, D, TC, H, BS=256):
        self.D = D
        self.TC = TC
        self.H = H
        self.BS = BS
        self.DT = D // 128
        self.TT = TC // 128
        self.E = D // 4
        self.ET = self.E // 128
        self.NCC = TC // BS
        self.NCB = 4 * self.NCC
        self.TH = TC // 2
        self.HT = H // 128
        self.TPC = BS // 128
        assert D % 512 == 0 and TC % 512 == 0 and BS == 256
        assert self.TH % 512 == 0 and H % 128 == 0


class SemS:
    def __init__(self, handle):
        self.h = handle
        self.n = 0


class Stream:

    def __init__(self, mk, name, k=2):
        self.sems = [mk(f"{name}{i}") for i in range(k)]
        self.k = k
        self._rr = 0

    def issue(self, eng, out, in_):
        s = self.sems[self._rr % self.k]
        self._rr += 1
        if s.n > 0:
            eng.wait_ge(s.h, s.n)
        eng.dma_start(out=out, in_=in_).then_inc(s.h, 16)
        s.n += 16
        return (s.h, s.n)

    def totals(self):
        return [(s.h, s.n) for s in self.sems]


def _wait(eng, tok):
    eng.wait_ge(tok[0], tok[1])


def _wait_all(eng, toks):
    for t in toks:
        eng.wait_ge(t[0], t[1])


def build_program(c: Cfg, phases=99):
    global DEBUG_DUMP
    nc = bass.Bass("TRN2", target_bir_lowering=False, detect_race_conditions=False)
    D, TC, DT, TT, BS, ET, E = c.D, c.TC, c.DT, c.TT, c.BS, c.ET, c.E
    NCC, NCB, TH, HT, TPC = c.NCC, c.NCB, c.TH, c.HT, c.TPC

    EQ = D // 4
    DQ = D // 4
    x_fm = nc.dram_tensor("x_fm", [D, TC], F32, kind="ExternalInput").ap()
    x_tm = nc.dram_tensor("x_tm", [TC, D], F32, kind="ExternalInput").ap()
    wq_t = nc.dram_tensor("wq_t", [DT, 128, DT * 128], F32, kind="ExternalInput").ap()
    wk_t = nc.dram_tensor("wk_t", [DT, 128, DT * 128], F32, kind="ExternalInput").ap()
    wv_t = nc.dram_tensor("wv_t", [4, 128, DT * EQ], F32, kind="ExternalInput").ap()
    wg_t = nc.dram_tensor("wg_t", [DT, 128, DT * 128], F32, kind="ExternalInput").ap()
    wp_t = nc.dram_tensor("wp_t", [DT, 128, DT * 128], F32, kind="ExternalInput").ap()
    wgate_b = nc.dram_tensor("wgate_b", [HT, 128, DT * 128], F32, kind="ExternalInput").ap()
    wup_b = nc.dram_tensor("wup_b", [HT, 128, DT * 128], F32, kind="ExternalInput").ap()
    wdown_b = nc.dram_tensor("wdown_b", [4, 128, HT * DQ], F32, kind="ExternalInput").ap()
    out_fm = nc.dram_tensor("out_fm", [D, TC], F32, kind="ExternalOutput").ap()

    ident_f = nc.inline_tensor(np.eye(128, dtype=np.float32), "ident_f").ap()
    ident_b = nc.inline_tensor(np.eye(128, dtype=np.float32).astype(ml_dtypes.bfloat16), "ident_b").ap()
    ones_f = nc.inline_tensor(np.ones((1, 128), np.float32), "ones_f").ap()
    mtri = np.triu(np.ones((BS, BS), np.float32))
    mtri = np.ascontiguousarray(
        mtri.reshape(TPC, 128, BS).transpose(1, 0, 2).reshape(128, TPC * BS)
    ).astype(ml_dtypes.bfloat16)
    mask_c = nc.inline_tensor(mtri, "mask_c").ap()

    def dram(name, shape, dt, **kw):
        return nc.dram_tensor(name, shape, dt, **kw).ap()

    ln_scr = dram("ln_scr", [DT, 128, TC], BF16)
    g_scr = dram("g_scr", [DT, 128, TC], BF16)
    kfm_scr = dram("kfm_scr", [DT, 128, TC], BF16)
    oi_scr = dram("oi_scr", [NCC, DT, 128, BS], BF16)
    gated_scr = dram("gated_scr", [HT, 128, TC], BF16)

    wq_bf = dram("wq_bf", [DT, 128, DT * 128], BF16)
    wk_bf = dram("wk_bf", [DT, 128, DT * 128], BF16)
    wv_bf = dram("wv_bf", [4, 128, DT * EQ], BF16)
    wg_bf = dram("wg_bf", [DT, 128, DT * 128], BF16)
    wp_bf = dram("wp_bf", [DT, 128, DT * 128], BF16)
    wgate_bf = dram("wgate_bf", [HT, 128, DT * 128], BF16)
    wup_bf = dram("wup_bf", [HT, 128, DT * 128], BF16)
    wdown_bf = dram("wdown_bf", [4, 128, HT * DQ], BF16)
    xattn_scr = dram("xattn_scr", [DT, 128, TC], F32)
    vecs_scr = dram("vecs_scr", [4 * TT, 128], F32)

    QSZ = NCC * DT * 128 * BS
    KSZ = NCC * TPC * 128 * D
    MGE = QSZ + 2 * KSZ
    mg_pub = dram("mg_pub", [MGE], BF16)
    mg_ag = dram("mg_ag", [4 * MGE], BF16)
    mg_pub2d = mg_pub.rearrange("(x c) -> x c", c=2048)
    mg_ag2d = mg_ag.rearrange("(x c) -> x c", c=2048)
    oint_pub = dram("oint_pub", [NCB * E, BS], BF16)
    oint_ag = dram("oint_ag", [4 * NCB * E, BS], BF16)

    v_loc = dram("v_loc", [4 * NCC * TPC * 128, E], BF16)
    oint_loc = dram("oint_loc", [4 * NCC * E, BS], BF16)

    q_pub_b = mg_pub[0:QSZ].rearrange("(l a p f) -> l a p f", l=NCC, a=DT, p=128)
    ktm_pub_b = mg_pub[QSZ:QSZ + KSZ].rearrange("(l t p d) -> l t p d", l=NCC, t=TPC, p=128)
    v_pub_b = mg_pub[QSZ + KSZ:MGE].rearrange("(l t p d) -> l t p d", l=NCC, t=TPC, p=128)
    q_ag_v = [mg_ag[r * MGE:r * MGE + QSZ].rearrange("(l a p f) -> l a p f", l=NCC, a=DT, p=128)
              for r in range(4)]
    ktm_ag_v = [mg_ag[r * MGE + QSZ:r * MGE + QSZ + KSZ].rearrange("(l t p d) -> l t p d", l=NCC, t=TPC, p=128)
                for r in range(4)]
    v_ag_v = [mg_ag[r * MGE + QSZ + KSZ:(r + 1) * MGE].rearrange("(x d) -> x d", d=D)
              for r in range(4)]
    v_loc_b = v_loc.rearrange("(r l t p) e -> r l t p e", r=4, l=NCC, t=TPC, p=128)
    oint_pub_b = oint_pub.rearrange("(ch e p) f -> ch e p f", ch=NCB, e=ET, p=128)
    oint_loc_b = oint_loc.rearrange("(r l e p) f -> r l e p f", r=4, l=NCC, e=ET, p=128)

    def sb(name, shape, dtype, off):
        return nc.alloc_sbuf_tensor_at(name, shape, dtype, offset=off).ap()

    AR = 16512
    A0, B0, C0 = AR, AR + DT * TC * 2, AR + 2 * DT * TC * 2
    rn_sb = sb("rn_sb", [128, DT * TC], BF16, A0)
    lnh_sb = sb("lnh_sb", [128, DT * TC], BF16, A0)
    og_sb = sb("og_sb", [128, DT * TH], BF16, A0)
    gh_sb = sb("gh_sb", [128, DT * TH], BF16, A0 + DT * TH * 2)
    wdA_sb = sb("wdA_sb", [128, HT * DQ], BF16, A0)
    ktm_sb = sb("ktm_sb", [128, TT * D], BF16, B0)
    p0need = 4 * TC * 4 + 2 * TC * 2 + 2 * TC * 4
    p0b = B0 if p0need <= DT * TC * 2 else C0 + 77824
    assert p0b + p0need <= 229344
    bcast = sb("bcast", [128, 4 * TC], F32, p0b)
    lnt_buf = sb("lnt_buf", [128, 2 * TC], BF16, p0b + 4 * TC * 4)
    tmp_v = sb("tmp_v", [128, TC], F32, p0b + 4 * TC * 4 + 2 * TC * 2)
    tmp_g = sb("tmp_g", [128, TC], F32, p0b + 4 * TC * 4 + 2 * TC * 2 + TC * 4)
    kv_sb = sb("kv_sb", [128, DT * E], F32, B0)
    kv_bf = sb("kv_bf", [128, DT * E], BF16, B0 + DT * E * 4)
    gatB_sb = sb("gatB_sb", [128, 4 * TC], BF16, B0 + 6 * TC * 4)
    silB_sb = sb("silB_sb", [128, 2 * TC], BF16, C0 + 71680)
    wdB_sb = sb("wdB_sb", [128, HT * DQ], BF16, B0)
    assert HT * DQ * 2 == DT * TC * 2

    xtm_buf = sb("xtm_buf", [128, 2 * D], F32, C0)
    wbA1 = sb("wbA1", [128, 2 * DT * 128], BF16, C0)
    wbA2 = sb("wbA2", [128, 2 * DT * 128], BF16, C0 + 8192)
    qc_buf = sb("qc_buf", [128, 2 * DT * BS], BF16, C0)
    xsq_buf = sb("xsq_buf", [128, 2 * D], F32, C0 + 16384)
    ktc_buf = sb("ktc_buf", [128, 2 * TPC * D], BF16, C0 + 16384)
    oint_buf = sb("oint_buf", [128, DT * BS], BF16, C0 + 16384)
    wvq_buf = sb("wvq_buf", [128, 2 * DT * EQ], BF16, C0 + 16384)
    xf_buf = sb("xf_buf", [128, 2 * TC], F32, C0 + 32768)
    vv_buf = sb("vv_buf", [128, 2 * TPC * D], BF16, C0 + 32768)
    vte_buf = sb("vte_buf", [128, 2 * TPC * E], BF16, C0 + 32768)
    oi4_sb = sb("oi4_sb", [128, DT * BS], BF16, C0 + 40960)
    bsrc_sb = sb("bsrc_sb", [1, 4 * TC], F32, C0)
    stat = sb("stat", [128, 18 * TT], F32, C0 + 49152)
    packT_sb = sb("packT_sb", [4 * TT, 128], F32, C0 + 50304)
    onesb = sb("onesb", [1, 128], F32, C0 + 50816)
    identf_sb = sb("identf_sb", [128, 128], F32, C0 + 51328)
    identb_sb = sb("identb_sb", [128, 128], BF16, C0 + 51840)
    mask_sb = sb("mask_sb", [128, TPC * BS], BF16, C0 + 52096)
    w_buf = sb("w_buf", [128, 2 * DT * 128], BF16, C0 + 53248)
    oi_sb = sb("oi_sb", [128, DT * BS], BF16, C0 + 53248)
    w_buf2 = sb("w_buf2", [128, 2 * DT * 128], BF16, C0 + 61440)
    xa_buf = sb("xa_buf", [128, 2 * 1024], F32, C0 + 61440)
    sT_sb = sb("sT_sb", [128, TPC * BS], BF16, C0 + 61440)
    strip_buf = sb("strip_buf", [128, 2 * TC], BF16, C0 + 69632)
    gld_buf = sb("gld_buf", [128, 8 * 512], BF16, C0 + 69632)
    oi3_sb = sb("oi3_sb", [128, ET * BS], BF16, C0 + 69632)
    assert 18 * TT * 4 <= 1152 and 4 * TT <= 128
    assert 2 * TPC * D * 2 <= 16384 and DT * BS * 2 <= 8192
    assert C0 + 77824 <= 229344, "sbuf overflow"

    pb = [nc.alloc_psum_tensor(f"pb{i}", [128, 512], F32).ap() for i in range(8)]

    guards = []

    def mk(name):
        g = nc.semaphore(name)
        h = g.__enter__()
        guards.append(g)
        return SemS(h)

    tC, vC, sC, gC = mk("tC"), mk("vC"), mk("sC"), mk("gC")
    p5aT, p5aS, p5aV = mk("p5aT"), mk("p5aS"), mk("p5aV")
    ccm, cco = mk("ccm"), mk("cco")
    barS = mk("bar")
    sX = Stream(mk, "sX", 4)
    tT = Stream(mk, "tT", 4)
    gW = Stream(mk, "gW", 4)
    sS = Stream(mk, "sS", 4)
    sL = Stream(mk, "sL", 4)
    oT = Stream(mk, "oT", 4)

    RG8 = [list(range(8))]
    RG4 = [[0, 1, 2, 3], [4, 5, 6, 7]]
    toks = {}
    misc = {}

    bg = nc.Block()
    block = bg.__enter__()

    def barrier():
        barS.n += 5
        tgt = barS.n
        for sec in (block.tensor, block.vector, block.scalar, block.gpsimd, block.sync):
            def _f(eng, tgt=tgt):
                eng.sem_inc(barS.h, 1)
                eng.wait_ge(barS.h, tgt)
            sec(_f)

    def p0():
        xtm_t = {}
        xf_t = {}
        ln_t = {}
        b_sC, b_vC, b_gC = sC.n, vC.n, gC.n

        @block.sync
        def _(sy):
            for nm, t, src in (("ones", onesb, ones_f), ("idf", identf_sb, ident_f),
                               ("idb", identb_sb, ident_b), ("mask", mask_sb, mask_c)):
                toks[nm] = sX.issue(sy, t[:, :], src[:, :])
            for j in range(TT):
                if j >= 2:
                    sy.wait_ge(sC.h, b_sC + j - 1)
                    sy.wait_ge(vC.h, b_vC + 2 * (j - 1))
                xtm_t[j] = sX.issue(sy, xtm_buf[:, (j % 2) * D:(j % 2 + 1) * D],
                                    x_tm[j * 128:(j + 1) * 128, :])

        @block.scalar
        def _(s):
            for j in range(TT):
                _wait(s, xtm_t[j])
                if j >= 2:
                    s.wait_ge(vC.h, b_vC + 2 * (j - 1))
                s.activation(out=xsq_buf[:, (j % 2) * D:(j % 2 + 1) * D],
                             in_=xtm_buf[:, (j % 2) * D:(j % 2 + 1) * D],
                             func=AF.Square).then_inc(sC.h, 1)
            s.wait_ge(vC.h, b_vC + 2 * TT + 4)
            s.activation(out=stat[:, 5 * TT:6 * TT], in_=stat[:, 3 * TT:4 * TT],
                         func=AF.Ln, bias=stat[:, 17 * TT:17 * TT + 1]).then_inc(sC.h, 1)
            s.wait_ge(sC.h, b_sC + TT + 1)
            s.activation(out=stat[:, 6 * TT:7 * TT], in_=stat[:, 5 * TT:6 * TT],
                         func=AF.Exp, scale=-0.5).then_inc(sC.h, 1)
            s.wait_ge(vC.h, b_vC + 2 * TT + 6)
            s.activation(out=stat[:, 8 * TT:9 * TT], in_=stat[:, 7 * TT:8 * TT],
                         func=AF.Ln, bias=stat[:, 17 * TT + 1:17 * TT + 2]).then_inc(sC.h, 1)
            s.wait_ge(sC.h, b_sC + TT + 3)
            s.activation(out=stat[:, 9 * TT:10 * TT], in_=stat[:, 8 * TT:9 * TT],
                         func=AF.Exp, scale=-0.5).then_inc(sC.h, 1)
        sC.n += TT + 4

        @block.vector
        def _(v):
            v.memset(stat[:, 17 * TT:17 * TT + 1], LN_EPS)
            v.memset(stat[:, 17 * TT + 1:17 * TT + 2], RMS_EPS)
            for j in range(TT):
                _wait(v, xtm_t[j])
                v.reduce_sum(stat[:, j:j + 1], xtm_buf[:, (j % 2) * D:(j % 2 + 1) * D],
                             axis=mybir.AxisListType.X).then_inc(vC.h, 1)
                v.wait_ge(sC.h, b_sC + j + 1)
                v.reduce_sum(stat[:, TT + j:TT + j + 1], xsq_buf[:, (j % 2) * D:(j % 2 + 1) * D],
                             axis=mybir.AxisListType.X).then_inc(vC.h, 1)
            ecnt = [2 * TT]

            def edge(ins):
                ecnt[0] += 1
                ins.then_inc(vC.h, 1)
                v.wait_ge(vC.h, b_vC + ecnt[0])

            sm, sq = stat[:, 0:TT], stat[:, TT:2 * TT]
            m, var, tmp = stat[:, 2 * TT:3 * TT], stat[:, 3 * TT:4 * TT], stat[:, 4 * TT:5 * TT]
            edge(v.tensor_scalar_mul(m, sm, 1.0 / D))
            edge(v.tensor_scalar_mul(var, sq, 1.0 / D))
            edge(v.tensor_mul(tmp, m, m))
            edge(v.tensor_sub(var, var, tmp))
            assert ecnt[0] == 2 * TT + 4
            r1 = stat[:, 6 * TT:7 * TT]
            u = stat[:, 7 * TT:8 * TT]
            v.wait_ge(sC.h, b_sC + TT + 2)
            edge(v.tensor_mul(u, var, r1))
            edge(v.tensor_mul(u, u, r1))
            v.wait_ge(sC.h, b_sC + TT + 4)
            r2s = stat[:, 9 * TT:10 * TT]
            r2 = stat[:, 10 * TT:11 * TT]
            mr1 = stat[:, 11 * TT:12 * TT]
            mr2 = stat[:, 12 * TT:13 * TT]
            edge(v.tensor_mul(r2, r1, r2s))
            edge(v.tensor_mul(mr1, m, r1))
            edge(v.tensor_mul(mr2, m, r2))
            edge(v.tensor_copy(out=stat[:, 13 * TT:14 * TT], in_=r1))
            edge(v.tensor_copy(out=stat[:, 14 * TT:15 * TT], in_=mr1))
            edge(v.tensor_copy(out=stat[:, 15 * TT:16 * TT], in_=r2))
            edge(v.tensor_copy(out=stat[:, 16 * TT:17 * TT], in_=mr2))
        vC.n += 2 * TT + 13

        b_tC = tC.n

        @block.tensor
        def _(t):
            t.wait_ge(vC.h, b_vC + 2 * TT + 13)
            _wait(t, toks["idf"])
            t.transpose(pb[4][0:4 * TT, 0:128], stat[:, 13 * TT:17 * TT],
                        identf_sb[:, :]).then_inc(tC.h, 1)
        tC.n += 1

        @block.vector
        def _(v):
            v.wait_ge(tC.h, b_tC + 1)
            v.tensor_copy(out=packT_sb[:, :], in_=pb[4][0:4 * TT, 0:128]).then_inc(vC.h, 1)
        vC.n += 1

        @block.scalar
        def _(s):
            s.wait_ge(vC.h, vC.n)
            toks["vecs"] = sS.issue(s, vecs_scr[:, :], packT_sb[:, :])

        @block.sync
        def _(sy):
            _wait(sy, toks["vecs"])
            toks["bsrc"] = sX.issue(
                sy, bsrc_sb[0:1, :], vecs_scr.rearrange("a p -> (a p)").unsqueeze(0))

        nch = TC // 512
        b_tC2 = tC.n
        b_vC2 = vC.n

        @block.tensor
        def _(t):
            _wait(t, toks["bsrc"])
            _wait(t, toks["ones"])
            idx = 0
            for vv in range(4):
                for ch in range(nch):
                    if idx >= 4:
                        t.wait_ge(vC.h, b_vC2 + idx - 3)
                    mmi = t.matmul(pb[idx % 4][:, :], onesb[0:1, :],
                             bsrc_sb[0:1, vv * TC + ch * 512: vv * TC + (ch + 1) * 512],
                             start=True, stop=True).then_inc(tC.h, 1)
                    idx += 1
        tC.n += 4 * nch

        @block.vector
        def _(v):
            idx = 0
            for vv in range(4):
                for ch in range(nch):
                    v.wait_ge(tC.h, b_tC2 + idx + 1)
                    v.tensor_copy(out=bcast[:, vv * TC + ch * 512: vv * TC + (ch + 1) * 512],
                                  in_=pb[idx % 4][:, :]).then_inc(vC.h, 1)
                    idx += 1
        vC.n += 4 * nch

        b_vC3 = vC.n
        b_gC3 = gC.n

        @block.sync
        def _(sy):
            for d in range(DT):
                if d >= 2:
                    sy.wait_ge(vC.h, b_vC3 + 2 * (d - 1))
                    sy.wait_ge(gC.h, b_gC3 + 2 * (d - 1))
                xf_t[d] = sX.issue(sy, xf_buf[:, (d % 2) * TC:(d % 2 + 1) * TC],
                                   x_fm[d * 128:(d + 1) * 128, :])

        @block.gpsimd
        def _(g):
            g.wait_ge(vC.h, b_vC3)
            for d in range(DT):
                _wait(g, xf_t[d])
                xf = xf_buf[:, (d % 2) * TC:(d % 2 + 1) * TC]
                g.tensor_mul(tmp_g[:, :], xf, bcast[:, 2 * TC:3 * TC]).then_inc(gC.h, 1)
                g.wait_ge(gC.h, b_gC3 + 2 * d + 1)
                g.tensor_tensor(out=rn_sb[:, d * TC:(d + 1) * TC],
                                in0=tmp_g[:, :], in1=bcast[:, 3 * TC:4 * TC],
                                op=ALU.subtract).then_inc(gC.h, 1)
        gC.n += 2 * DT

        @block.scalar
        def _(s):
            for d in range(DT):
                s.wait_ge(vC.h, b_vC3 + 2 * (d + 1))
                ln_t[d] = sS.issue(s, ln_scr[d, :, :],
                                   lnt_buf[:, (d % 2) * TC:(d % 2 + 1) * TC])
            _wait_all(s, sS.totals())

        @block.vector
        def _(v):
            for d in range(DT):
                _wait(v, xf_t[d])
                if d >= 2:
                    _wait(v, ln_t[d - 2])
                xf = xf_buf[:, (d % 2) * TC:(d % 2 + 1) * TC]
                v.tensor_mul(tmp_v[:, :], xf, bcast[:, 0:TC]).then_inc(vC.h, 1)
                v.wait_ge(vC.h, b_vC3 + 2 * d + 1)
                v.tensor_tensor(out=lnt_buf[:, (d % 2) * TC:(d % 2 + 1) * TC],
                                in0=tmp_v[:, :], in1=bcast[:, TC:2 * TC],
                                op=ALU.subtract).then_inc(vC.h, 1)
        vC.n += 2 * DT


    pc = {}
    pcq = []

    def _fill_pcq():
        for i in range(16):
            pcq.append(("wgate", i, wgate_bf[4*i:4*i+4, :, :], wgate_b[4*i:4*i+4, :, :]))
            pcq.append(("wup", i, wup_bf[4*i:4*i+4, :, :], wup_b[4*i:4*i+4, :, :]))
        for i in range(4):
            pcq.append(("wp", i, wp_bf[4*i:4*i+4, :, :], wp_t[4*i:4*i+4, :, :]))
        for q in range(4):
            for i in range(4):
                pcq.append(("wdown", 4*q + i,
                            wdown_bf[q, :, i*8192:(i+1)*8192],
                            wdown_b[q, :, i*8192:(i+1)*8192]))

    def emit_piece(g, n=1):
        for _ in range(n):
            if not pcq:
                return
            name, i, dst, srcap = pcq.pop(0)
            pc.setdefault(name, {})[i] = gW.issue(g, dst, srcap)

    def proj_fm(wt_ap, act_fn, kind, pcname, dest=None):
        nch = TC // 512
        b_tC, b_sC, b_vC, b_gC = tC.n, sC.n, vC.n, gC.n
        per_o_t = nch + (TT if kind == "strips_ktm" else 0)
        w_t = {}
        mm_mil = {}
        tp_mil = {}

        @block.gpsimd
        def _(g):
            for o in range(DT):
                if o < 2:
                    g.wait_ge(tC.h, b_tC)
                else:
                    g.wait_ge(tC.h, b_tC + per_o_t * (o - 1))
                w_t[o] = gW.issue(
                    g,
                    w_buf[:, (o % 2) * DT * 128:(o % 2 + 1) * DT * 128],
                    wt_ap[o, :, :])
                emit_piece(g)

        @block.tensor
        def _(t):
            mil = b_tC

            def emit_tp(s_, j):
                nonlocal mil
                lin = s_ * TT + j
                if lin >= 4:
                    t.wait_ge(vC.h, b_vC + lin - 3)
                t.transpose(pb[4 + (j % 4)].bitcast(BF16)[:, 0:128],
                            strip_buf[:, (s_ % 2) * TC + j * 128:(s_ % 2) * TC + (j + 1) * 128],
                            identb_sb[:, :]).then_inc(tC.h, 1)
                mil += 1
                tp_mil[(s_, j)] = mil

            for o in range(DT):
                _wait(t, w_t[o])
                if o == 0:
                    t.wait_ge(sC.h, b_sC)
                    t.wait_ge(gC.h, b_gC)
                for tci in range(nch):
                    if o >= 1:
                        t.wait_ge(sC.h, b_sC + nch * (o - 1) + tci + 1)
                    for a in range(DT):
                        mmi = t.matmul(pb[tci % 4][:, :],
                                 w_buf[:, (o % 2) * DT * 128 + a * 128:(o % 2) * DT * 128 + (a + 1) * 128],
                                 rn_sb[:, a * TC + tci * 512:a * TC + (tci + 1) * 512],
                                 start=(a == 0), stop=(a == DT - 1))
                    mmi.then_inc(tC.h, 1)
                    mil += 1
                    mm_mil[(o, tci)] = mil
                if kind == "strips_ktm" and o >= 1:
                    t.wait_ge(sC.h, b_sC + nch * o)
                    for j in range(TT):
                        emit_tp(o - 1, j)
            if kind == "strips_ktm":
                t.wait_ge(sC.h, b_sC + nch * DT)
                for j in range(TT):
                    emit_tp(DT - 1, j)
        tC.n += nch * DT + (TT * DT if kind == "strips_ktm" else 0)

        st_toks = {}

        @block.scalar
        def _(s):
            _wait_all(s, sS.totals())
            for o in range(DT):
                if kind == "strips_ktm" and o >= 2:
                    s.wait_ge(tC.h, tp_mil[(o - 2, TT - 1)])
                if kind != "assemble" and o >= 2:
                    _wait(s, st_toks[o - 2])
                for tci in range(nch):
                    s.wait_ge(tC.h, mm_mil[(o, tci)])
                    if kind == "assemble":
                        dst = ktm_sb[:, o * TC + tci * 512:o * TC + (tci + 1) * 512]
                    else:
                        dst = strip_buf[:, (o % 2) * TC + tci * 512:(o % 2) * TC + (tci + 1) * 512]
                    s.activation(out=dst, in_=pb[tci % 4][:, :], func=act_fn).then_inc(sC.h, 1)
                if kind != "assemble":
                    s.wait_ge(sC.h, b_sC + nch * (o + 1))
                    st_toks[o] = sS.issue(s, dest[o, :, :], strip_buf[:, (o % 2) * TC:(o % 2 + 1) * TC])
        sC.n += nch * DT

        if kind == "strips_ktm":
            @block.vector
            def _(v):
                for s_ in range(DT):
                    for j in range(TT):
                        v.wait_ge(tC.h, tp_mil[(s_, j)])
                        v.tensor_copy(out=ktm_sb[:, j * D + s_ * 128:j * D + (s_ + 1) * 128],
                                      in_=pb[4 + (j % 4)].bitcast(BF16)[:, 0:128]).then_inc(vC.h, 1)
            vC.n += DT * TT

    def p1():
        proj_fm(wk_t, AF.Silu, "strips_ktm", "wk", dest=kfm_scr)
        b_vC = vC.n

        @block.scalar
        def _(s):
            s.wait_ge(vC.h, b_vC)
            for l in range(NCC):
                sS.issue(s, ktm_pub_b[l, :, :, :].rearrange("t p d -> p t d"),
                         ktm_sb.rearrange("p (t d) -> p t d", t=TT)[:, TPC * l:TPC * (l + 1), :])
            _wait_all(s, sS.totals())

        b_tC = tC.n
        b_sC = sC.n
        wv_tok = {}
        vst_t = {}

        @block.gpsimd
        def _(g):
            for ev in range(4):
                if ev >= 2:
                    g.wait_ge(tC.h, b_tC + TT * (ev - 1))
                wv_tok[ev] = gW.issue(g, wvq_buf[:, (ev % 2) * DT * EQ:(ev % 2 + 1) * DT * EQ],
                                      wv_t[ev, :, :])
                emit_piece(g)

        @block.tensor
        def _(t):
            for ev in range(4):
                _wait(t, wv_tok[ev])
                for j in range(TT):
                    m = ev * TT + j
                    if m == 0:
                        t.wait_ge(sC.h, b_sC)
                    elif m >= 4:
                        t.wait_ge(sC.h, b_sC + m - 3)
                    for a in range(DT):
                        mmi = t.matmul(pb[m % 4][:, 0:EQ],
                                 rn_sb[:, a * TC + j * 128:a * TC + (j + 1) * 128],
                                 wvq_buf[:, (ev % 2) * DT * EQ + a * EQ:(ev % 2) * DT * EQ + (a + 1) * EQ],
                                 start=(a == 0), stop=(a == DT - 1))
                    mmi.then_inc(tC.h, 1)
        tC.n += 4 * TT

        @block.scalar
        def _(s):
            _wait_all(s, sS.totals())
            for ev in range(4):
                for j in range(TT):
                    m = ev * TT + j
                    if m >= 4:
                        _wait(s, vst_t[m - 4])
                    s.wait_ge(tC.h, b_tC + m + 1)
                    s.activation(out=strip_buf[:, (m % 4) * EQ:(m % 4 + 1) * EQ],
                                 in_=pb[m % 4][:, 0:EQ], func=AF.Silu).then_inc(sC.h, 1)
                    s.wait_ge(sC.h, b_sC + m + 1)
                    vst_t[m] = sS.issue(s, v_pub_b[j // TPC, j % TPC, :, ev * EQ:(ev + 1) * EQ],
                                        strip_buf[:, (m % 4) * EQ:(m % 4 + 1) * EQ])
        sC.n += 4 * TT

        proj_fm(wg_t, AF.Sigmoid, "strips", "wg", dest=g_scr)

        proj_fm(wq_t, AF.Silu, "assemble", "wq")
        b_sC3 = sC.n

        @block.scalar
        def _(s):
            s.wait_ge(sC.h, b_sC3)
            for l in range(NCC):
                sS.issue(s, q_pub_b[l, :, :, :].rearrange("a p t -> p a t"),
                         ktm_sb.rearrange("p (a t) -> p a t", a=DT)[:, :, l * BS:(l + 1) * BS])
            _wait_all(s, sS.totals())

        misc["g_end_tC"] = tC.n

        @block.gpsimd
        def _(g):
            emit_piece(g, n=len(pcq))
            _wait_all(g, sS.totals())
            g.wait_ge(tC.h, tC.n + (TPC + (DT + 7) // 8) * NCC)
            g.collective_compute("AllGather", ALU.bypass, replica_groups=RG4,
                                 ins=[mg_pub2d.opt()], outs=[mg_ag2d.opt()]).then_inc(ccm.h, 1)
        ccm.n += 1

    def p25():
        b_tC, b_vC = tC.n, vC.n
        nr = (DT + 7) // 8
        per_t = TPC + nr
        per_v = TPC + nr
        ld_t = {}
        oist_t = {}

        @block.scalar
        def _(s):
            _wait_all(s, sS.totals())
            for l in range(NCC):
                if l >= 2:
                    s.wait_ge(tC.h, b_tC + per_t * (l - 1))
                h0 = (l % 2) * TPC * D
                t2 = sL.issue(s, ktc_buf[:, h0:h0 + DT * BS].rearrange("p (a f) -> p a f", a=DT),
                              kfm_scr[:, :, l * BS:(l + 1) * BS].rearrange("a p f -> p a f"))
                t3 = sL.issue(s, vv_buf[:, h0:h0 + TPC * D].rearrange("p (t d) -> p t d", t=TPC),
                              v_pub_b[l, :, :, :].rearrange("t p d -> p t d"))
                ld_t[l] = [t2, t3]
                if l >= 1:
                    s.wait_ge(vC.h, b_vC + l * per_v)
                    oist_t[l - 1] = sS.issue(s, oi_scr[l - 1, :, :, :].rearrange("a p f -> p a f"),
                                             oi_sb.rearrange("p (a f) -> p a f", a=DT))
            s.wait_ge(vC.h, b_vC + NCC * per_v)
            oist_t[NCC - 1] = sS.issue(s, oi_scr[NCC - 1, :, :, :].rearrange("a p f -> p a f"),
                                       oi_sb.rearrange("p (a f) -> p a f", a=DT))
            _wait_all(s, sS.totals())

        @block.tensor
        def _(t):
            t.wait_ge(sC.h, sC.n)
            cnt = 0
            for l in range(NCC):
                _wait_all(t, ld_t[l])
                h0 = (l % 2) * TPC * D
                for tp in range(TPC):
                    t.wait_ge(vC.h, b_vC + cnt)
                    for a in range(DT):
                        mmi = t.matmul(pb[6][:, 0:BS],
                                 ktc_buf[:, h0 + a * BS + tp * 128:h0 + a * BS + (tp + 1) * 128],
                                 ktm_sb[:, a * TC + l * BS:a * TC + (l + 1) * BS],
                                 start=(a == 0), stop=(a == DT - 1))
                    mmi.then_inc(tC.h, 1)
                    cnt += 1
                for r_ in range(nr):
                    t.wait_ge(vC.h, b_vC + cnt)
                    for a in range(8 * r_, min(8 * r_ + 8, DT)):
                        for tp in range(TPC):
                            mmi = t.matmul(pb[a % 8][:, 0:BS],
                                     vv_buf[:, h0 + tp * D + a * 128:h0 + tp * D + (a + 1) * 128],
                                     sT_sb[:, tp * BS:(tp + 1) * BS],
                                     start=(tp == 0), stop=(tp == TPC - 1))
                    mmi.then_inc(tC.h, 1)
                    cnt += 1
        tC.n += per_t * NCC

        @block.vector
        def _(v):
            _wait(v, toks["mask"])
            for l in range(NCC):
                for tp in range(TPC):
                    v.wait_ge(tC.h, b_tC + l * per_t + tp + 1)
                    v.tensor_tensor(out=sT_sb[:, tp * BS:(tp + 1) * BS],
                                    in0=pb[6][:, 0:BS],
                                    in1=mask_sb[:, tp * BS:(tp + 1) * BS],
                                    op=ALU.mult).then_inc(vC.h, 1)
                for r_ in range(nr):
                    v.wait_ge(tC.h, b_tC + l * per_t + TPC + r_ + 1)
                    if r_ == 0 and l >= 1:
                        _wait(v, oist_t[l - 1])
                    ins = None
                    for a in range(8 * r_, min(8 * r_ + 8, DT)):
                        ins = v.tensor_copy(out=oi_sb[:, a * BS:(a + 1) * BS],
                                            in_=pb[a % 8][:, 0:BS])
                    ins.then_inc(vC.h, 1)
        vC.n += per_v * NCC
        misc["p25_end_tC"] = tC.n
        misc["p25_oist"] = sS.totals()

        @block.sync
        def _(sy):
            sy.wait_ge(tC.h, misc["g_end_tC"])
            lnh_toks = []
            for qa in range(4):
                lnh_toks.append(
                    tT.issue(sy, lnh_sb[:, qa * 4 * TC:(qa + 1) * 4 * TC].rearrange("p (a f) -> p a f", a=4),
                             ln_scr[qa * 4:(qa + 1) * 4, :, :].rearrange("a p f -> p a f")))
            misc["lnh_toks"] = lnh_toks

    def group_copies():
        @block.gpsimd
        def _(g):
            g.wait_ge(ccm.h, ccm.n)
            s0 = gW.sems[0]
            KR = NCC * TPC * 128
            for arm in g.switch_core_id(n=128):
                ej = arm.logical % 4
                for r in range(4):
                    g.dma_start(out=v_loc[r * KR:(r + 1) * KR, :],
                                in_=v_ag_v[r][:, ej * E:(ej + 1) * E]).then_inc(s0.h, 16)
            s0.n += 64
        misc["vloc_tok"] = (gW.sems[0].h, gW.sems[0].n)

    def p3():
        @block.vector
        def _(v):
            v.memset(kv_sb[:, :], 0.0).then_inc(vC.h, 1)
        vC.n += 1

        b_sC, b_tC, b_vC = sC.n, tC.n, vC.n
        per_t = ET + 4
        per_v = 5
        ld_t = {}
        ost_t = {}

        @block.sync
        def _(sy):
            sy.wait_ge(ccm.h, ccm.n)
            _wait(sy, misc["vloc_tok"])
            sy.wait_ge(tC.h, misc["p25_end_tC"])
            sy.wait_ge(p5aT.h, p5aT.n)
            for ch in range(NCB):
                if ch >= 2:
                    sy.wait_ge(tC.h, b_tC + per_t * (ch - 1))
                r, l = ch // NCC, ch % NCC
                h0 = (ch % 2) * TPC * D
                t1 = sX.issue(sy, qc_buf[:, (ch % 2) * DT * BS:(ch % 2 + 1) * DT * BS].rearrange("p (a f) -> p a f", a=DT),
                              q_ag_v[r][l, :, :, :].rearrange("a p f -> p a f"))
                t2 = sX.issue(sy, ktc_buf[:, h0:h0 + TPC * D].rearrange("p (t d) -> p t d", t=TPC),
                              ktm_ag_v[r][l, :, :, :].rearrange("t p d -> p t d"))
                t3 = sX.issue(sy, vte_buf[:, (ch % 2) * TPC * E:(ch % 2 + 1) * TPC * E].rearrange("p (t e) -> p t e", t=TPC),
                              v_loc_b[r, l, :, :, :].rearrange("t p e -> p t e"))
                ld_t[ch] = [t1, t2, t3]

        @block.scalar
        def _(s):
            for ch in range(NCB):
                for qd in range(4):
                    if ch >= 1:
                        s.wait_ge(vC.h, b_vC + per_v * (ch - 1) + qd + 2)
                    elif qd == 0:
                        s.wait_ge(vC.h, b_vC)
                    s.copy(out=kv_bf[:, qd * 4 * E:(qd + 1) * 4 * E],
                           in_=kv_sb[:, qd * 4 * E:(qd + 1) * 4 * E]).then_inc(sC.h, 1)
                if ch >= 1:
                    ost_t[ch - 1] = sS.issue(s, oint_pub_b[ch - 1, :, :, :].rearrange("e p f -> p e f"),
                                             oi3_sb[:, :].rearrange("p (e f) -> p e f", e=ET))
            s.wait_ge(vC.h, b_vC + per_v * NCB)
            ost_t[NCB - 1] = sS.issue(s, oint_pub_b[NCB - 1, :, :, :].rearrange("e p f -> p e f"),
                                      oi3_sb[:, :].rearrange("p (e f) -> p e f", e=ET))
            _wait_all(s, sS.totals())
        sC.n += 4 * NCB

        @block.tensor
        def _(t):
            t.wait_ge(p5aS.h, p5aS.n)
            t.wait_ge(p5aV.h, p5aV.n)
            for ch in range(NCB):
                _wait_all(t, ld_t[ch])
                if ch >= 1:
                    t.wait_ge(vC.h, b_vC + per_v * ch)
                qc0 = (ch % 2) * DT * BS
                h0 = (ch % 2) * TPC * D
                vt0 = (ch % 2) * TPC * E
                for a in range(DT):
                    if a % 4 == 0:
                        t.wait_ge(sC.h, b_sC + 4 * ch + a // 4 + 1)
                    for et in range(ET):
                        mmi = t.matmul(pb[[6, 7, 0, 1][et]][:, 0:BS],
                                 kv_bf[:, a * E + et * 128:a * E + (et + 1) * 128],
                                 qc_buf[:, qc0 + a * BS:qc0 + (a + 1) * BS],
                                 start=(a == 0), stop=(a == DT - 1))
                    if a % 4 == 3:
                        mmi.then_inc(tC.h, 1)
                nq = max(1, DT // 4)
                for qd in range(4):
                    grp = (2, 3, 4, 5) if qd % 2 == 0 else (6, 7, 0, 1)
                    if qd == 0:
                        if ch >= 1:
                            t.wait_ge(vC.h, b_vC + per_v * (ch - 1) + 4)
                    elif qd == 1:
                        t.wait_ge(vC.h, b_vC + per_v * ch + 1)
                    else:
                        t.wait_ge(vC.h, b_vC + per_v * ch + qd)
                    for ai in range(nq):
                        a = qd * nq + ai
                        if a >= DT:
                            continue
                        for tp in range(TPC):
                            mmi = t.matmul(pb[grp[a % 4]][:, 0:E],
                                     ktc_buf[:, h0 + tp * D + a * 128:h0 + tp * D + (a + 1) * 128],
                                     vte_buf[:, vt0 + tp * E:vt0 + (tp + 1) * E],
                                     start=(tp == 0), stop=(tp == TPC - 1))
                    mmi.then_inc(tC.h, 1)
        tC.n += per_t * NCB

        @block.vector
        def _(v):
            _wait_all(v, misc["p25_oist"])
            for ch in range(NCB):
                v.wait_ge(tC.h, b_tC + ch * per_t + ET)
                if ch >= 1:
                    _wait(v, ost_t[ch - 1])
                ins = None
                for et in range(ET):
                    ins = v.tensor_copy(out=oi3_sb[:, et * BS:(et + 1) * BS],
                                        in_=pb[[6, 7, 0, 1][et]][:, 0:BS])
                ins.then_inc(vC.h, 1)
                nq = max(1, DT // 4)
                for qd in range(4):
                    grp = (2, 3, 4, 5) if qd % 2 == 0 else (6, 7, 0, 1)
                    v.wait_ge(tC.h, b_tC + ch * per_t + ET + qd + 1)
                    for ai in range(nq):
                        a = qd * nq + ai
                        if a >= DT:
                            continue
                        ins = v.tensor_tensor(out=kv_sb[:, a * E:(a + 1) * E],
                                              in0=kv_sb[:, a * E:(a + 1) * E],
                                              in1=pb[grp[a % 4]][:, 0:E],
                                              op=ALU.add)
                    ins.then_inc(vC.h, 1)
        vC.n += per_v * NCB

        misc["p3_end_tC"] = tC.n
        misc["p3_sS"] = sS.totals()

    def oint_gather():
        @block.gpsimd
        def _(g):
            _wait_all(g, misc["p3_sS"])
            g.collective_compute("AllGather", ALU.bypass, replica_groups=RG4,
                                 ins=[oint_pub.opt()], outs=[oint_ag.opt()]).then_inc(cco.h, 1)
        cco.n += 1

    def oint_copy():
        @block.gpsimd
        def _(g):
            g.wait_ge(cco.h, cco.n)
            s1 = gW.sems[1]
            for arm in g.switch_core_id(n=128):
                q4 = arm.logical % 4
                for r in range(4):
                    base = r * NCB * E + q4 * NCC * E
                    g.dma_start(out=oint_loc[r * NCC * E:(r + 1) * NCC * E, :],
                                in_=oint_ag[base:base + NCC * E, :]).then_inc(s1.h, 16)
            s1.n += 64
        misc["oint_tok"] = (gW.sems[1].h, gW.sems[1].n)

    def emit_p5a(h0, h1):
        ntc = TC // 512
        w_t = misc.setdefault("p5a_wt", {})
        gst_t = misc.setdefault("p5a_gst", {})
        pre_sS = sS.totals()
        wg1, wg2 = (wbA1, wbA2) if h0 == 0 else (w_buf, w_buf2)

        @block.sync
        def _(sy):
            if h0 > 0:
                _wait_all(sy, misc["p25_oist"])
            for h in range(h0, h1):
                if h >= 2:
                    sy.wait_ge(p5aT.h, 2 * ntc * (h - 1))
                elif h0 == 0:
                    sy.wait_ge(tC.h, misc["p25_end_tC"])
                _wait(sy, pc["wgate"][h // 4])
                _wait(sy, pc["wup"][h // 4])
                w_t[(h, 0)] = sX.issue(
                    sy, wg1[:, (h % 2) * DT * 128:(h % 2 + 1) * DT * 128],
                    wgate_bf[h, :, :])
                w_t[(h, 1)] = sX.issue(
                    sy, wg2[:, (h % 2) * DT * 128:(h % 2 + 1) * DT * 128],
                    wup_bf[h, :, :])

        @block.tensor
        def _(t):
            if h0 == 0:
                _wait_all(t, misc["lnh_toks"])
            t.wait_ge(sC.h, sC.n)
            t.wait_ge(vC.h, vC.n)
            for h in range(h0, h1):
                _wait(t, w_t[(h, 0)])
                _wait(t, w_t[(h, 1)])
                for tci in range(ntc):
                    if h >= 1:
                        t.wait_ge(p5aS.h, ntc * (h - 1) + tci + 1)
                    for a in range(DT):
                        mmi = t.matmul(pb[tci][:, :],
                                 wg1[:, (h % 2) * DT * 128 + a * 128:(h % 2) * DT * 128 + (a + 1) * 128],
                                 lnh_sb[:, a * TC + tci * 512:a * TC + (tci + 1) * 512],
                                 start=(a == 0), stop=(a == DT - 1))
                    mmi.then_inc(p5aT.h, 1)
                for tci in range(ntc):
                    if h >= 1:
                        t.wait_ge(p5aV.h, ntc * (h - 1) + tci + 1)
                    for a in range(DT):
                        mmi = t.matmul(pb[4 + tci][:, :],
                                 wg2[:, (h % 2) * DT * 128 + a * 128:(h % 2) * DT * 128 + (a + 1) * 128],
                                 lnh_sb[:, a * TC + tci * 512:a * TC + (tci + 1) * 512],
                                 start=(a == 0), stop=(a == DT - 1))
                    mmi.then_inc(p5aT.h, 1)
        p5aT.n += 2 * ntc * (h1 - h0)

        @block.scalar
        def _(s):
            for h in range(h0, h1):
                if h >= 2:
                    s.wait_ge(p5aV.h, ntc * (h - 1))
                for tci in range(ntc):
                    s.wait_ge(p5aT.h, h * 2 * ntc + tci + 1)
                    s.activation(out=silB_sb[:, (h % 2) * TC + tci * 512:(h % 2) * TC + (tci + 1) * 512],
                                 in_=pb[tci][:, :], func=AF.Silu).then_inc(p5aS.h, 1)
                if h >= 1:
                    s.wait_ge(p5aV.h, ntc * h)
                    gst_t[h - 1] = sS.issue(s, gated_scr[h - 1, :, :],
                                            gatB_sb[:, ((h - 1) % 4) * TC:((h - 1) % 4) * TC + TC])
            if h1 == HT:
                s.wait_ge(p5aV.h, ntc * HT)
                gst_t[HT - 1] = sS.issue(s, gated_scr[HT - 1, :, :],
                                         gatB_sb[:, ((HT - 1) % 4) * TC:((HT - 1) % 4) * TC + TC])
        p5aS.n += ntc * (h1 - h0)

        @block.vector
        def _(v):
            if h0 == 0:
                _wait_all(v, pre_sS)
            for h in range(h0, h1):
                if h >= 4:
                    _wait(v, gst_t[h - 4])
                for tci in range(ntc):
                    v.wait_ge(p5aT.h, h * 2 * ntc + ntc + tci + 1)
                    v.wait_ge(p5aS.h, ntc * h + tci + 1)
                    v.tensor_tensor(out=gatB_sb[:, (h % 4) * TC + tci * 512:(h % 4) * TC + (tci + 1) * 512],
                                    in0=silB_sb[:, (h % 2) * TC + tci * 512:(h % 2) * TC + (tci + 1) * 512],
                                    in1=pb[4 + tci][:, :], op=ALU.mult).then_inc(p5aV.h, 1)
        p5aV.n += ntc * (h1 - h0)

    def p4():
        nhc = NCC // 2
        ntc = TH // 512
        b_sC_em = sC.n
        for th in range(2):
            b_tC0 = tC.n
            b_vC = vC.n
            ld_t = {}
            x_t = {}
            xst_t = {}
            w_t = {}

            @block.sync
            def _(sy):
                if th == 0:
                    _wait_all(sy, misc["p3_sS"])
                    _wait(sy, misc["oint_tok"])
                    sy.wait_ge(p5aT.h, p5aT.n)
                else:
                    sy.wait_ge(tC.h, b_tC0)
                ld_t["gh"] = sX.issue(sy, gh_sb[:, :].rearrange("p (a f) -> p a f", a=DT),
                                      g_scr[:, :, th * TH:(th + 1) * TH].rearrange("a p f -> p a f"))
                for li in range(nhc):
                    l = th * nhc + li
                    if li >= 1:
                        sy.wait_ge(vC.h, b_vC + 2 * li)
                    tokens = [sX.issue(sy, oi4_sb[:, :].rearrange("p (a f) -> p a f", a=DT),
                                       oi_scr[l, :, :, :].rearrange("a p f -> p a f"))]
                    for r in range(4):
                        tokens.append(sX.issue(
                            sy, oint_buf[:, r * ET * BS:(r + 1) * ET * BS].rearrange("p (e f) -> p e f", e=ET),
                            oint_loc_b[r, l, :, :, :].rearrange("e p f -> p e f")))
                    ld_t[li] = tokens

            @block.vector
            def _(v):
                if th == 0:
                    v.wait_ge(p5aT.h, p5aT.n)
                else:
                    v.wait_ge(tC.h, b_tC0)
                _wait(v, ld_t["gh"])
                for li in range(nhc):
                    _wait_all(v, ld_t[li])
                    v.tensor_add(oi4_sb[:, :], oi4_sb[:, :], oint_buf[:, :]).then_inc(vC.h, 1)
                    v.wait_ge(vC.h, b_vC + 2 * li + 1)
                    v.tensor_tensor(
                        out=og_sb.rearrange("p (a f) -> p a f", a=DT)[:, :, li * BS:(li + 1) * BS],
                        in0=oi4_sb.rearrange("p (a f) -> p a f", a=DT),
                        in1=gh_sb.rearrange("p (a f) -> p a f", a=DT)[:, :, li * BS:(li + 1) * BS],
                        op=ALU.mult).then_inc(vC.h, 1)
            vC.n += 2 * nhc

            b_tC, b_vC2 = tC.n, vC.n

            @block.scalar
            def _(s):
                if th == 0:
                    _wait_all(s, sS.totals())
                    s.wait_ge(tC.h, misc["p3_end_tC"])
                    _wait_all(s, [pc["wdown"][i] for i in range(4)])
                    misc["wd_q"] = {0: sL.issue(s, wdB_sb[:, :], wdown_bf[0, :, :])}
                for do in range(DT):
                    if do < 2:
                        if th == 0:
                            s.wait_ge(p5aT.h, p5aT.n)
                        else:
                            s.wait_ge(tC.h, b_tC)
                    else:
                        s.wait_ge(tC.h, b_tC + ntc * (do - 1))
                    _wait(s, pc["wp"][do // 4])
                    w_t[do] = sL.issue(
                        s, w_buf[:, (do % 2) * DT * 128:(do % 2 + 1) * DT * 128],
                        wp_bf[do, :, :])

            @block.sync
            def _(sy):
                for do in range(DT):
                    if do >= 4:
                        sy.wait_ge(vC.h, b_vC2 + ntc * (do - 3))
                    x_t[do] = sX.issue(sy, xf_buf[:, (do % 4) * TH:(do % 4 + 1) * TH],
                                       x_fm[do * 128:(do + 1) * 128, th * TH:(th + 1) * TH])
                    if do >= 1:
                        sy.wait_ge(vC.h, b_vC2 + ntc * do)
                        xst_t[do - 1] = tT.issue(sy, xattn_scr[do - 1, :, th * TH:(th + 1) * TH],
                                                 xa_buf[:, ((do - 1) % 2) * TH:((do - 1) % 2) * TH + TH])
                sy.wait_ge(vC.h, b_vC2 + ntc * DT)
                xst_t[DT - 1] = tT.issue(sy, xattn_scr[DT - 1, :, th * TH:(th + 1) * TH],
                                         xa_buf[:, ((DT - 1) % 2) * TH:((DT - 1) % 2) * TH + TH])
                _wait_all(sy, tT.totals())

            @block.tensor
            def _(t):
                t.wait_ge(vC.h, b_vC + 2 * nhc)
                for do in range(DT):
                    _wait(t, w_t[do])
                    for tci in range(ntc):
                        if do == 0 and tci == 0 and th == 0:
                            t.wait_ge(p5aS.h, p5aS.n)
                        if do >= 2:
                            t.wait_ge(vC.h, b_vC2 + ntc * (do - 2) + tci + 1)
                        for a in range(DT):
                            mmi = t.matmul(pb[(do % 2) * 2 + tci][:, :],
                                     w_buf[:, (do % 2) * DT * 128 + a * 128:(do % 2) * DT * 128 + (a + 1) * 128],
                                     og_sb[:, a * TH + tci * 512:a * TH + (tci + 1) * 512],
                                     start=(a == 0), stop=(a == DT - 1))
                        mmi.then_inc(tC.h, 1)
            tC.n += ntc * DT

            @block.vector
            def _(v):
                for do in range(DT):
                    _wait(v, x_t[do])
                    if do >= 2:
                        _wait(v, xst_t[do - 2])
                    for tci in range(ntc):
                        v.wait_ge(tC.h, b_tC + ntc * do + tci + 1)
                        v.tensor_tensor(out=xa_buf[:, (do % 2) * TH + tci * 512:(do % 2) * TH + (tci + 1) * 512],
                                        in0=pb[(do % 2) * 2 + tci][:, :],
                                        in1=xf_buf[:, (do % 4) * TH + tci * 512:(do % 4) * TH + (tci + 1) * 512],
                                        op=ALU.add).then_inc(vC.h, 1)
            vC.n += ntc * DT
        misc["p4_end_tC"] = tC.n
        misc["p4_tT"] = tT.totals()

    def p5b():
        nq = 4
        ndt = DQ // 128
        ntcf = TC // 512
        NCI = nq * ntcf
        b_tC, b_vC = tC.n, vC.n
        gld_t = {}
        xld_t = {}
        ost_t = {}
        wd_q = misc["wd_q"]
        wdq_sb = [wdB_sb, wdA_sb, wdB_sb, wdA_sb]

        @block.scalar
        def _(s):
            s.wait_ge(tC.h, misc["p4_end_tC"])
            _wait_all(s, [pc["wdown"][4 + i] for i in range(4)])
            wd_q[1] = sL.issue(s, wdA_sb[:, :], wdown_bf[1, :, :])
            s.wait_ge(p5aV.h, p5aV.n)
            _wait_all(s, misc["p3_sS"])
            for ci in range(NCI):
                q, tci = divmod(ci, ntcf)
                if ci == ntcf:
                    s.wait_ge(tC.h, b_tC + ntcf * HT)
                    _wait_all(s, [pc["wdown"][8 + i] for i in range(4)])
                    wd_q[2] = sL.issue(s, wdB_sb[:, :], wdown_bf[2, :, :])
                elif ci == 2 * ntcf:
                    s.wait_ge(tC.h, b_tC + 2 * ntcf * HT)
                    _wait_all(s, [pc["wdown"][12 + i] for i in range(4)])
                    wd_q[3] = sL.issue(s, wdA_sb[:, :], wdown_bf[3, :, :])
                for hp in range(HT // 2):
                    pj = ci * (HT // 2) + hp
                    h0 = 2 * hp
                    if pj >= 4:
                        s.wait_ge(tC.h, b_tC + 2 * pj - 6)
                    _wait(s, misc["p5a_gst"][h0])
                    _wait(s, misc["p5a_gst"][h0 + 1])
                    gld_t[pj] = sL.issue(
                        s, gld_buf[:, (pj % 4) * 1024:(pj % 4) * 1024 + 1024].rearrange("p (h f) -> p h f", h=2),
                        gated_scr[h0:h0 + 2, :, tci * 512:(tci + 1) * 512].rearrange("h p f -> p h f"))
                for a in range(ndt):
                    idx2 = ci * ndt + a
                    if idx2 < 4:
                        _wait_all(s, misc["p4_tT"])
                    else:
                        s.wait_ge(oT.sems[idx2 % 4].h, 16 * ((idx2 - 4) // 4 + 1))
                    xld_t[idx2] = sL.issue(s, xa_buf[:, (idx2 % 4) * 512:(idx2 % 4) * 512 + 512],
                                           xattn_scr[q * ndt + a, :, tci * 512:(tci + 1) * 512])

        @block.tensor
        def _(t):
            for q in range(nq):
                _wait(t, wd_q[q])
                for tci in range(ntcf):
                    ci = q * ntcf + tci
                    if ci == 0:
                        t.wait_ge(vC.h, b_vC)
                        t.wait_ge(p5aV.h, p5aV.n)
                    elif ci >= 2:
                        t.wait_ge(vC.h, b_vC + ndt * (ci - 1))
                    for h in range(HT):
                        pj = ci * (HT // 2) + h // 2
                        if h % 2 == 0:
                            _wait(t, gld_t[pj])
                        for a in range(ndt):
                            mmi = t.matmul(pb[(tci % 2) * 4 + a][:, :],
                                     wdq_sb[q][:, h * DQ + a * 128:h * DQ + (a + 1) * 128],
                                     gld_buf[:, (pj % 4) * 1024 + (h % 2) * 512:(pj % 4) * 1024 + (h % 2) * 512 + 512],
                                     start=(h == 0), stop=(h == HT - 1))
                        mmi.then_inc(tC.h, 1)
        tC.n += NCI * HT

        @block.vector
        def _(v):
            for ci in range(NCI):
                q, tci = divmod(ci, ntcf)
                v.wait_ge(tC.h, b_tC + (ci + 1) * HT)
                for a in range(ndt):
                    idx2 = ci * ndt + a
                    _wait(v, xld_t[idx2])
                    v.tensor_tensor(out=xa_buf[:, (idx2 % 4) * 512:(idx2 % 4) * 512 + 512],
                                    in0=pb[(tci % 2) * 4 + a][:, :],
                                    in1=xa_buf[:, (idx2 % 4) * 512:(idx2 % 4) * 512 + 512],
                                    op=ALU.add).then_inc(vC.h, 1)
        vC.n += NCI * ndt

        @block.sync
        def _(sy):
            for ci in range(NCI):
                q, tci = divmod(ci, ntcf)
                for a in range(ndt):
                    idx2 = ci * ndt + a
                    d_abs = q * ndt + a
                    sy.wait_ge(vC.h, b_vC + idx2 + 1)
                    ost_t[idx2] = oT.issue(sy, out_fm[d_abs * 128:(d_abs + 1) * 128,
                                                      tci * 512:(tci + 1) * 512],
                                           xa_buf[:, (idx2 % 4) * 512:(idx2 % 4) * 512 + 512])
            _wait_all(sy, tT.totals())
            _wait_all(sy, oT.totals())

    _fill_pcq()
    p0()
    barrier()
    if phases == 0.3:
        proj_fm(wk_t, AF.Silu, "strips", "wk", dest=kfm_scr)
    elif phases == 0.6:
        proj_fm(wk_t, AF.Silu, "strips_ktm", "wk", dest=kfm_scr)
    elif phases == 0.8:
        proj_fm(wk_t, AF.Silu, "strips_ktm", "wk", dest=kfm_scr)

        @block.gpsimd
        def _(g):
            _wait_all(g, sS.totals())
            g.collective_compute("AllGather", ALU.bypass, replica_groups=RG4,
                                 ins=[mg_pub2d.opt()], outs=[mg_ag2d.opt()]).then_inc(ccm.h, 1)
            g.wait_ge(ccm.h, 1)
    if phases >= 1:
        p1()
    if phases >= 2:
        p25()
    if phases >= 3:
        group_copies()
    H1 = 48
    if phases >= 6:
        emit_p5a(0, H1)
    if phases >= 4:
        p3()
    if phases >= 6:
        emit_p5a(H1, HT)
    if phases >= 5:
        oint_gather()
        oint_copy()
    if phases >= 7:
        p4()
    if phases >= 8:
        p5b()
    if phases < 8:
        @block.sync
        def _(sy):
            _wait_all(sy, sX.totals())
            _wait_all(sy, tT.totals())
        @block.scalar
        def _(s):
            _wait_all(s, sS.totals())
            _wait_all(s, sL.totals())
        @block.gpsimd
        def _(g):
            _wait_all(g, gW.totals())

    if DEBUG_DUMP:
        scr_map = dict(ln_scr=ln_scr, g_scr=g_scr, kfm_scr=kfm_scr, oi_scr=oi_scr,
                       gated_scr=gated_scr, xattn_scr=xattn_scr, oint_pub=oint_pub,
                       v_loc=v_loc, oint_loc=oint_loc)
        ten = scr_map[DEBUG_DUMP]
        flat = ten.rearrange(" ".join(chr(97 + i) for i in range(len(ten.shape))) + " -> ("
                             + " ".join(chr(97 + i) for i in range(len(ten.shape))) + ")")
        dbg = nc.dram_tensor("dbg_out", list(ten.shape), ten.dtype, kind="ExternalOutput").ap()
        dbgf = dbg.rearrange(" ".join(chr(97 + i) for i in range(len(ten.shape))) + " -> ("
                             + " ".join(chr(97 + i) for i in range(len(ten.shape))) + ")")

        @block.sync
        def _(sy):
            _wait_all(sy, tT.totals())
            sy.dma_start(out=dbgf, in_=flat).then_inc(tT.sems[0].h, 16)
            tT.sems[0].n += 16
            sy.wait_ge(tT.sems[0].h, tT.sems[0].n)

    bg.__exit__(None, None, None)
    for g in reversed(guards):
        g.__exit__(None, None, None)
    return nc



def _stage_inputs(c: Cfg, x, Wq, Wk, Wv, Wg, Wp, w_gate, w_up, w_down):
    asc = np.ascontiguousarray
    DT = c.DT

    def blk(wT):
        a = wT.reshape(DT, 128, DT, 128).transpose(2, 1, 0, 3)
        return asc(a.reshape(DT, 128, DT * 128))

    wq_t = blk(Wq.T)
    wk_t = blk(Wk.T)
    wg_t = blk(Wg.T)
    wp_t = blk(Wp.T)
    EQ = c.D // 4
    wv_t = asc(Wv.T.reshape(DT, 128, 4, EQ).transpose(2, 1, 0, 3).reshape(4, 128, DT * EQ))
    wgate_b = asc(w_gate.T.reshape(DT, 128, c.HT, 128).transpose(2, 1, 0, 3).reshape(c.HT, 128, DT * 128))
    wup_b = asc(w_up.T.reshape(DT, 128, c.HT, 128).transpose(2, 1, 0, 3).reshape(c.HT, 128, DT * 128))
    DQ = c.D // 4
    wdown_b = asc(w_down.T.reshape(c.HT, 128, 4, DQ).transpose(2, 1, 0, 3).reshape(4, 128, c.HT * DQ))
    in_maps = []
    for core in range(8):
        b, q4 = core // 4, core % 4
        sl = x[b, q4 * c.TC:(q4 + 1) * c.TC, :]
        in_maps.append(dict(
            x_fm=asc(sl.T), x_tm=asc(sl),
            wq_t=wq_t, wk_t=wk_t, wv_t=wv_t, wg_t=wg_t, wp_t=wp_t,
            wgate_b=wgate_b, wup_b=wup_b, wdown_b=wdown_b,
        ))
    return in_maps


def run(c: Cfg, x, Wq, Wk, Wv, Wg, Wp, w_gate, w_up, w_down):
    nc = build_program(c)
    in_maps = _stage_inputs(c, x, Wq, Wk, Wv, Wg, Wp, w_gate, w_up, w_down)
    res = run_bass_kernel_spmd(nc, in_maps, core_ids=list(range(8)))
    out = np.empty((2, 4 * c.TC, c.D), np.float32)
    for core in range(8):
        b, q4 = core // 4, core % 4
        out[b, q4 * c.TC:(q4 + 1) * c.TC, :] = res.results[core]["out_fm"].T
    return out


def kernel(L0_attn_x, L0_attn_Wq, L0_attn_Wk, L0_attn_Wv, L0_attn_Wg, L0_attn_Wp,
           L0_ffn_w_gate, L0_ffn_w_up, L0_ffn_w_down):
    c = Cfg(D=2048, TC=2048, H=8192)
    return run(c, np.asarray(L0_attn_x), np.asarray(L0_attn_Wq), np.asarray(L0_attn_Wk),
               np.asarray(L0_attn_Wv), np.asarray(L0_attn_Wg), np.asarray(L0_attn_Wp),
               np.asarray(L0_ffn_w_gate), np.asarray(L0_ffn_w_up), np.asarray(L0_ffn_w_down))
